# version 10
# baseline (speedup 1.0000x reference)
import numpy as np
import ml_dtypes

import concourse.bass as bass
import concourse.mybir as mybir
from concourse.bass_utils import run_bass_kernel_spmd

F32 = mybir.dt.float32
BF16 = mybir.dt.bfloat16
U8 = mybir.dt.uint8
AF = mybir.ActivationFunctionType
ALU = mybir.AluOpType
AX = mybir.AxisListType
BFNP = ml_dtypes.bfloat16

B = 2
S = 8192
D = 4096
H = 32
DH = 128
BLK = 256
NCORE = 8
TCORE = 2048
HALF = 1024
NSUB = 8
NIC = 32
NBLK_HALF = 4
EPS = 1e-5
HG = 4
NGRP = H // HG
OCB = 256
NOCB = D // OCB

DEBUG_SPILLS = False


def _slopes():
    base = 1.0 / 2.0 ** (8.0 / H)
    factor = 1.0 - 1.0 / (31 + 1e-5) + 1e-5
    return (base ** (np.arange(H, dtype=np.float64) + 1.0)) * factor


def _consts():
    sl = _slopes()
    r = np.arange(BLK, dtype=np.float64)
    diff = r[None, :] - r[:, None]
    maskT = np.where(diff >= 0, np.exp(-sl[:, None, None] * diff[None]), 0.0)
    maskp = maskT.reshape(H, 2, 128, BLK).transpose(0, 2, 1, 3).reshape(H, 128, 2 * BLK)
    maskp = maskp.astype(np.float32).astype(BFNP)
    qdl = np.exp(-sl[:, None] * (r[None, :] + 1.0))
    kdec = np.exp(-sl[:, None] * (BLK - 1.0 - r[None, :]))
    qdlp = qdl.reshape(H, 2, 128).transpose(2, 0, 1).reshape(128, 64).astype(np.float32)
    kdecp = kdec.reshape(H, 2, 128).transpose(2, 0, 1).reshape(128, 64).astype(np.float32)
    Dm = np.exp(-sl * BLK)
    onesp = np.ones((128, 8), dtype=BFNP)
    return maskp, qdlp, kdecp, Dm, onesp


def _wcomb_for_core(core, Dm):
    b, jj = core // 4, core % 4
    w = np.zeros((H, NCORE), dtype=np.float64)
    for c in range(jj):
        w[:, b * 4 + c] = Dm ** (8.0 * (jj - 1 - c))
    out = np.zeros((128, NCORE * H), dtype=np.float32)
    for rk in range(NCORE):
        for h in range(H):
            out[:, rk * H + h] = np.float32(w[h, rk])
    return out


class _Ctr:
    __slots__ = ("sem", "n")

    def __init__(self, sem):
        self.sem = sem
        self.n = 0

    def bump(self, k=1):
        self.n += k
        return self.n


def _build_nc():
    nc = bass.Bass("TRN2", target_bir_lowering=False)
    skind = "ExternalOutput" if DEBUG_SPILLS else "Internal"

    x_d = nc.dram_tensor("x_sh", [TCORE, D], F32, kind="ExternalInput").ap()
    qkvp_d = nc.dram_tensor("qkvp", [96, 128, D], F32, kind="ExternalInput").ap()
    gatep_d = nc.dram_tensor("gatep", [32, 128, D], F32, kind="ExternalInput").ap()
    outp_d = nc.dram_tensor("outp", [NOCB, 128, NIC * OCB], F32,
                            kind="ExternalInput").ap()
    ffnp_d = nc.dram_tensor("ffnp", [NOCB, 128, NIC * OCB], F32, kind="ExternalInput").ap()
    rwp_d = nc.dram_tensor("rwp", [128, 256], F32, kind="ExternalInput").ap()
    icsp_d = nc.dram_tensor("icsp", [8, D], F32, kind="ExternalInput").ap()
    ocsp_d = nc.dram_tensor("ocsp", [8, D], F32, kind="ExternalInput").ap()
    nwp_d = nc.dram_tensor("nwp", [128, D], F32, kind="ExternalInput").ap()
    biasp_d = nc.dram_tensor("biasp", [128, D], F32, kind="ExternalInput").ap()
    maskp_d = nc.dram_tensor("maskp", [H, 128, 512], BF16, kind="ExternalInput").ap()
    qdlp_d = nc.dram_tensor("qdlp", [128, 64], F32, kind="ExternalInput").ap()
    kdecp_d = nc.dram_tensor("kdecp", [128, 64], F32, kind="ExternalInput").ap()
    wcombp_d = nc.dram_tensor("wcombp", [128, 256], F32, kind="ExternalInput").ap()
    onesp_d = nc.dram_tensor("onesp", [128, 8], BF16, kind="ExternalInput").ap()

    out_d = nc.dram_tensor("out_sh", [TCORE, D], F32, kind="ExternalOutput").ap()

    aspill = nc.dram_tensor("aspill", [TCORE, D], BF16, kind=skind)
    hspill = nc.dram_tensor("hspill", [128, 32768], BF16, kind=skind)
    qspill = nc.dram_tensor("qspill", [2, H, 128, HALF], BF16, kind=skind)
    gth_in = nc.dram_tensor("gth_in", [128, H * DH], F32, kind=skind)
    gth_out = nc.dram_tensor("gth_out", [NCORE * 128, H * DH], F32, kind=skind)

    Dm64 = np.exp(-_slopes() * BLK)
    Dmf = [float(np.float32(v)) for v in Dm64]
    Dm_pow = [[float(np.float32(Dm64[h] ** m)) for m in range(8)] for h in range(H)]

    A1_OFF = 0
    A2_OFF = 65536
    A3_OFF = 131072
    FX_OFF = A3_OFF + 49152
    fx = {}
    cur = [FX_OFF]

    def fxa(name, nbytes):
        fx[name] = cur[0]
        cur[0] += (nbytes + 31) // 32 * 32

    fxa("h_bf", 8192)
    fxa("dump", 4096)
    fxa("kvst_bf", 8192)
    fxa("eo_bf", 8192)
    fxa("stats", 128)
    fxa("qdl", 256)
    fxa("kdec", 256)
    fxa("wcomb", 1024)
    fxa("ones", 32)
    fxa("rw_bf", 512)
    TOTAL = cur[0]
    assert TOTAL <= 212800, TOTAL

    arena = nc.alloc_sbuf_tensor("arena", [128, TOTAL], U8)
    base = nc.lookup_mloc(arena).addr

    def sb(name, shape, dtype, off):
        return nc.alloc_sbuf_tensor_at(name, shape, dtype, offset=base + off).ap()

    hT = sb("hT", [128, 32768], BF16, A1_OFF)
    q_raw = sb("q_raw", [128, 4096], BF16, A2_OFF + 0)
    qchk = sb("qchk", [128, 4096], BF16, A2_OFF + 8192)
    k_sb = sb("k_sb", [128, 4096], BF16, A2_OFF + 16384)
    ktld = sb("ktld", [128, 4096], BF16, A2_OFF + 24576)
    v_tm = sb("v_tm", [128, 4096], BF16, A2_OFF + 32768)
    kv_hist = sb("kv_hist", [128, 2048], BF16, A2_OFF + 40960)
    o_sb = sb("o_sb", [128, 4096], BF16, A2_OFF + 49152)
    smask = sb("smask", [128, 2048], BF16, A2_OFF + 57344)
    gnT = sb("gnT", [128, 32768], BF16, A2_OFF)
    wbuf = sb("wbuf", [128, 8192], BF16, A3_OFF)
    mstr = sb("mstr", [128, 2048], BF16, A3_OFF + 16384)
    ttmp = sb("ttmp", [128, 2048], BF16, A3_OFF + 20480)
    qtm = sb("qtm", [128, 4096], BF16, A3_OFF + 24576)
    kv_all = sb("kv_all", [128, 4096], F32, A3_OFF + 32768)
    xbuf = sb("xbuf", [128, 8192], BF16, A3_OFF + 16384)
    gbuf = sb("gbuf", [128, 8192], BF16, A3_OFF + 16384)
    nw_bc = sb("nw_bc", [128, 4096], BF16, A3_OFF + 24576)
    ntmp = sb("ntmp", [128, 4096], BF16, A3_OFF + 16384)
    qrd = sb("qrd", [128, 2048], BF16, A3_OFF + 32768)
    bias_bc = sb("bias_bc", [128, 4096], BF16, A3_OFF)
    expsb = sb("expsb", [8, 1024], BF16, A3_OFF + 8192)
    h_bf2 = sb("h_bf2", [128, 4096], BF16, A3_OFF + 36864)
    rrow = sb("rrow", [1, 1024], F32, A3_OFF + 10240)
    rrow_bf = sb("rrow_bf", [1, 1024], BF16, A3_OFF + 14336)
    woc = sb("woc", [128, 16384], BF16, A3_OFF)
    outsb = sb("outsb", [128, 2048], F32, A3_OFF + 32768)

    h_bf = sb("h_bf_t", [128, 4096], BF16, fx["h_bf"])
    dump = sb("dump_t", [128, 2048], BF16, fx["dump"])
    kvst_bf = sb("kvst_t", [128, 4096], BF16, fx["kvst_bf"])
    xbuf_p1 = sb("xbuf_p1", [128, 8192], BF16, fx["kvst_bf"])
    eo_bf = sb("eo_bf_t", [8, 4096], BF16, fx["eo_bf"])
    os_sb = sb("os_sb_t", [128, 2048], BF16, A3_OFF + 40960)
    rT = sb("rT_t", [8, 1024], BF16, A3_OFF + 45056)
    stats = sb("stats_t", [128, 32], F32, fx["stats"])
    qdl_sb = sb("qdl_t", [128, 64], F32, fx["qdl"])
    kdec_sb = sb("kdec_t", [128, 64], F32, fx["kdec"])
    wcomb_sb = sb("wcomb_t", [128, 256], F32, fx["wcomb"])
    ones_sb = sb("ones_t", [128, 8], BF16, fx["ones"])
    rw_bf = sb("rw_bf_t", [128, 256], BF16, fx["rw_bf"])

    SUMC, SQC, NMC, VARC, M2C, STDC, RSTDC, SQ1C = 0, 1, 2, 3, 4, 5, 6, 7

    pst = nc.alloc_psum_tensor("ps", [128, 8, 512], F32)
    ps = pst.ap()

    def pbank(b_, lo=0, hi=512):
        return ps[:, b_, lo:hi]

    def ppair(s):
        return ps[:, 2 * s:2 * s + 2, :]

    def mk(name):
        return _Ctr(nc.alloc_semaphore(name))

    s_x = [mk("s_x0"), mk("s_x1")]
    s_tr = mk("s_tr")
    s_wl = [mk("s_wl0"), mk("s_wl1")]
    s_mm = mk("s_mm")
    s_ev = [mk("s_ev0"), mk("s_ev1")]
    s_msk = mk("s_msk")
    s_po = mk("s_po")
    s_oe = mk("s_oe")
    s_kvp = mk("s_kvp")
    s_kvu = mk("s_kvu")
    s_osp = [mk("s_osp0"), mk("s_osp1"), mk("s_osp2"), mk("s_osp3")]
    s_qsp = mk("s_qsp")
    s_sum = mk("s_sum")
    s_rd = [mk("s_rd0"), mk("s_rd1")]
    s_att = mk("s_att")
    s_mld = mk("s_mld")
    s_nw = mk("s_nw")
    s_bia = mk("s_bia")
    s_eo = mk("s_eo")
    s_cc = mk("s_cc")
    s_gb = [mk("s_gb0"), mk("s_gb1")]
    s_st = mk("s_st")
    s_dv = mk("s_dv")
    s_ac = mk("s_ac")
    s_ou = mk("s_ou")
    s_pe = mk("s_pe")
    s_cst = mk("s_cst")
    s_rw = mk("s_rw")
    s_hs = mk("s_hs")
    s_hr = mk("s_hr")

    st = {
        "x_users": [(None, 0), (None, 0)],
        "xp1_users": [(None, 0), (None, 0)],
        "wl_freed": [0, 0],
        "wl_cnt": [0, 0],
        "ps_main": [(None, 0), (None, 0)],
        "scores_free": [0, 0],
        "smask_free": [0, 0, 0, 0],
        "obank_free": [0, 0, 0, 0],
        "osb_used": [0, 0, 0, 0],
        "mstr_free": 0,
        "head_kvu": {},
        "ph2_pe": 0,
        "ps45_free": 0,
        "cw_used": [0, 0],
        "ctm_ps": 0,
        "kvps_free": [0, 0, 0, 0],
        "v_tr": [0, 0, 0, 0],
        "ttmp_user": [(None, 0), (None, 0)],
        "prev_scan_pe": 0,
        "prev_spill": 0,
        "qrd_used": [0, 0],
        "ps_c": [0, 0],
        "ntmp_used": 0,
        "woc_used": [0, 0],
        "os_used": 0,
        "ffn_ps_used": 0,
        "outsb_used": 0,
        "comb_done": 0,
        "gb_free": [0, 0],
        "prev_grp_tr": 0,
        "comb_prev": 0,
        "eo_user": 0,
        "hbf_free": 0,
        "hbf2_free": 0,
    }

    block = nc.Block()
    block.__enter__()

    @block.sync
    def _(s):
        s.dma_start(qdl_sb, qdlp_d).then_inc(s_cst.sem, 16)
        s.dma_start(kdec_sb, kdecp_d).then_inc(s_cst.sem, 16)
        s.dma_start(wcomb_sb, wcombp_d).then_inc(s_cst.sem, 16)
        s.dma_start(ones_sb, onesp_d).then_inc(s_cst.sem, 16)
    s_cst.bump(64)

    @block.gpsimd
    def _(g):
        g.dma_start(rw_bf, rwp_d).then_inc(s_rw.sem, 16)
    s_rw.bump(16)
    CST_TOTAL = s_cst.n

    @block.vector
    def _(v):
        v.memset(kv_all, 0.0)
        v.engine_nop().then_inc(s_dv.sem, 1)
    kv_zero = s_dv.bump()

    def tr_dma(s, dst, src):
        s.wait_ge(s_tr.sem, s_tr.n)
        s.dma_start(dst, src, transpose=True).then_inc(s_tr.sem, 16)
        return s_tr.bump(16)

    def chain(eng, ins):
        ins.then_inc(s_dv.sem, 1)
        eng.wait_ge(s_dv.sem, s_dv.bump())

    def ln_subtile(half, sub, dst2d, hT_guard_mm=0, spill_wait=0):
        slot = sub % 2
        xb = xbuf_p1[:, slot * 4096:(slot + 1) * 4096]

        @block.gpsimd
        def _(g):
            c, v = st["xp1_users"][slot]
            if c is not None:
                g.wait_ge(c.sem, v)
            g.dma_start(xb, x_d[half * HALF + sub * 128:
                               half * HALF + (sub + 1) * 128, :]).then_inc(
                s_x[slot].sem, 16)
        xv = s_x[slot].bump(16)

        @block.scalar
        def _(sc):
            sc.wait_ge(s_x[slot].sem, xv)
            sc.wait_ge(s_st.sem, s_st.n)
            sc.activation(dump, xb[:, 0:2048], AF.Square,
                          accum_out=stats[:, SQC:SQC + 1]).then_inc(s_ac.sem, 1)
            sc.wait_ge(s_ac.sem, s_ac.bump())
            sc.activation(dump, xb[:, 2048:4096], AF.Square,
                          accum_out=stats[:, SQ1C:SQ1C + 1]).then_inc(s_st.sem, 1)
        sq_v = s_st.bump()

        @block.vector
        def _(v):
            v.wait_ge(s_x[slot].sem, xv)
            chain(v, v.reduce_sum(stats[:, SUMC:SUMC + 1], xb, axis=AX.X))
            v.wait_ge(s_st.sem, sq_v)
            chain(v, v.tensor_tensor(stats[:, SQC:SQC + 1], stats[:, SQC:SQC + 1],
                                     stats[:, SQ1C:SQ1C + 1], ALU.add))
            chain(v, v.tensor_scalar(stats[:, NMC:NMC + 1],
                                     stats[:, SUMC:SUMC + 1],
                                     -1.0 / D, None, ALU.mult))
            chain(v, v.tensor_scalar(stats[:, VARC:VARC + 1],
                                     stats[:, SQC:SQC + 1],
                                     1.0 / D, EPS, ALU.mult, ALU.add))
            chain(v, v.tensor_tensor(stats[:, M2C:M2C + 1], stats[:, NMC:NMC + 1],
                                     stats[:, NMC:NMC + 1], ALU.mult))
            v.tensor_tensor(stats[:, VARC:VARC + 1], stats[:, VARC:VARC + 1],
                            stats[:, M2C:M2C + 1], ALU.subtract).then_inc(s_dv.sem, 1)
        var_v = s_dv.bump()

        @block.scalar
        def _(sc):
            sc.wait_ge(s_dv.sem, var_v)
            sc.activation(stats[:, STDC:STDC + 1], stats[:, VARC:VARC + 1],
                          AF.Sqrt).then_inc(s_ac.sem, 1)
        std_v = s_ac.bump()

        @block.vector
        def _(v):
            v.wait_ge(s_ac.sem, std_v)
            chain(v, v.reciprocal(stats[:, RSTDC:RSTDC + 1],
                                  stats[:, STDC:STDC + 1]))
            v.wait_ge(s_tr.sem, st["hbf_free"])
            v.tensor_scalar(h_bf, xb, stats[:, NMC:NMC + 1],
                            stats[:, RSTDC:RSTDC + 1], ALU.add, ALU.mult).then_inc(
                s_dv.sem, 1)
        norm_v = s_dv.bump()
        st["xp1_users"][slot] = (s_dv, norm_v)

        @block.sync
        def _(s):
            s.wait_ge(s_dv.sem, norm_v)
            if sub == 0:
                if hT_guard_mm > 0:
                    s.wait_ge(s_mm.sem, hT_guard_mm)
                if spill_wait > 0:
                    s.wait_ge(s_hs.sem, spill_wait)
            tr_dma(s,
                   dst2d.rearrange("p (c t) -> p c t", t=1024)[
                       :, :, sub * 128:(sub + 1) * 128],
                   h_bf.rearrange("p (c f) -> p c f", f=128))
        st["hbf_free"] = s_tr.n

    def load_wcb(src_ap, elems=4096, extra=()):
        t = load_wcb.count
        slot = t % 2
        load_wcb.count += 1

        @block.gpsimd
        def _(g):
            if st["wl_cnt"][slot] > 0:
                g.wait_ge(s_mm.sem, st["wl_freed"][slot])
            for cc, vv in extra:
                if vv > 0:
                    g.wait_ge(cc.sem, vv)
            g.dma_start(wbuf[:, slot * 4096:slot * 4096 + elems],
                        src_ap).then_inc(s_wl[slot].sem, 16)
        st["wl_cnt"][slot] += 1
        return slot, s_wl[slot].bump(16)

    load_wcb.count = 0

    def fm_cb_matmul(wslot, wlv, rhs2d, pslot, pre=None):
        @block.tensor
        def _(t):
            t.wait_ge(s_wl[wslot].sem, wlv)
            c, v = st["ps_main"][pslot]
            if c is not None:
                t.wait_ge(c.sem, v)
            if pre is not None:
                for (cc, vv) in pre:
                    if vv > 0:
                        t.wait_ge(cc.sem, vv)
            for ic in range(NIC):
                lhsT = wbuf[:, wslot * 4096 + ic * 128:wslot * 4096 + (ic + 1) * 128]
                t.matmul(ps[:, 2 * pslot, :], lhsT,
                         rhs2d[:, ic * 1024:ic * 1024 + 512],
                         start=(ic == 0), stop=(ic == NIC - 1))
                ins = t.matmul(ps[:, 2 * pslot + 1, :], lhsT,
                               rhs2d[:, ic * 1024 + 512:(ic + 1) * 1024],
                               start=(ic == 0), stop=(ic == NIC - 1))
                if ic == NIC - 1:
                    ins.then_inc(s_mm.sem, 1)
        st["wl_freed"][wslot] = s_mm.bump()
        return s_mm.n

    def p1_half(half, hT_guard_mm=0, spill_wait=0):
        for sub in range(NSUB):
            ln_subtile(half, sub, hT, hT_guard_mm, spill_wait)
        hT_ready = s_tr.n

        for hg in range(NGRP):
            heads = [hg * HG + i for i in range(HG)]
            ev_marks = {}
            prev_pe = st["prev_scan_pe"]
            prev_grp_tr = st["prev_grp_tr"]
            scan_entry = (s_msk.n, s_oe.n)

            pend_ev = []

            def qkv_mm(comp, i4):
                cb = hg * 12 + comp * 4 + i4
                wslot, wlv = load_wcb(qkvp_d[cb])
                pslot = load_wcb.count % 2

                @block.tensor
                def _(t):
                    t.wait_ge(s_wl[wslot].sem, wlv)
                    t.wait_ge(s_tr.sem, hT_ready)
                    if comp == 0 and i4 < 2:
                        if scan_entry[0] > 0:
                            t.wait_ge(s_msk.sem, scan_entry[0])
                        if scan_entry[1] > 0:
                            t.wait_ge(s_oe.sem, scan_entry[1])
                    c, v = st["ps_main"][pslot]
                    if c is not None:
                        t.wait_ge(c.sem, v)
                    for ic in range(NIC):
                        lhsT = wbuf[:, wslot * 4096 + ic * 128:
                                    wslot * 4096 + (ic + 1) * 128]
                        t.matmul(ps[:, 2 * pslot, :], lhsT,
                                 hT[:, ic * 1024:ic * 1024 + 512],
                                 start=(ic == 0), stop=(ic == NIC - 1))
                        ins = t.matmul(ps[:, 2 * pslot + 1, :], lhsT,
                                       hT[:, ic * 1024 + 512:(ic + 1) * 1024],
                                       start=(ic == 0), stop=(ic == NIC - 1))
                        if ic == NIC - 1:
                            ins.then_inc(s_mm.sem, 1)
                st["wl_freed"][wslot] = s_mm.bump()
                mmv = s_mm.n

                if comp < 2:
                    evdst = (q_raw, k_sb)[comp][:, i4 * 1024:(i4 + 1) * 1024]
                else:
                    evdst = ttmp[:, (i4 % 2) * 1024:(i4 % 2 + 1) * 1024]
                tw = st["ttmp_user"][i4 % 2] if comp == 2 else None
                pend_ev.append((pslot, mmv, evdst, tw))
                s_ev[pslot].bump()
                st["ps_main"][pslot] = (s_ev[pslot], s_ev[pslot].n)
                ev_marks[(comp, i4)] = (s_ev[pslot], s_ev[pslot].n)

            def flush_ev():
                evs = list(pend_ev)
                pend_ev.clear()

                @block.scalar
                def _(sc):
                    if prev_pe > 0:
                        sc.wait_ge(s_po.sem, prev_pe)
                    if prev_grp_tr > 0:
                        sc.wait_ge(s_tr.sem, prev_grp_tr)
                    for (pslot, mmv, evdst, tw) in evs:
                        sc.wait_ge(s_mm.sem, mmv)
                        if tw is not None and tw[0] is not None:
                            sc.wait_ge(tw[0].sem, tw[1])
                        sc.activation(evdst, ppair(pslot), AF.Silu).then_inc(
                            s_ev[pslot].sem, 1)

            def v_tr(i4):
                @block.sync
                def _(s):
                    c, v = ev_marks[(2, i4)]
                    s.wait_ge(c.sem, v)
                    if prev_pe > 0:
                        s.wait_ge(s_po.sem, prev_pe)
                    tr_dma(s,
                           v_tm[:, i4 * 1024:(i4 + 1) * 1024].rearrange(
                               "p (c t) -> p c t", t=128),
                           ttmp[:, (i4 % 2) * 1024:(i4 % 2 + 1) * 1024]
                           .rearrange("p (c f) -> p c f", f=128))
                st["ttmp_user"][i4 % 2] = (s_tr, s_tr.n)
                st["v_tr"][i4] = s_tr.n

            for comp in range(2):
                for i4 in range(HG):
                    qkv_mm(comp, i4)
                flush_ev()

            kq_ev = [(s_ev[0], s_ev[0].n), (s_ev[1], s_ev[1].n)]

            @block.sync
            def _(s):
                for cc, vv in kq_ev:
                    s.wait_ge(cc.sem, vv)
                if prev_pe > 0:
                    s.wait_ge(s_po.sem, prev_pe)
                tr_dma(s, ktld.rearrange("p (c t) -> p c t", t=128),
                       k_sb.rearrange("p (c f) -> p c f", f=128))
            ktr = s_tr.n

            @block.vector
            def _(v):
                v.wait_ge(s_tr.sem, ktr)
                v.wait_ge(s_cst.sem, CST_TOTAL)
                for i4 in range(HG):
                    h = heads[i4]
                    for par in range(2):
                        view = ktld[:, i4 * 1024:(i4 + 1) * 1024].rearrange(
                            "p (c t) -> p c t", t=128)[:, par::2, :]
                        ins = v.tensor_scalar(view, view,
                                              kdec_sb[:, 2 * h + par:2 * h + par + 1],
                                              None, ALU.mult)
                        if i4 == HG - 1 and par == 1:
                            ins.then_inc(s_dv.sem, 1)
            kscale_v = s_dv.bump()

            @block.sync
            def _(s):
                for cc, vv in kq_ev:
                    s.wait_ge(cc.sem, vv)
                tr_dma(s, qtm.rearrange("p (c t) -> p c t", t=128),
                       q_raw.rearrange("p (c f) -> p c f", f=128))
            qtr = s_tr.n

            @block.vector
            def _(v):
                v.wait_ge(s_tr.sem, qtr)
                v.wait_ge(s_cst.sem, CST_TOTAL)
                for i4 in range(HG):
                    h = heads[i4]
                    for par in range(2):
                        view = qtm[:, i4 * 1024:(i4 + 1) * 1024].rearrange(
                            "p (c t) -> p c t", t=128)[:, par::2, :]
                        ins = v.tensor_scalar(view, view,
                                              qdl_sb[:, 2 * h + par:2 * h + par + 1],
                                              None, ALU.mult)
                        if i4 == HG - 1 and par == 1:
                            ins.then_inc(s_dv.sem, 1)
            qscale_v = s_dv.bump()

            @block.sync
            def _(s):
                s.wait_ge(s_dv.sem, qscale_v)
                s.wait_ge(s_qsp.sem, st["prev_spill"])
                if prev_pe > 0:
                    s.wait_ge(s_po.sem, prev_pe)
                tr_dma(s, qchk.rearrange("p (c t) -> p c t", t=128),
                       qtm.rearrange("p (c f) -> p c f", f=128))
            qctr = s_tr.n

            @block.sync
            def _(s):
                s.wait_ge(s_tr.sem, qctr)
                dview = qspill.ap()[half, hg * HG:(hg + 1) * HG].rearrange(
                    "h p t -> p h t")
                s.dma_start(dview,
                            qchk.rearrange("p (h t) -> p h t", t=HALF)
                            ).then_inc(s_qsp.sem, 16)
            s_qsp.bump(16)
            st["prev_spill"] = s_qsp.n

            @block.sync
            def _(s):
                if st["mstr_free"] > 0:
                    s.wait_ge(s_msk.sem, st["mstr_free"])
                dview = maskp_d[hg * HG:(hg + 1) * HG].rearrange("h p n -> p h n")
                s.dma_start(mstr.rearrange("p (h n) -> p h n", n=512),
                            dview).then_inc(s_mld.sem, 16)
            mload = s_mld.bump(16)

            for i4 in range(2):
                qkv_mm(2, i4)
            flush_ev()
            v_tr(0)
            v_tr(1)
            for i4 in range(2, HG):
                qkv_mm(2, i4)
            flush_ev()
            v_tr(2)
            v_tr(3)

            grp_dv = s_dv.n
            kq_ev2 = [(s_ev[0], s_ev[0].n), (s_ev[1], s_ev[1].n)]

            def emit_scores(k):
                blk, i4 = k // HG, k % HG
                sslot = k % 2

                @block.tensor
                def _(t):
                    if k == 0:
                        t.wait_ge(s_dv.sem, max(grp_dv, kv_zero))
                        for cc, vv in kq_ev2:
                            t.wait_ge(cc.sem, vv)
                    if st["scores_free"][sslot] > 0:
                        t.wait_ge(s_msk.sem, st["scores_free"][sslot])
                    for tkc in range(2):
                        lhsT = k_sb[:, i4 * 1024 + blk * 256 + tkc * 128:
                                    i4 * 1024 + blk * 256 + (tkc + 1) * 128]
                        rhs = q_raw[:, i4 * 1024 + blk * 256:
                                    i4 * 1024 + (blk + 1) * 256]
                        ins = t.matmul(pbank(sslot, tkc * 256, (tkc + 1) * 256),
                                       lhsT, rhs, start=True, stop=True)
                        if tkc == 1:
                            ins.then_inc(s_pe.sem, 1)
                return s_pe.bump()

            sc_done = {}
            sc_done[0] = emit_scores(0)
            oe_of_head = [0, 0, 0, 0]
            for blk in range(NBLK_HALF):
                pend_oe = []
                for i4 in range(HG):
                    h = heads[i4]
                    k = blk * HG + i4
                    sslot = k % 2
                    obank = 2 + i4 % 2
                    ocolh = i4 // 2
                    kslot = k % 4
                    kbank = 6 + kslot % 2
                    kcol = (kslot // 2) * 128

                    if k + 1 < NBLK_HALF * HG:
                        sc_done[k + 1] = emit_scores(k + 1)

                    @block.vector
                    def _(v):
                        v.wait_ge(s_pe.sem, sc_done[k])
                        v.wait_ge(s_mld.sem, mload)
                        if st["smask_free"][i4] > 0:
                            v.wait_ge(s_po.sem, st["smask_free"][i4])
                        v.tensor_tensor(smask[:, i4 * 512:(i4 + 1) * 512],
                                        pbank(sslot),
                                        mstr[:, i4 * 512:(i4 + 1) * 512],
                                        ALU.mult).then_inc(s_msk.sem, 1)
                    msk_v = s_msk.bump()
                    st["scores_free"][sslot] = msk_v

                    @block.tensor
                    def _(t):
                        t.wait_ge(s_msk.sem, msk_v)
                        if blk == 0:
                            t.wait_ge(s_tr.sem, st["v_tr"][i4])
                        if st["obank_free"][i4] > 0:
                            t.wait_ge(s_oe.sem, st["obank_free"][i4])
                        for tqc in range(2):
                            dst = pbank(obank, ocolh * 256 + tqc * 128,
                                        ocolh * 256 + (tqc + 1) * 128)
                            for tkc in range(2):
                                lhsT = smask[:, i4 * 512 + tkc * 256 + tqc * 128:
                                             i4 * 512 + tkc * 256 + (tqc + 1) * 128]
                                rhs = v_tm[:, i4 * 1024 + (blk * 2 + tkc) * 128:
                                           i4 * 1024 + (blk * 2 + tkc + 1) * 128]
                                ins = t.matmul(dst, lhsT, rhs,
                                               start=(tkc == 0), stop=(tkc == 1))
                                if tqc == 1 and tkc == 1:
                                    ins.then_inc(s_po.sem, 1)
                        if st["kvps_free"][kslot] > 0:
                            t.wait_ge(s_kvu.sem, st["kvps_free"][kslot])
                        for tkc in range(2):
                            ins = t.matmul(
                                pbank(kbank, kcol, kcol + 128),
                                ktld[:, i4 * 1024 + (blk * 2 + tkc) * 128:
                                     i4 * 1024 + (blk * 2 + tkc + 1) * 128],
                                v_tm[:, i4 * 1024 + (blk * 2 + tkc) * 128:
                                     i4 * 1024 + (blk * 2 + tkc + 1) * 128],
                                start=(tkc == 0), stop=(tkc == 1))
                            if tkc == 1:
                                ins.then_inc(s_kvp.sem, 1)
                    po_v = s_po.bump()
                    kvp_v = s_kvp.bump()
                    st["smask_free"][i4] = po_v

                    osw = st["osb_used"][i4] if blk == 0 else 0
                    pend_oe.append((po_v, osw, i4, obank, ocolh))
                    st["obank_free"][i4] = s_oe.bump()
                    oe_of_head[i4] = s_oe.n

                    @block.vector
                    def _(v):
                        v.wait_ge(s_kvp.sem, kvp_v)
                        if blk == 0:
                            v.wait_ge(s_dv.sem, kv_zero)
                            if st["ph2_pe"] > 0:
                                v.wait_ge(s_po.sem, st["ph2_pe"])
                        v.wait_ge(s_kvu.sem, s_kvu.n)
                        kvc = kv_all[:, h * 128:(h + 1) * 128]
                        v.tensor_copy(
                            kv_hist[:, i4 * 512 + blk * 128:
                                    i4 * 512 + (blk + 1) * 128],
                            kvc).then_inc(s_kvu.sem, 1)
                        v.scalar_tensor_tensor(
                            kvc, kvc, Dmf[h], pbank(kbank, kcol, kcol + 128),
                            ALU.mult, ALU.add).then_inc(s_kvu.sem, 1)
                    s_kvu.bump(2)
                    st["head_kvu"][h] = s_kvu.n - 1
                    st["kvps_free"][kslot] = s_kvu.n

                @block.scalar
                def _(sc):
                    for (po_vv, osw, i4b, obankb, och) in pend_oe:
                        sc.wait_ge(s_po.sem, po_vv)
                        if osw > 0:
                            sc.wait_ge(s_osp[i4b].sem, osw)
                        sc.activation(
                            o_sb[:, i4b * 1024 + blk * 256:
                                 i4b * 1024 + (blk + 1) * 256],
                            pbank(obankb, och * 256, (och + 1) * 256),
                            AF.Copy).then_inc(s_oe.sem, 1)

            st["mstr_free"] = s_msk.n

            for i4 in range(HG):
                h = heads[i4]

                @block.tensor
                def _(t):
                    t.wait_ge(s_kvu.sem, st["head_kvu"][h])
                    if st["ps45_free"] > 0:
                        t.wait_ge(s_dv.sem, st["ps45_free"])
                    for m in range(NBLK_HALF):
                        rhs = kv_hist[:, i4 * 512 + m * 128:
                                      i4 * 512 + (m + 1) * 128]
                        for tqc in range(2):
                            ins = t.matmul(
                                pbank(4 + m // 2, (m % 2) * 256 + tqc * 128,
                                      (m % 2) * 256 + (tqc + 1) * 128),
                                qchk[:, i4 * 1024 + m * 256 + tqc * 128:
                                     i4 * 1024 + m * 256 + (tqc + 1) * 128],
                                rhs, start=True, stop=True)
                            if m == NBLK_HALF - 1 and tqc == 1:
                                ins.then_inc(s_po.sem, 1)
                ph2_v = s_po.bump()

                @block.vector
                def _(v):
                    v.wait_ge(s_po.sem, ph2_v)
                    v.wait_ge(s_oe.sem, oe_of_head[i4])
                    ov = o_sb[:, i4 * 1024:(i4 + 1) * 1024].rearrange(
                        "p (a b) -> p a b", b=512)
                    v.tensor_tensor(ov, ps[:, 4:6, :], ov, ALU.add).then_inc(
                        s_dv.sem, 1)
                add_v = s_dv.bump()
                st["ps45_free"] = add_v

                @block.sync
                def _(s):
                    s.wait_ge(s_dv.sem, add_v)
                    view = aspill.ap()[half * HALF:(half + 1) * HALF,
                                       h * 128:(h + 1) * 128].rearrange(
                        "(q p) e -> p q e", p=128)
                    s.dma_start(view,
                                o_sb[:, i4 * 1024:(i4 + 1) * 1024]
                                .rearrange("p (q e) -> p q e", e=128)
                                ).then_inc(s_osp[i4].sem, 16)
                st["osb_used"][i4] = s_osp[i4].bump(16)
            st["ph2_pe"] = s_po.n

            st["prev_scan_pe"] = s_po.n
            st["prev_grp_tr"] = qtr

    def p2_half(half, skip_A, do_combine=False):
        p2_tr_entry = s_tr.n
        dv_entry = s_dv.n
        ac_entry = s_ac.n
        mm_entry = s_mm.n

        if not skip_A:
            @block.sync
            def _(s):
                s.wait_ge(s_dv.sem, dv_entry)
                s.wait_ge(s_ac.sem, ac_entry)
                s.wait_ge(s_mm.sem, mm_entry)
                s.dma_start(hT[:, 0:16384],
                            hspill.ap()[:, 0:16384]).then_inc(s_hr.sem, 16)

            @block.scalar
            def _(sc):
                sc.wait_ge(s_dv.sem, dv_entry)
                sc.wait_ge(s_mm.sem, mm_entry)
                sc.dma_start(hT[:, 16384:32768],
                             hspill.ap()[:, 16384:32768]).then_inc(s_hr.sem, 16)
            hT_wait = (s_hr, s_hr.bump(32))
        else:
            hT_wait = (s_tr, s_tr.n)
        hT_ready = s_tr.n

        b_scan_entry = (s_msk.n, s_oe.n)
        b_guard = dict(kvp=s_kvp.n, kvu=s_kvu.n, po=s_po.n,
                       osp=[s_osp[ii].n for ii in range(4)],
                       qsp=s_qsp.n, msk=s_msk.n, oe=s_oe.n)
        pend_gev = []

        def flush_gev():
            evs = list(pend_gev)
            pend_gev.clear()

            @block.scalar
            def _(sc):
                for (fcb, pslot, mmv) in evs:
                    sc.wait_ge(s_mm.sem, mmv)
                    if fcb == 0:
                        sc.wait_ge(s_kvp.sem, b_guard["kvp"])
                        sc.wait_ge(s_kvu.sem, b_guard["kvu"])
                        sc.wait_ge(s_po.sem, b_guard["po"])
                        for ii in range(4):
                            sc.wait_ge(s_osp[ii].sem, b_guard["osp"][ii])
                        sc.wait_ge(s_qsp.sem, b_guard["qsp"])
                        sc.wait_ge(s_msk.sem, b_guard["msk"])
                        sc.wait_ge(s_oe.sem, b_guard["oe"])
                        sc.wait_ge(s_dv.sem, dv_entry)
                        sc.wait_ge(s_tr.sem, hT_ready)
                    sc.activation(gnT[:, fcb * 1024:(fcb + 1) * 1024].rearrange(
                        "p (a b) -> p a b", b=512),
                        ppair(pslot), AF.Sigmoid).then_inc(s_ev[pslot].sem, 1)

        for fc in range(NIC):
            extra = [(s_mm, mm_entry), (s_ac, ac_entry)] if fc < 2 else ()
            wslot, wlv = load_wcb(gatep_d[fc], extra=extra)
            pslot = fc % 2

            @block.tensor
            def _(t):
                t.wait_ge(s_wl[wslot].sem, wlv)
                if fc < 2:
                    t.wait_ge(hT_wait[0].sem, hT_wait[1])
                    t.wait_ge(s_dv.sem, dv_entry)
                    if b_scan_entry[0] > 0:
                        t.wait_ge(s_msk.sem, b_scan_entry[0])
                    if b_scan_entry[1] > 0:
                        t.wait_ge(s_oe.sem, b_scan_entry[1])
                c, v = st["ps_main"][pslot]
                if c is not None:
                    t.wait_ge(c.sem, v)
                for ic in range(NIC):
                    lhsT = wbuf[:, wslot * 4096 + ic * 128:wslot * 4096 + (ic + 1) * 128]
                    t.matmul(ps[:, 2 * pslot, :], lhsT,
                             hT[:, ic * 1024:ic * 1024 + 512],
                             start=(ic == 0), stop=(ic == NIC - 1))
                    ins = t.matmul(ps[:, 2 * pslot + 1, :], lhsT,
                                   hT[:, ic * 1024 + 512:(ic + 1) * 1024],
                                   start=(ic == 0), stop=(ic == NIC - 1))
                    if ic == NIC - 1:
                        ins.then_inc(s_mm.sem, 1)
            st["wl_freed"][wslot] = s_mm.bump()
            mmv = s_mm.n

            pend_gev.append((fc, pslot, mmv))
            s_ev[pslot].bump()
            st["ps_main"][pslot] = (s_ev[pslot], s_ev[pslot].n)
            if fc % 4 == 3:
                flush_gev()
        b_mm_done = s_mm.n

        if do_combine:
            emit_combine()

        @block.gpsimd
        def _(g):
            g.wait_ge(s_mm.sem, b_mm_done)
            g.wait_ge(s_tr.sem, p2_tr_entry)
            g.wait_ge(s_dv.sem, st["comb_done"])
            g.dma_start(nw_bc, nwp_d).then_inc(s_nw.sem, 16)
        nw_v = s_nw.bump(16)

        osp_marks = [s_osp[ii].n for ii in range(4)]

        @block.sync
        def _(s):
            s.wait_ge(s_mm.sem, b_mm_done)
            for ii in range(4):
                s.wait_ge(s_osp[ii].sem, osp_marks[ii])
            view = hT.rearrange("p (sub f) -> p sub f", f=4096)[:, 0:4, :]
            src = aspill.ap()[half * HALF:half * HALF + 512, :].rearrange(
                "(sub p) f -> p sub f", p=128)
            s.dma_start(view, src).then_inc(s_att.sem, 16)

        @block.scalar
        def _(sc):
            sc.wait_ge(s_mm.sem, b_mm_done)
            for ii in range(4):
                sc.wait_ge(s_osp[ii].sem, osp_marks[ii])
            view = hT.rearrange("p (sub f) -> p sub f", f=4096)[:, 4:8, :]
            src = aspill.ap()[half * HALF + 512:(half + 1) * HALF, :].rearrange(
                "(sub p) f -> p sub f", p=128)
            sc.dma_start(view, src).then_inc(s_att.sem, 16)
        attn_v = s_att.bump(32)

        qsp_total = s_qsp.n
        for h in range(H):
            qslot = h % 2

            def qrd_load(q):
                if st["qrd_used"][qslot] > 0:
                    q.wait_ge(s_pe.sem, st["qrd_used"][qslot])
                if h < 2:
                    q.wait_ge(s_dv.sem, st["comb_done"])
                    if st["outsb_used"] > 0:
                        q.wait_ge(s_ou.sem, st["outsb_used"])
                q.wait_ge(s_qsp.sem, qsp_total)
                q.dma_start(qrd[:, qslot * 1024:(qslot + 1) * 1024],
                            qspill.ap()[half, h]).then_inc(s_rd[qslot].sem, 16)

            if qslot == 0:
                @block.sync
                def _(s):
                    qrd_load(s)
            else:
                @block.scalar
                def _(sc):
                    qrd_load(sc)
            qv = s_rd[qslot].bump(16)

            for m in range(NBLK_HALF):
                gm = half * NBLK_HALF + m
                cslot = (h * NBLK_HALF + m) % 2

                @block.tensor
                def _(t):
                    t.wait_ge(s_rd[qslot].sem, qv)
                    if h == 0 and m == 0:
                        t.wait_ge(s_dv.sem, max(st["comb_done"], dv_entry))
                    if st["ps_c"][cslot] > 0:
                        t.wait_ge(s_dv.sem, st["ps_c"][cslot])
                    for tqc in range(2):
                        ins = t.matmul(
                            pbank(4 + cslot, tqc * 128, (tqc + 1) * 128),
                            qrd[:, qslot * 1024 + m * 256 + tqc * 128:
                                qslot * 1024 + m * 256 + (tqc + 1) * 128],
                            kvst_bf[:, h * 128:(h + 1) * 128],
                            start=True, stop=True)
                        if tqc == 1:
                            ins.then_inc(s_pe.sem, 1)
                pe_v = s_pe.bump()

                @block.vector
                def _(v):
                    v.wait_ge(s_pe.sem, pe_v)
                    v.wait_ge(s_att.sem, attn_v)
                    attn_vw = hT.rearrange("p (sub f) -> p sub f", f=4096)[
                        :, m * 2:m * 2 + 2, h * 128:(h + 1) * 128]
                    psv = pbank(4 + cslot, 0, 256).rearrange(
                        "p (q e) -> p q e", e=128)
                    v.scalar_tensor_tensor(attn_vw, psv, Dm_pow[h][gm], attn_vw,
                                           ALU.mult, ALU.add).then_inc(s_dv.sem, 1)
                st["ps_c"][cslot] = s_dv.bump()
            st["qrd_used"][qslot] = s_pe.n
        corr_done = s_dv.n

        for sub in range(NSUB):
            attn_t = hT[:, sub * 4096:(sub + 1) * 4096]

            @block.scalar
            def _(sc):
                if sub == 0:
                    sc.wait_ge(s_dv.sem, corr_done)
                else:
                    sc.wait_ge(s_ac.sem, s_ac.n)
                sc.activation(dump, attn_t[:, 0:2048], AF.Square,
                              accum_out=stats[:, sub:sub + 1]).then_inc(s_ac.sem, 1)
                sc.wait_ge(s_ac.sem, s_ac.bump())
                ins = sc.activation(dump, attn_t[:, 2048:4096], AF.Square,
                                    accum_out=stats[:, 8 + sub:9 + sub])
                if sub == NSUB - 1:
                    ins.then_inc(s_st.sem, 1)
                else:
                    ins.then_inc(s_ac.sem, 1)
                    s_ac.bump()
        sq_all = s_st.bump()

        @block.vector
        def _(v):
            v.wait_ge(s_st.sem, sq_all)
            chain(v, v.tensor_tensor(stats[:, 0:8], stats[:, 0:8],
                                     stats[:, 8:16], ALU.add))
            v.tensor_scalar(stats[:, 16:24], stats[:, 0:8],
                            1.0 / D, EPS, ALU.mult, ALU.add).then_inc(s_dv.sem, 1)
        var_v = s_dv.bump()

        @block.scalar
        def _(sc):
            sc.wait_ge(s_dv.sem, var_v)
            sc.activation(stats[:, 16:24], stats[:, 16:24],
                          AF.Sqrt).then_inc(s_ac.sem, 1)
        std_v = s_ac.bump()

        @block.vector
        def _(v):
            v.wait_ge(s_ac.sem, std_v)
            v.reciprocal(stats[:, 24:32], stats[:, 16:24]).then_inc(s_dv.sem, 1)
        rstd_v = s_dv.bump()

        gn_marks = []
        for sub in range(NSUB):
            attn_t = hT[:, sub * 4096:(sub + 1) * 4096]
            hb = h_bf if sub % 2 == 0 else h_bf2
            hb_key = "hbf2_free" if sub % 2 else "hbf_free"

            @block.vector
            def _(v):
                if sub == 0:
                    v.wait_ge(s_dv.sem, rstd_v)
                    v.wait_ge(s_nw.sem, nw_v)
                v.wait_ge(s_tr.sem, st[hb_key])
                v.scalar_tensor_tensor(hb, attn_t,
                                       stats[:, 24 + sub:25 + sub],
                                       nw_bc, ALU.mult, ALU.mult).then_inc(
                    s_dv.sem, 1)
            norm_v = s_dv.bump()

            @block.sync
            def _(s):
                s.wait_ge(s_dv.sem, norm_v)
                if st["ntmp_used"] > 0:
                    s.wait_ge(s_dv.sem, st["ntmp_used"])
                tr_dma(s, ntmp.rearrange("p (c t) -> p c t", t=128),
                       hb.rearrange("p (c f) -> p c f", f=128))
            ntr = s_tr.n
            st[hb_key] = ntr

            @block.vector
            def _(v):
                v.wait_ge(s_tr.sem, ntr)
                gv = gnT.rearrange("p (c t) -> p c t", t=1024)[
                    :, :, sub * 128:(sub + 1) * 128]
                v.tensor_tensor(gv, gv,
                                ntmp.rearrange("p (c t) -> p c t", t=128),
                                ALU.mult).then_inc(s_dv.sem, 1)
            st["ntmp_used"] = s_dv.bump()
            gn_marks.append(s_dv.n)
        gn_done = s_dv.n

        for ocb in range(NOCB):
            wslot = ocb % 2

            @block.gpsimd
            def _(g):
                if st["cw_used"][wslot] > 0:
                    g.wait_ge(s_mm.sem, st["cw_used"][wslot])
                if ocb == 0:
                    g.wait_ge(s_mm.sem, max(st["wl_freed"][0],
                                            st["wl_freed"][1]))
                    g.wait_ge(s_dv.sem, dv_entry)
                elif ocb == 1:
                    g.wait_ge(s_mm.sem, max(st["wl_freed"][0],
                                            st["wl_freed"][1]))
                    g.wait_ge(s_dv.sem, gn_done)
                g.dma_start(woc[:, wslot * 8192:(wslot + 1) * 8192],
                            outp_d[ocb]).then_inc(s_wl[wslot].sem, 16)
            wlv = s_wl[wslot].bump(16)

            @block.tensor
            def _(t):
                t.wait_ge(s_wl[wslot].sem, wlv)
                if ocb == 0:
                    c0, v0 = st["ps_main"][0]
                    c1, v1 = st["ps_main"][1]
                    if c0 is not None:
                        t.wait_ge(c0.sem, v0)
                    if c1 is not None:
                        t.wait_ge(c1.sem, v1)
                if st["ctm_ps"] > 0:
                    t.wait_ge(s_dv.sem, st["ctm_ps"])
                for tt in range(8):
                    if ocb == 0:
                        t.wait_ge(s_dv.sem, gn_marks[tt])
                    for ic in range(NIC):
                        rhs = woc[:, wslot * 8192 + ic * 256:
                                  wslot * 8192 + (ic + 1) * 256]
                        ins = t.matmul(pbank(tt // 2, (tt % 2) * 256,
                                             (tt % 2 + 1) * 256),
                                       gnT[:, ic * 1024 + tt * 128:
                                           ic * 1024 + (tt + 1) * 128],
                                       rhs, start=(ic == 0), stop=(ic == NIC - 1))
                        if ic == NIC - 1 and tt == 7:
                            ins.then_inc(s_mm.sem, 1)
            st["cw_used"][wslot] = s_mm.bump()
            cmm_v = s_mm.n

            @block.vector
            def _(v):
                v.wait_ge(s_mm.sem, cmm_v)
                v.tensor_copy(
                    hT.rearrange("p (sub f) -> p sub f", f=4096)[
                        :, :, ocb * 256:(ocb + 1) * 256],
                    ps[:, 0:4, :].rearrange("p b (c o) -> p (b c) o", o=256)
                ).then_inc(s_dv.sem, 1)
            st["ctm_ps"] = s_dv.bump()
        c_evac_done = s_dv.n
        st["ps_main"][0] = (s_dv, c_evac_done)
        st["ps_main"][1] = (s_dv, c_evac_done)

        for sub in range(NSUB):
            slot = sub % 2
            xb = xbuf[:, slot * 4096:(slot + 1) * 4096]

            @block.gpsimd
            def _(g):
                c, v = st["x_users"][slot]
                if c is not None:
                    g.wait_ge(c.sem, v)
                g.wait_ge(s_mm.sem, max(st["cw_used"][0], st["cw_used"][1]))
                g.dma_start(xb, x_d[half * HALF + sub * 128:
                                    half * HALF + (sub + 1) * 128, :]).then_inc(
                    s_x[slot].sem, 16)
            xv = s_x[slot].bump(16)

            @block.vector
            def _(v):
                v.wait_ge(s_x[slot].sem, xv)
                v.wait_ge(s_dv.sem, c_evac_done)
                x2s = hT[:, sub * 4096:(sub + 1) * 4096]
                v.tensor_tensor(x2s, x2s, xb, ALU.add).then_inc(s_dv.sem, 1)
            st["x_users"][slot] = (s_dv, s_dv.bump())
        x2tm_done_dv = s_dv.n

        @block.gpsimd
        def _(g):
            g.wait_ge(s_dv.sem, x2tm_done_dv)
            g.dma_start(bias_bc, biasp_d).then_inc(s_bia.sem, 16)
        bias_v = s_bia.bump(16)

        for sub in range(NSUB):
            x2t = hT[:, sub * 4096:(sub + 1) * 4096]

            @block.scalar
            def _(sc):
                if sub == 0:
                    sc.wait_ge(s_dv.sem, x2tm_done_dv)
                else:
                    sc.wait_ge(s_ac.sem, s_ac.n)
                sc.activation(dump, x2t[:, 0:2048], AF.Square,
                              accum_out=stats[:, 8 + sub:9 + sub]).then_inc(
                    s_ac.sem, 1)
                sc.wait_ge(s_ac.sem, s_ac.bump())
                ins = sc.activation(dump, x2t[:, 2048:4096], AF.Square,
                                    accum_out=stats[:, 16 + sub:17 + sub])
                if sub == NSUB - 1:
                    ins.then_inc(s_st.sem, 1)
                else:
                    ins.then_inc(s_ac.sem, 1)
                    s_ac.bump()
        for sub in range(NSUB):
            x2t = hT[:, sub * 4096:(sub + 1) * 4096]

            @block.vector
            def _(v):
                if sub == 0:
                    v.wait_ge(s_dv.sem, x2tm_done_dv)
                ins = v.reduce_sum(stats[:, sub:sub + 1], x2t, axis=AX.X)
                if sub == NSUB - 1:
                    ins.then_inc(s_dv.sem, 1)
        sq_all = s_st.bump()
        sum_all = s_dv.bump()

        @block.vector
        def _(v):
            v.wait_ge(s_st.sem, sq_all)
            v.wait_ge(s_dv.sem, sum_all)
            chain(v, v.tensor_tensor(stats[:, 8:16], stats[:, 8:16],
                                     stats[:, 16:24], ALU.add))
            chain(v, v.tensor_scalar(stats[:, 0:8], stats[:, 0:8],
                                     -1.0 / D, None, ALU.mult))
            chain(v, v.tensor_scalar(stats[:, 8:16], stats[:, 8:16],
                                     1.0 / D, EPS, ALU.mult, ALU.add))
            chain(v, v.tensor_tensor(stats[:, 16:24], stats[:, 0:8],
                                     stats[:, 0:8], ALU.mult))
            v.tensor_tensor(stats[:, 8:16], stats[:, 8:16],
                            stats[:, 16:24], ALU.subtract).then_inc(s_dv.sem, 1)
        var_v = s_dv.bump()

        @block.scalar
        def _(sc):
            sc.wait_ge(s_dv.sem, var_v)
            sc.activation(stats[:, 8:16], stats[:, 8:16],
                          AF.Sqrt).then_inc(s_ac.sem, 1)
        std_v = s_ac.bump()

        @block.vector
        def _(v):
            v.wait_ge(s_ac.sem, std_v)
            v.reciprocal(stats[:, 24:32], stats[:, 8:16]).then_inc(s_dv.sem, 1)
        rstd_v = s_dv.bump()

        for sub in range(NSUB):
            x2t = hT[:, sub * 4096:(sub + 1) * 4096]
            hb = h_bf if sub % 2 == 0 else h_bf2
            hb_key = "hbf2_free" if sub % 2 else "hbf_free"

            @block.vector
            def _(v):
                if sub == 0:
                    v.wait_ge(s_dv.sem, rstd_v)
                    v.wait_ge(s_bia.sem, bias_v)
                v.wait_ge(s_tr.sem, st[hb_key])
                chain(v, v.tensor_scalar(hb, x2t, stats[:, sub:sub + 1],
                                         stats[:, 24 + sub:25 + sub],
                                         ALU.add, ALU.mult))
                v.tensor_tensor(x2t, x2t, bias_bc, ALU.add).then_inc(s_dv.sem, 1)
            h2_v = s_dv.bump()

            @block.sync
            def _(s):
                s.wait_ge(s_dv.sem, h2_v)
                tr_dma(s,
                       gnT.rearrange("p (c t) -> p c t", t=1024)[
                           :, :, sub * 128:(sub + 1) * 128],
                       hb.rearrange("p (c f) -> p c f", f=128))
            st[hb_key] = s_tr.n
        h2T_done = s_tr.n

        @block.gpsimd
        def _(g):
            if st["eo_user"] > 0:
                g.wait_ge(s_pe.sem, st["eo_user"])
            g.dma_start(eo_bf, icsp_d).then_inc(s_eo.sem, 16)
        ics_v = s_eo.bump(16)

        @block.tensor
        def _(t):
            t.wait_ge(s_tr.sem, h2T_done)
            t.wait_ge(s_cst.sem, CST_TOTAL)
            t.wait_ge(s_rw.sem, 16)
            t.wait_ge(s_dv.sem, dv_entry)
            for ic in range(NIC):
                for nh in range(2):
                    ins = t.matmul(ps[0:8, 6 + nh, :],
                                   rw_bf[:, ic * 8:(ic + 1) * 8],
                                   gnT[:, ic * 1024 + nh * 512:
                                       ic * 1024 + (nh + 1) * 512],
                                   start=(ic == 0), stop=(ic == NIC - 1))
                    if ic == NIC - 1 and nh == 1:
                        ins.then_inc(s_pe.sem, 1)
        rtr_mm = s_pe.bump()

        @block.scalar
        def _(sc):
            sc.wait_ge(s_pe.sem, rtr_mm)
            sc.activation(expsb, ps[0:8, 6:8, :], AF.Exp).then_inc(s_ac.sem, 1)
        exp_v = s_ac.bump()

        @block.tensor
        def _(t):
            t.wait_ge(s_ac.sem, exp_v)
            for nh in range(2):
                ins = t.matmul(ps[0:1, 6 + nh, :], ones_sb[0:8, 0:1],
                               expsb[:, nh * 512:(nh + 1) * 512],
                               start=True, stop=True)
                if nh == 1:
                    ins.then_inc(s_pe.sem, 1)
        sum_v = s_pe.bump()

        @block.vector
        def _(v):
            v.wait_ge(s_pe.sem, sum_v)
            chain(v, v.reciprocal(rrow,
                                  ps[0:1, 6:8, :].rearrange("p a b -> p (a b)")))
            v.tensor_copy(rrow_bf, rrow).then_inc(s_dv.sem, 1)
        rcp_v = s_dv.bump()

        @block.tensor
        def _(t):
            t.wait_ge(s_dv.sem, rcp_v)
            for nh in range(2):
                ins = t.matmul(ps[0:8, 6 + nh, :], ones_sb[0:1, 0:8],
                               rrow_bf[:, nh * 512:(nh + 1) * 512],
                               start=True, stop=True)
                if nh == 1:
                    ins.then_inc(s_pe.sem, 1)
        bc_v = s_pe.bump()

        @block.vector
        def _(v):
            v.wait_ge(s_pe.sem, bc_v)
            v.tensor_tensor(rT.rearrange("p (a b) -> p a b", b=512),
                            expsb.rearrange("p (a b) -> p a b", b=512),
                            ps[0:8, 6:8, :], ALU.mult).then_inc(s_dv.sem, 1)
        rT_v = s_dv.bump()

        for ic in range(NIC):
            pslot = ic % 2

            @block.tensor
            def _(t):
                if ic == 0:
                    t.wait_ge(s_dv.sem, rT_v)
                    t.wait_ge(s_eo.sem, ics_v)
                if ic < 2:
                    t.wait_ge(s_dv.sem, c_evac_done)
                else:
                    t.wait_ge(s_dv.sem, is_dv_guard[pslot])
                for nh in range(2):
                    ins = t.matmul(ps[:, 2 * pslot + nh, :],
                                   eo_bf[:, ic * 128:(ic + 1) * 128],
                                   rT[:, nh * 512:(nh + 1) * 512],
                                   start=True, stop=True)
                    if nh == 1:
                        ins.then_inc(s_pe.sem, 1)
            is_mm = s_pe.bump()

            @block.vector
            def _(v):
                v.wait_ge(s_pe.sem, is_mm)
                hc = gnT[:, ic * 1024:(ic + 1) * 1024].rearrange(
                    "p (a b) -> p a b", b=512)
                v.tensor_tensor(hc, hc, ppair(pslot), ALU.mult).then_inc(
                    s_dv.sem, 1)
            is_dv_guard[pslot] = s_dv.bump()
        st["eo_user"] = s_pe.n
        xsT_done = s_dv.n

        @block.gpsimd
        def _(g):
            g.wait_ge(s_pe.sem, st["eo_user"])
            g.dma_start(eo_bf, ocsp_d).then_inc(s_eo.sem, 16)
        ocs_v = s_eo.bump(16)

        for ocb in range(NOCB):
            wslot = ocb % 2

            @block.gpsimd
            def _(g):
                if st["woc_used"][wslot] > 0:
                    g.wait_ge(s_mm.sem, st["woc_used"][wslot])
                if ocb < 2:
                    g.wait_ge(s_mm.sem, max(st["cw_used"][0],
                                            st["cw_used"][1]))
                    g.wait_ge(s_dv.sem, rT_v)
                    g.wait_ge(s_pe.sem, sum_v)
                g.dma_start(woc[:, wslot * 8192:(wslot + 1) * 8192],
                            ffnp_d[ocb]).then_inc(s_wl[wslot].sem, 16)
            wlv = s_wl[wslot].bump(16)

            sign_vs = []

            @block.scalar
            def _(sc):
                sc.wait_ge(s_wl[wslot].sem, wlv)
                for j in range(4):
                    wv = woc[:, wslot * 8192 + j * 2048:
                             wslot * 8192 + (j + 1) * 2048]
                    sc.activation(wv, wv, AF.Sign).then_inc(s_ac.sem, 1)
                    sign_vs.append(s_ac.bump())

            @block.tensor
            def _(t):
                if ocb == 0:
                    t.wait_ge(s_eo.sem, ocs_v)
                    t.wait_ge(s_dv.sem, rT_v)
                if st["os_used"] > 0:
                    t.wait_ge(s_dv.sem, st["os_used"])
                for tt in range(8):
                    ins = t.matmul(pbank(4 + tt // 2, (tt % 2) * 256,
                                         (tt % 2 + 1) * 256),
                                   rT[:, tt * 128:(tt + 1) * 128],
                                   eo_bf[:, ocb * 256:(ocb + 1) * 256],
                                   start=True, stop=True)
                    if tt == 7:
                        ins.then_inc(s_pe.sem, 1)

            os_mm = s_pe.bump()

            @block.vector
            def _(v):
                v.wait_ge(s_pe.sem, os_mm)
                v.tensor_copy(os_sb.rearrange("p (a b) -> p a b", b=256),
                              ps[:, 4:8, :].rearrange("p b (c o) -> p (b c) o",
                                                      o=256)).then_inc(s_dv.sem, 1)
            os_ev = s_dv.bump()
            st["os_used"] = os_ev

            @block.tensor
            def _(t):
                if ocb == 0:
                    t.wait_ge(s_dv.sem, xsT_done)
                if st["ffn_ps_used"] > 0:
                    t.wait_ge(s_dv.sem, st["ffn_ps_used"])
                for tt in range(8):
                    for ic in range(NIC):
                        if tt == 0 and ic % 8 == 0:
                            t.wait_ge(s_ac.sem, sign_vs[ic // 8])
                        rhs = woc[:, wslot * 8192 + ic * 256:
                                  wslot * 8192 + (ic + 1) * 256]
                        ins = t.matmul(pbank(tt // 2, (tt % 2) * 256,
                                             (tt % 2 + 1) * 256),
                                       gnT[:, ic * 1024 + tt * 128:
                                           ic * 1024 + (tt + 1) * 128],
                                       rhs, start=(ic == 0), stop=(ic == NIC - 1))
                        if ic == NIC - 1 and tt == 7:
                            ins.then_inc(s_mm.sem, 1)
            st["woc_used"][wslot] = s_mm.bump()
            ffn_mm = s_mm.n

            @block.vector
            def _(v):
                v.wait_ge(s_mm.sem, ffn_mm)
                v.wait_ge(s_dv.sem, os_ev)
                if st["outsb_used"] > 0:
                    v.wait_ge(s_ou.sem, st["outsb_used"])
                for tt in range(8):
                    pf = pbank(tt // 2, (tt % 2) * 256, (tt % 2 + 1) * 256)
                    ot = outsb[:, tt * 256:(tt + 1) * 256]
                    chain(v, v.tensor_tensor(ot, pf,
                                             os_sb[:, tt * 256:(tt + 1) * 256],
                                             ALU.mult))
                    bx = hT[:, tt * 4096 + ocb * 256:tt * 4096 + (ocb + 1) * 256]
                    ins = v.tensor_tensor(ot, ot, bx, ALU.add)
                    if tt == 7:
                        ins.then_inc(s_dv.sem, 1)
            ev_v = s_dv.bump()
            st["ffn_ps_used"] = ev_v

            @block.sync
            def _(s):
                s.wait_ge(s_dv.sem, ev_v)
                s.wait_ge(s_ou.sem, s_ou.n)
                view = out_d[half * HALF:(half + 1) * HALF,
                             ocb * 256:(ocb + 1) * 256].rearrange(
                    "(tt p) o -> p tt o", p=128)
                s.dma_start(view, outsb.rearrange("p (tt o) -> p tt o", o=256)
                            ).then_inc(s_ou.sem, 16)
            st["outsb_used"] = s_ou.bump(16)

    is_dv_guard = [0, 0]

    def emit_combine():
        tr_entry = s_tr.n
        mk0 = s_msk.n

        @block.vector
        def _(v):
            v.memset(kvst_bf, 0.0)
            v.engine_nop().then_inc(s_dv.sem, 1)
        kvz = s_dv.bump()

        for rk in range(NCORE):
            gs = rk % 2

            @block.gpsimd
            def _(g):
                if rk < 2:
                    g.wait_ge(s_cc.sem, 1)
                    g.wait_ge(s_tr.sem, tr_entry)
                    g.wait_ge(s_msk.sem, mk0)
                else:
                    g.wait_ge(s_dv.sem, st["gb_free"][gs])
                g.dma_start(gbuf[:, gs * 4096:(gs + 1) * 4096],
                            gth_out.ap()[rk * 128:(rk + 1) * 128, :]).then_inc(
                    s_gb[gs].sem, 16)
            gv = s_gb[gs].bump(16)

            @block.vector
            def _(v):
                v.wait_ge(s_gb[gs].sem, gv)
                v.wait_ge(s_dv.sem, kvz if rk == 0 else st["comb_prev"])
                for h in range(H):
                    kc = kvst_bf[:, h * 128:(h + 1) * 128]
                    ins = v.scalar_tensor_tensor(
                        kc, gbuf[:, gs * 4096 + h * 128:gs * 4096 + (h + 1) * 128],
                        wcomb_sb[:, rk * H + h:rk * H + h + 1], kc,
                        ALU.mult, ALU.add)
                    if h == H - 1:
                        ins.then_inc(s_dv.sem, 1)
            st["gb_free"][gs] = s_dv.bump()
            st["comb_prev"] = s_dv.n
        st["comb_done"] = s_dv.n

    p1_half(0)
    mm_after_h0 = s_mm.n

    @block.sync
    def _(s):
        s.wait_ge(s_mm.sem, mm_after_h0)
        s.dma_start(hspill.ap(), hT).then_inc(s_hs.sem, 16)
    s_hs.bump(16)

    p1_half(1, hT_guard_mm=mm_after_h0, spill_wait=16)

    kvu_total = s_kvu.n

    @block.sync
    def _(s):
        s.wait_ge(s_kvu.sem, kvu_total)
        s.dma_start(gth_in.ap(), kv_all).then_inc(s_sum.sem, 16)
    sum_out = s_sum.bump(16)

    @block.gpsimd
    def _(g):
        g.wait_ge(s_sum.sem, sum_out)
        g.collective_compute(
            "AllGather", ALU.bypass,
            replica_groups=[list(range(NCORE))],
            ins=[gth_in.ap()],
            outs=[gth_out.ap()],
        ).then_inc(s_cc.sem, 1)

    p2_half(1, skip_A=True, do_combine=True)
    p2_half(0, skip_A=False)

    ou_total = s_ou.n

    @block.sync
    def _(s):
        s.wait_ge(s_ou.sem, ou_total)

    block.__exit__(None, None, None)
    return nc


def _prep_inputs(inputs):
    x = np.ascontiguousarray(np.asarray(inputs["L0_attn_hidden_states"], np.float32))
    qkv_w = np.asarray(inputs["L0_attn_qkv_w"], np.float32)
    out_w = np.asarray(inputs["L0_attn_out_w"], np.float32)
    gate_w = np.asarray(inputs["L0_attn_gate_w"], np.float32)
    norm_w = np.asarray(inputs["L0_attn_norm_w"], np.float32)
    ffn_w = np.asarray(inputs["L0_ffn_weight"], np.float32)
    ffn_b = np.asarray(inputs["L0_ffn_bias"], np.float32)
    rw = np.asarray(inputs["L0_ffn_gate_w"], np.float32)
    ics = np.asarray(inputs["L0_ffn_in_channel_scale"], np.float32)
    ocs = np.asarray(inputs["L0_ffn_out_channel_scale"], np.float32)

    t = qkv_w.reshape(8, HG, 3, DH, NIC, 128)
    qkvp = np.ascontiguousarray(
        t.transpose(0, 2, 1, 5, 4, 3).reshape(96, 128, D))
    gatep = np.ascontiguousarray(
        gate_w.reshape(NIC, 128, NIC, 128).transpose(0, 3, 2, 1).reshape(NIC, 128, D))
    outp = np.ascontiguousarray(
        out_w.reshape(NOCB, OCB, NIC, 128).transpose(0, 3, 2, 1).reshape(
            NOCB, 128, NIC * OCB))
    ffnp = np.ascontiguousarray(
        ffn_w.reshape(NOCB, OCB, NIC, 128).transpose(0, 3, 2, 1).reshape(
            NOCB, 128, NIC * OCB))
    rwp = np.ascontiguousarray(
        rw.T.reshape(NIC, 128, 8).transpose(1, 0, 2).reshape(128, NIC * 8))
    nwp = np.ascontiguousarray(np.broadcast_to(norm_w[None, :], (128, D)))
    biasp = np.ascontiguousarray(np.broadcast_to(ffn_b[None, :], (128, D)))

    maskp, qdlp, kdecp, Dm, onesp = _consts()

    shared = dict(qkvp=qkvp, gatep=gatep, outp=outp, ffnp=ffnp, rwp=rwp,
                  icsp=np.ascontiguousarray(ics), ocsp=np.ascontiguousarray(ocs),
                  nwp=nwp, biasp=biasp, maskp=np.ascontiguousarray(maskp),
                  qdlp=qdlp, kdecp=kdecp, onesp=onesp)

    in_maps = []
    for c in range(NCORE):
        b, jj = c // 4, c % 4
        m = dict(shared)
        m["x_sh"] = np.ascontiguousarray(x[b, jj * TCORE:(jj + 1) * TCORE, :])
        m["wcombp"] = _wcomb_for_core(c, Dm)
        in_maps.append(m)
    return in_maps


def kernel(**inputs):
    in_maps = _prep_inputs(inputs)
    nc = _build_nc()
    res = run_bass_kernel_spmd(nc, in_maps, core_ids=list(range(NCORE)))
    out = np.empty((B, S, D), dtype=np.float32)
    for c in range(NCORE):
        b, jj = c // 4, c % 4
        out[b, jj * TCORE:(jj + 1) * TCORE, :] = np.asarray(res.results[c]["out_sh"])
    return out


# version 13
# speedup vs baseline: 1.0474x; 1.0474x over previous
import numpy as np
import ml_dtypes

import concourse.bass as bass
import concourse.mybir as mybir
from concourse.bass_utils import run_bass_kernel_spmd

F32 = mybir.dt.float32
BF16 = mybir.dt.bfloat16
U8 = mybir.dt.uint8
AF = mybir.ActivationFunctionType
ALU = mybir.AluOpType
AX = mybir.AxisListType
BFNP = ml_dtypes.bfloat16

B = 2
S = 8192
D = 4096
H = 32
DH = 128
BLK = 256
NCORE = 8
TCORE = 2048
HALF = 1024
NSUB = 8
NIC = 32
NBLK_HALF = 4
EPS = 1e-5
HG = 4
NGRP = H // HG
OCB = 256
NOCB = D // OCB

DEBUG_SPILLS = False


def _slopes():
    base = 1.0 / 2.0 ** (8.0 / H)
    factor = 1.0 - 1.0 / (31 + 1e-5) + 1e-5
    return (base ** (np.arange(H, dtype=np.float64) + 1.0)) * factor


def _consts():
    sl = _slopes()
    r = np.arange(BLK, dtype=np.float64)
    diff = r[None, :] - r[:, None]
    maskT = np.where(diff >= 0, np.exp(-sl[:, None, None] * diff[None]), 0.0)
    maskp = maskT.reshape(H, 2, 128, BLK).transpose(0, 2, 1, 3).reshape(H, 128, 2 * BLK)
    maskp = maskp.astype(np.float32).astype(BFNP)
    qdl = np.exp(-sl[:, None] * (r[None, :] + 1.0))
    kdec = np.exp(-sl[:, None] * (BLK - 1.0 - r[None, :]))
    qdlp = qdl.reshape(H, 2, 128).transpose(2, 0, 1).reshape(128, 64).astype(np.float32)
    kdecp = kdec.reshape(H, 2, 128).transpose(2, 0, 1).reshape(128, 64).astype(np.float32)
    Dm = np.exp(-sl * BLK)
    onesp = np.ones((128, 136), dtype=BFNP)
    return maskp, qdlp, kdecp, Dm, onesp


def _wcomb_for_core(core, Dm):
    b, jj = core // 4, core % 4
    w = np.zeros((H, NCORE), dtype=np.float64)
    for c in range(jj):
        w[:, b * 4 + c] = Dm ** (8.0 * (jj - 1 - c))
    out = np.zeros((128, NCORE * H), dtype=np.float32)
    for rk in range(NCORE):
        for h in range(H):
            out[:, rk * H + h] = np.float32(w[h, rk])
    return out


class _Ctr:
    __slots__ = ("sem", "n")

    def __init__(self, sem):
        self.sem = sem
        self.n = 0

    def bump(self, k=1):
        self.n += k
        return self.n


def _build_nc():
    nc = bass.Bass("TRN2", target_bir_lowering=False)
    skind = "ExternalOutput" if DEBUG_SPILLS else "Internal"

    x_d = nc.dram_tensor("x_sh", [TCORE, D], F32, kind="ExternalInput").ap()
    xT_d = nc.dram_tensor("xT_sh", [D, TCORE], F32, kind="ExternalInput").ap()
    qkvp_d = nc.dram_tensor("qkvp", [96, 128, D], F32, kind="ExternalInput").ap()
    gatep_d = nc.dram_tensor("gatep", [32, 128, D], F32, kind="ExternalInput").ap()
    outpT_d = nc.dram_tensor("outpT", [32, 128, D], F32,
                             kind="ExternalInput").ap()
    ffnpT_d = nc.dram_tensor("ffnpT", [32, 128, D], F32,
                             kind="ExternalInput").ap()
    rwp_d = nc.dram_tensor("rwp", [128, 256], F32, kind="ExternalInput").ap()
    icsp_d = nc.dram_tensor("icsp", [8, D], F32, kind="ExternalInput").ap()
    ocsp_d = nc.dram_tensor("ocsp", [8, D], F32, kind="ExternalInput").ap()
    nwp_d = nc.dram_tensor("nwp", [128, D], F32, kind="ExternalInput").ap()
    biascp_d = nc.dram_tensor("biascp", [128, 32], F32, kind="ExternalInput").ap()
    maskp_d = nc.dram_tensor("maskp", [H, 128, 512], BF16, kind="ExternalInput").ap()
    qdlp_d = nc.dram_tensor("qdlp", [128, 64], F32, kind="ExternalInput").ap()
    kdecp_d = nc.dram_tensor("kdecp", [128, 64], F32, kind="ExternalInput").ap()
    wcombp_d = nc.dram_tensor("wcombp", [128, 256], F32, kind="ExternalInput").ap()
    onesp_d = nc.dram_tensor("onesp", [128, 136], BF16, kind="ExternalInput").ap()

    outT_d = nc.dram_tensor("out_shT", [D, TCORE], F32, kind="ExternalOutput").ap()

    aspill = nc.dram_tensor("aspill", [TCORE, D], BF16, kind=skind)
    hspill = nc.dram_tensor("hspill", [128, 32768], BF16, kind=skind)
    qspill = nc.dram_tensor("qspill", [2, H, 128, HALF], BF16, kind=skind)
    gth_in = nc.dram_tensor("gth_in", [128, H * DH], F32, kind=skind)
    gth_out = nc.dram_tensor("gth_out", [NCORE * 128, H * DH], F32, kind=skind)

    Dm64 = np.exp(-_slopes() * BLK)
    Dmf = [float(np.float32(v)) for v in Dm64]
    Dm_pow = [[float(np.float32(Dm64[h] ** m)) for m in range(8)] for h in range(H)]

    A1_OFF = 0
    A2_OFF = 65536
    A3_OFF = 131072
    FX_OFF = A3_OFF + 49152
    fx = {}
    cur = [FX_OFF]

    def fxa(name, nbytes):
        fx[name] = cur[0]
        cur[0] += (nbytes + 31) // 32 * 32

    fxa("h_bf", 8192)
    fxa("dump", 4096)
    fxa("kvst_bf", 8192)
    fxa("eo_bf", 8192)
    fxa("stats", 128)
    fxa("qdl", 256)
    fxa("kdec", 256)
    fxa("wcomb", 1024)
    fxa("ones", 272)
    fxa("rw_bf", 512)
    fxa("bias_col", 128)
    TOTAL = cur[0]
    assert TOTAL <= 212800, TOTAL

    arena = nc.alloc_sbuf_tensor("arena", [128, TOTAL], U8)
    base = nc.lookup_mloc(arena).addr

    def sb(name, shape, dtype, off):
        return nc.alloc_sbuf_tensor_at(name, shape, dtype, offset=base + off).ap()

    hT = sb("hT", [128, 32768], BF16, A1_OFF)
    q_raw = sb("q_raw", [128, 4096], BF16, A2_OFF + 0)
    qchk = sb("qchk", [128, 4096], BF16, A2_OFF + 8192)
    k_sb = sb("k_sb", [128, 4096], BF16, A2_OFF + 16384)
    ktld = sb("ktld", [128, 4096], BF16, A2_OFF + 24576)
    v_tm = sb("v_tm", [128, 4096], BF16, A2_OFF + 32768)
    kv_hist = sb("kv_hist", [128, 2048], BF16, A2_OFF + 40960)
    o_sb = sb("o_sb", [128, 4096], BF16, A2_OFF + 49152)
    smask = sb("smask", [128, 2048], BF16, A2_OFF + 57344)
    gnT = sb("gnT", [128, 32768], BF16, A2_OFF)
    wbuf = sb("wbuf", [128, 8192], BF16, A3_OFF)
    mstr = sb("mstr", [128, 2048], BF16, A3_OFF + 16384)
    ttmp = sb("ttmp", [128, 2048], BF16, A3_OFF + 20480)
    qtm = sb("qtm", [128, 4096], BF16, A3_OFF + 24576)
    kv_all = sb("kv_all", [128, 4096], F32, A3_OFF + 32768)
    gbuf = sb("gbuf", [128, 8192], BF16, A3_OFF + 16384)
    nw_bc = sb("nw_bc", [128, 4096], BF16, A3_OFF + 24576)
    ntmp = sb("ntmp", [128, 4096], BF16, A3_OFF + 16384)
    qrd = sb("qrd", [128, 2048], BF16, A3_OFF + 32768)
    expsb = sb("expsb", [8, 1024], BF16, A3_OFF + 8192)
    h_bf2 = sb("h_bf2", [128, 4096], BF16, A3_OFF + 36864)
    rrow = sb("rrow", [1, 1024], F32, A3_OFF + 10240)
    rrow_bf = sb("rrow_bf", [1, 1024], BF16, A3_OFF + 14336)
    woc = sb("woc", [128, 8192], BF16, A3_OFF)
    outsb = sb("outsb", [128, 2048], F32, A3_OFF + 16384)
    xt_sb = sb("xt_sb", [128, 2048], BF16, A3_OFF + 32768)
    sq_sb = sb("sq_sb", [128, 2048], BF16, A3_OFF + 36864)
    srow = sb("srow", [1, 2048], F32, A3_OFF + 24576)

    h_bf = sb("h_bf_t", [128, 4096], BF16, fx["h_bf"])
    dump = sb("dump_t", [128, 2048], BF16, fx["dump"])
    kvst_bf = sb("kvst_t", [128, 4096], BF16, fx["kvst_bf"])
    xbuf_p1 = sb("xbuf_p1", [128, 8192], BF16, fx["kvst_bf"])
    eo_bf = sb("eo_bf_t", [8, 4096], BF16, fx["eo_bf"])
    os_sb = sb("os_sb_t", [128, 2048], BF16, A3_OFF + 40960)
    rT = sb("rT_t", [8, 1024], BF16, A3_OFF + 45056)
    stats = sb("stats_t", [128, 32], F32, fx["stats"])
    qdl_sb = sb("qdl_t", [128, 64], F32, fx["qdl"])
    kdec_sb = sb("kdec_t", [128, 64], F32, fx["kdec"])
    wcomb_sb = sb("wcomb_t", [128, 256], F32, fx["wcomb"])
    ones_sb = sb("ones_t", [128, 136], BF16, fx["ones"])
    rw_bf = sb("rw_bf_t", [128, 256], BF16, fx["rw_bf"])
    bias_col = sb("bias_col_t", [128, 32], F32, fx["bias_col"])
    vtmp_row = sb("vtmp_row", [1, 1024], F32, fx["dump"])

    SUMC, SQC, NMC, VARC, M2C, STDC, RSTDC, SQ1C = 0, 1, 2, 3, 4, 5, 6, 7

    pst = nc.alloc_psum_tensor("ps", [128, 8, 512], F32)
    ps = pst.ap()

    def pbank(b_, lo=0, hi=512):
        return ps[:, b_, lo:hi]

    def ppair(s):
        return ps[:, 2 * s:2 * s + 2, :]

    def mk(name):
        return _Ctr(nc.alloc_semaphore(name))

    s_x = [mk("s_x0"), mk("s_x1")]
    s_tr = mk("s_tr")
    s_wl = [mk("s_wl0"), mk("s_wl1")]
    s_mm = mk("s_mm")
    s_ev = [mk("s_ev0"), mk("s_ev1")]
    s_msk = mk("s_msk")
    s_po = mk("s_po")
    s_oe = mk("s_oe")
    s_kvp = mk("s_kvp")
    s_kvu = mk("s_kvu")
    s_osp = [mk("s_osp0"), mk("s_osp1"), mk("s_osp2"), mk("s_osp3")]
    s_qsp = mk("s_qsp")
    s_sum = mk("s_sum")
    s_rd = [mk("s_rd0"), mk("s_rd1")]
    s_att = mk("s_att")
    s_mld = mk("s_mld")
    s_nw = mk("s_nw")
    s_bia = mk("s_bia")
    s_eo = mk("s_eo")
    s_cc = mk("s_cc")
    s_gb = [mk("s_gb0"), mk("s_gb1")]
    s_st = mk("s_st")
    s_dv = mk("s_dv")
    s_ac = mk("s_ac")
    s_ou = mk("s_ou")
    s_pe = mk("s_pe")
    s_cst = mk("s_cst")
    s_rw = mk("s_rw")
    s_hs = mk("s_hs")
    s_hr = mk("s_hr")

    st = {
        "xt_users": [(None, 0), (None, 0)],
        "xp1_users": [(None, 0), (None, 0)],
        "sq_used": [0, 0],
        "srow_add": 0,
        "wl_freed": [0, 0],
        "wl_cnt": [0, 0],
        "ps_main": [(None, 0), (None, 0)],
        "scores_free": [0, 0],
        "smask_free": [0, 0, 0, 0],
        "obank_free": [0, 0, 0, 0],
        "osb_used": [0, 0, 0, 0],
        "mstr_free": 0,
        "head_kvu": {},
        "ph2_pe": 0,
        "ps45_free": 0,
        "cw_used": [0, 0],
        "ctm_ps": [0, 0],
        "kvps_free": [0, 0, 0, 0],
        "v_tr": [0, 0, 0, 0],
        "ttmp_user": [(None, 0), (None, 0)],
        "prev_scan_pe": 0,
        "prev_spill": 0,
        "qrd_used": [0, 0],
        "ps_c": [0, 0],
        "ntmp_used": 0,
        "woc_used": [0, 0],
        "os_used": [0, 0],
        "ffn_ps_used": [0, 0],
        "outsb_used": [0, 0],
        "comb_done": 0,
        "gb_free": [0, 0],
        "prev_grp_tr": 0,
        "comb_prev": 0,
        "eo_user": 0,
        "hbf_free": 0,
        "hbf2_free": 0,
    }

    block = nc.Block()
    block.__enter__()

    @block.sync
    def _(s):
        s.dma_start(qdl_sb, qdlp_d).then_inc(s_cst.sem, 16)
        s.dma_start(kdec_sb, kdecp_d).then_inc(s_cst.sem, 16)
        s.dma_start(wcomb_sb, wcombp_d).then_inc(s_cst.sem, 16)
        s.dma_start(ones_sb, onesp_d).then_inc(s_cst.sem, 16)
        s.dma_start(bias_col, biascp_d).then_inc(s_cst.sem, 16)
    s_cst.bump(80)

    @block.gpsimd
    def _(g):
        g.dma_start(rw_bf, rwp_d).then_inc(s_rw.sem, 16)
    s_rw.bump(16)
    CST_TOTAL = s_cst.n

    @block.vector
    def _(v):
        v.memset(kv_all, 0.0)
        v.engine_nop().then_inc(s_dv.sem, 1)
    kv_zero = s_dv.bump()

    def tr_dma(s, dst, src):
        s.wait_ge(s_tr.sem, s_tr.n)
        s.dma_start(dst, src, transpose=True).then_inc(s_tr.sem, 16)
        return s_tr.bump(16)

    def chain(eng, ins):
        ins.then_inc(s_dv.sem, 1)
        eng.wait_ge(s_dv.sem, s_dv.bump())

    def ln_subtile(half, sub, dst2d, hT_guard_mm=0, spill_wait=0):
        slot = sub % 2
        xb = xbuf_p1[:, slot * 4096:(slot + 1) * 4096]

        @block.gpsimd
        def _(g):
            c, v = st["xp1_users"][slot]
            if c is not None:
                g.wait_ge(c.sem, v)
            g.dma_start(xb, x_d[half * HALF + sub * 128:
                               half * HALF + (sub + 1) * 128, :]).then_inc(
                s_x[slot].sem, 16)
        xv = s_x[slot].bump(16)

        @block.scalar
        def _(sc):
            sc.wait_ge(s_x[slot].sem, xv)
            sc.wait_ge(s_st.sem, s_st.n)
            sc.activation(dump, xb[:, 0:2048], AF.Square,
                          accum_out=stats[:, SQC:SQC + 1]).then_inc(s_ac.sem, 1)
            sc.wait_ge(s_ac.sem, s_ac.bump())
            sc.activation(dump, xb[:, 2048:4096], AF.Square,
                          accum_out=stats[:, SQ1C:SQ1C + 1]).then_inc(s_st.sem, 1)
        sq_v = s_st.bump()

        @block.vector
        def _(v):
            v.wait_ge(s_x[slot].sem, xv)
            chain(v, v.reduce_sum(stats[:, SUMC:SUMC + 1], xb, axis=AX.X))
            v.wait_ge(s_st.sem, sq_v)
            chain(v, v.tensor_tensor(stats[:, SQC:SQC + 1], stats[:, SQC:SQC + 1],
                                     stats[:, SQ1C:SQ1C + 1], ALU.add))
            chain(v, v.tensor_scalar(stats[:, NMC:NMC + 1],
                                     stats[:, SUMC:SUMC + 1],
                                     -1.0 / D, None, ALU.mult))
            chain(v, v.tensor_scalar(stats[:, VARC:VARC + 1],
                                     stats[:, SQC:SQC + 1],
                                     1.0 / D, EPS, ALU.mult, ALU.add))
            chain(v, v.tensor_tensor(stats[:, M2C:M2C + 1], stats[:, NMC:NMC + 1],
                                     stats[:, NMC:NMC + 1], ALU.mult))
            v.tensor_tensor(stats[:, VARC:VARC + 1], stats[:, VARC:VARC + 1],
                            stats[:, M2C:M2C + 1], ALU.subtract).then_inc(s_dv.sem, 1)
        var_v = s_dv.bump()

        @block.scalar
        def _(sc):
            sc.wait_ge(s_dv.sem, var_v)
            sc.activation(stats[:, STDC:STDC + 1], stats[:, VARC:VARC + 1],
                          AF.Sqrt).then_inc(s_ac.sem, 1)
        std_v = s_ac.bump()

        @block.vector
        def _(v):
            v.wait_ge(s_ac.sem, std_v)
            chain(v, v.reciprocal(stats[:, RSTDC:RSTDC + 1],
                                  stats[:, STDC:STDC + 1]))
            v.wait_ge(s_tr.sem, st["hbf_free"])
            v.tensor_scalar(h_bf, xb, stats[:, NMC:NMC + 1],
                            stats[:, RSTDC:RSTDC + 1], ALU.add, ALU.mult).then_inc(
                s_dv.sem, 1)
        norm_v = s_dv.bump()
        st["xp1_users"][slot] = (s_dv, norm_v)

        @block.sync
        def _(s):
            s.wait_ge(s_dv.sem, norm_v)
            if sub == 0:
                if hT_guard_mm > 0:
                    s.wait_ge(s_mm.sem, hT_guard_mm)
                if spill_wait > 0:
                    s.wait_ge(s_hs.sem, spill_wait)
            tr_dma(s,
                   dst2d.rearrange("p (c t) -> p c t", t=1024)[
                       :, :, sub * 128:(sub + 1) * 128],
                   h_bf.rearrange("p (c f) -> p c f", f=128))
        st["hbf_free"] = s_tr.n

    def load_wcb(src_ap, elems=4096, extra=()):
        t = load_wcb.count
        slot = t % 2
        load_wcb.count += 1

        @block.gpsimd
        def _(g):
            if st["wl_cnt"][slot] > 0:
                g.wait_ge(s_mm.sem, st["wl_freed"][slot])
            for cc, vv in extra:
                if vv > 0:
                    g.wait_ge(cc.sem, vv)
            g.dma_start(wbuf[:, slot * 4096:slot * 4096 + elems],
                        src_ap).then_inc(s_wl[slot].sem, 16)
        st["wl_cnt"][slot] += 1
        return slot, s_wl[slot].bump(16)

    load_wcb.count = 0

    def fm_cb_matmul(wslot, wlv, rhs2d, pslot, pre=None):
        @block.tensor
        def _(t):
            t.wait_ge(s_wl[wslot].sem, wlv)
            c, v = st["ps_main"][pslot]
            if c is not None:
                t.wait_ge(c.sem, v)
            if pre is not None:
                for (cc, vv) in pre:
                    if vv > 0:
                        t.wait_ge(cc.sem, vv)
            for ic in range(NIC):
                lhsT = wbuf[:, wslot * 4096 + ic * 128:wslot * 4096 + (ic + 1) * 128]
                t.matmul(ps[:, 2 * pslot, :], lhsT,
                         rhs2d[:, ic * 1024:ic * 1024 + 512],
                         start=(ic == 0), stop=(ic == NIC - 1))
                ins = t.matmul(ps[:, 2 * pslot + 1, :], lhsT,
                               rhs2d[:, ic * 1024 + 512:(ic + 1) * 1024],
                               start=(ic == 0), stop=(ic == NIC - 1))
                if ic == NIC - 1:
                    ins.then_inc(s_mm.sem, 1)
        st["wl_freed"][wslot] = s_mm.bump()
        return s_mm.n

    def p1_half(half, hT_guard_mm=0, spill_wait=0):
        for sub in range(NSUB):
            ln_subtile(half, sub, hT, hT_guard_mm, spill_wait)
        hT_ready = s_tr.n

        for hg in range(NGRP):
            heads = [hg * HG + i for i in range(HG)]
            ev_marks = {}
            prev_pe = st["prev_scan_pe"]
            prev_grp_tr = st["prev_grp_tr"]
            scan_entry = (s_msk.n, s_oe.n)

            pend_ev = []

            def qkv_mm(comp, i4):
                cb = hg * 12 + comp * 4 + i4
                wslot, wlv = load_wcb(qkvp_d[cb])
                pslot = load_wcb.count % 2

                @block.tensor
                def _(t):
                    t.wait_ge(s_wl[wslot].sem, wlv)
                    t.wait_ge(s_tr.sem, hT_ready)
                    if comp == 0 and i4 < 2:
                        if scan_entry[0] > 0:
                            t.wait_ge(s_msk.sem, scan_entry[0])
                        if scan_entry[1] > 0:
                            t.wait_ge(s_oe.sem, scan_entry[1])
                    c, v = st["ps_main"][pslot]
                    if c is not None:
                        t.wait_ge(c.sem, v)
                    for ic in range(NIC):
                        lhsT = wbuf[:, wslot * 4096 + ic * 128:
                                    wslot * 4096 + (ic + 1) * 128]
                        t.matmul(ps[:, 2 * pslot, :], lhsT,
                                 hT[:, ic * 1024:ic * 1024 + 512],
                                 start=(ic == 0), stop=(ic == NIC - 1))
                        ins = t.matmul(ps[:, 2 * pslot + 1, :], lhsT,
                                       hT[:, ic * 1024 + 512:(ic + 1) * 1024],
                                       start=(ic == 0), stop=(ic == NIC - 1))
                        if ic == NIC - 1:
                            ins.then_inc(s_mm.sem, 1)
                st["wl_freed"][wslot] = s_mm.bump()
                mmv = s_mm.n

                if comp < 2:
                    evdst = (q_raw, k_sb)[comp][:, i4 * 1024:(i4 + 1) * 1024]
                else:
                    evdst = ttmp[:, (i4 % 2) * 1024:(i4 % 2 + 1) * 1024]
                tw = st["ttmp_user"][i4 % 2] if comp == 2 else None
                pend_ev.append((pslot, mmv, evdst, tw))
                s_ev[pslot].bump()
                st["ps_main"][pslot] = (s_ev[pslot], s_ev[pslot].n)
                ev_marks[(comp, i4)] = (s_ev[pslot], s_ev[pslot].n)

            def flush_ev():
                evs = list(pend_ev)
                pend_ev.clear()

                @block.scalar
                def _(sc):
                    if prev_pe > 0:
                        sc.wait_ge(s_po.sem, prev_pe)
                    if prev_grp_tr > 0:
                        sc.wait_ge(s_tr.sem, prev_grp_tr)
                    for (pslot, mmv, evdst, tw) in evs:
                        sc.wait_ge(s_mm.sem, mmv)
                        if tw is not None and tw[0] is not None:
                            sc.wait_ge(tw[0].sem, tw[1])
                        sc.activation(evdst, ppair(pslot), AF.Silu).then_inc(
                            s_ev[pslot].sem, 1)

            def v_tr(i4):
                @block.sync
                def _(s):
                    c, v = ev_marks[(2, i4)]
                    s.wait_ge(c.sem, v)
                    if prev_pe > 0:
                        s.wait_ge(s_po.sem, prev_pe)
                    tr_dma(s,
                           v_tm[:, i4 * 1024:(i4 + 1) * 1024].rearrange(
                               "p (c t) -> p c t", t=128),
                           ttmp[:, (i4 % 2) * 1024:(i4 % 2 + 1) * 1024]
                           .rearrange("p (c f) -> p c f", f=128))
                st["ttmp_user"][i4 % 2] = (s_tr, s_tr.n)
                st["v_tr"][i4] = s_tr.n

            for comp in range(2):
                for i4 in range(HG):
                    qkv_mm(comp, i4)
                flush_ev()

            kq_ev = [(s_ev[0], s_ev[0].n), (s_ev[1], s_ev[1].n)]

            @block.sync
            def _(s):
                for cc, vv in kq_ev:
                    s.wait_ge(cc.sem, vv)
                if prev_pe > 0:
                    s.wait_ge(s_po.sem, prev_pe)
                tr_dma(s, ktld.rearrange("p (c t) -> p c t", t=128),
                       k_sb.rearrange("p (c f) -> p c f", f=128))
            ktr = s_tr.n

            @block.vector
            def _(v):
                v.wait_ge(s_tr.sem, ktr)
                v.wait_ge(s_cst.sem, CST_TOTAL)
                for i4 in range(HG):
                    h = heads[i4]
                    for par in range(2):
                        view = ktld[:, i4 * 1024:(i4 + 1) * 1024].rearrange(
                            "p (c t) -> p c t", t=128)[:, par::2, :]
                        ins = v.tensor_scalar(view, view,
                                              kdec_sb[:, 2 * h + par:2 * h + par + 1],
                                              None, ALU.mult)
                        if i4 == HG - 1 and par == 1:
                            ins.then_inc(s_dv.sem, 1)
            kscale_v = s_dv.bump()

            @block.sync
            def _(s):
                for cc, vv in kq_ev:
                    s.wait_ge(cc.sem, vv)
                tr_dma(s, qtm.rearrange("p (c t) -> p c t", t=128),
                       q_raw.rearrange("p (c f) -> p c f", f=128))
            qtr = s_tr.n

            @block.vector
            def _(v):
                v.wait_ge(s_tr.sem, qtr)
                v.wait_ge(s_cst.sem, CST_TOTAL)
                for i4 in range(HG):
                    h = heads[i4]
                    for par in range(2):
                        view = qtm[:, i4 * 1024:(i4 + 1) * 1024].rearrange(
                            "p (c t) -> p c t", t=128)[:, par::2, :]
                        ins = v.tensor_scalar(view, view,
                                              qdl_sb[:, 2 * h + par:2 * h + par + 1],
                                              None, ALU.mult)
                        if i4 == HG - 1 and par == 1:
                            ins.then_inc(s_dv.sem, 1)
            qscale_v = s_dv.bump()

            @block.sync
            def _(s):
                s.wait_ge(s_dv.sem, qscale_v)
                s.wait_ge(s_qsp.sem, st["prev_spill"])
                if prev_pe > 0:
                    s.wait_ge(s_po.sem, prev_pe)
                tr_dma(s, qchk.rearrange("p (c t) -> p c t", t=128),
                       qtm.rearrange("p (c f) -> p c f", f=128))
            qctr = s_tr.n

            @block.sync
            def _(s):
                s.wait_ge(s_tr.sem, qctr)
                dview = qspill.ap()[half, hg * HG:(hg + 1) * HG].rearrange(
                    "h p t -> p h t")
                s.dma_start(dview,
                            qchk.rearrange("p (h t) -> p h t", t=HALF)
                            ).then_inc(s_qsp.sem, 16)
            s_qsp.bump(16)
            st["prev_spill"] = s_qsp.n

            @block.sync
            def _(s):
                if st["mstr_free"] > 0:
                    s.wait_ge(s_msk.sem, st["mstr_free"])
                dview = maskp_d[hg * HG:(hg + 1) * HG].rearrange("h p n -> p h n")
                s.dma_start(mstr.rearrange("p (h n) -> p h n", n=512),
                            dview).then_inc(s_mld.sem, 16)
            mload = s_mld.bump(16)

            for i4 in range(2):
                qkv_mm(2, i4)
            flush_ev()
            v_tr(0)
            v_tr(1)
            for i4 in range(2, HG):
                qkv_mm(2, i4)
            flush_ev()
            v_tr(2)
            v_tr(3)

            grp_dv = s_dv.n
            kq_ev2 = [(s_ev[0], s_ev[0].n), (s_ev[1], s_ev[1].n)]

            def emit_scores(k):
                blk, i4 = k // HG, k % HG
                sslot = k % 2

                @block.tensor
                def _(t):
                    if k == 0:
                        t.wait_ge(s_dv.sem, max(grp_dv, kv_zero))
                        for cc, vv in kq_ev2:
                            t.wait_ge(cc.sem, vv)
                    if st["scores_free"][sslot] > 0:
                        t.wait_ge(s_msk.sem, st["scores_free"][sslot])
                    for tkc in range(2):
                        lhsT = k_sb[:, i4 * 1024 + blk * 256 + tkc * 128:
                                    i4 * 1024 + blk * 256 + (tkc + 1) * 128]
                        rhs = q_raw[:, i4 * 1024 + blk * 256:
                                    i4 * 1024 + (blk + 1) * 256]
                        ins = t.matmul(pbank(sslot, tkc * 256, (tkc + 1) * 256),
                                       lhsT, rhs, start=True, stop=True)
                        if tkc == 1:
                            ins.then_inc(s_pe.sem, 1)
                return s_pe.bump()

            sc_done = {}
            sc_done[0] = emit_scores(0)
            oe_of_head = [0, 0, 0, 0]
            for blk in range(NBLK_HALF):
                pend_oe = []
                for i4 in range(HG):
                    h = heads[i4]
                    k = blk * HG + i4
                    sslot = k % 2
                    obank = 2 + i4 % 2
                    ocolh = i4 // 2
                    kslot = k % 4
                    kbank = 6 + kslot % 2
                    kcol = (kslot // 2) * 128

                    if k + 1 < NBLK_HALF * HG:
                        sc_done[k + 1] = emit_scores(k + 1)

                    @block.vector
                    def _(v):
                        v.wait_ge(s_pe.sem, sc_done[k])
                        v.wait_ge(s_mld.sem, mload)
                        if st["smask_free"][i4] > 0:
                            v.wait_ge(s_po.sem, st["smask_free"][i4])
                        v.tensor_tensor(smask[:, i4 * 512:(i4 + 1) * 512],
                                        pbank(sslot),
                                        mstr[:, i4 * 512:(i4 + 1) * 512],
                                        ALU.mult).then_inc(s_msk.sem, 1)
                    msk_v = s_msk.bump()
                    st["scores_free"][sslot] = msk_v

                    @block.tensor
                    def _(t):
                        t.wait_ge(s_msk.sem, msk_v)
                        if blk == 0:
                            t.wait_ge(s_tr.sem, st["v_tr"][i4])
                        if st["obank_free"][i4] > 0:
                            t.wait_ge(s_oe.sem, st["obank_free"][i4])
                        for tqc in range(2):
                            dst = pbank(obank, ocolh * 256 + tqc * 128,
                                        ocolh * 256 + (tqc + 1) * 128)
                            for tkc in range(2):
                                lhsT = smask[:, i4 * 512 + tkc * 256 + tqc * 128:
                                             i4 * 512 + tkc * 256 + (tqc + 1) * 128]
                                rhs = v_tm[:, i4 * 1024 + (blk * 2 + tkc) * 128:
                                           i4 * 1024 + (blk * 2 + tkc + 1) * 128]
                                ins = t.matmul(dst, lhsT, rhs,
                                               start=(tkc == 0), stop=(tkc == 1))
                                if tqc == 1 and tkc == 1:
                                    ins.then_inc(s_po.sem, 1)
                        if st["kvps_free"][kslot] > 0:
                            t.wait_ge(s_kvu.sem, st["kvps_free"][kslot])
                        for tkc in range(2):
                            ins = t.matmul(
                                pbank(kbank, kcol, kcol + 128),
                                ktld[:, i4 * 1024 + (blk * 2 + tkc) * 128:
                                     i4 * 1024 + (blk * 2 + tkc + 1) * 128],
                                v_tm[:, i4 * 1024 + (blk * 2 + tkc) * 128:
                                     i4 * 1024 + (blk * 2 + tkc + 1) * 128],
                                start=(tkc == 0), stop=(tkc == 1))
                            if tkc == 1:
                                ins.then_inc(s_kvp.sem, 1)
                    po_v = s_po.bump()
                    kvp_v = s_kvp.bump()
                    st["smask_free"][i4] = po_v

                    osw = st["osb_used"][i4] if blk == 0 else 0
                    pend_oe.append((po_v, osw, i4, obank, ocolh))
                    st["obank_free"][i4] = s_oe.bump()
                    oe_of_head[i4] = s_oe.n

                    @block.vector
                    def _(v):
                        v.wait_ge(s_kvp.sem, kvp_v)
                        if blk == 0:
                            v.wait_ge(s_dv.sem, kv_zero)
                            if st["ph2_pe"] > 0:
                                v.wait_ge(s_po.sem, st["ph2_pe"])
                        v.wait_ge(s_kvu.sem, s_kvu.n)
                        kvc = kv_all[:, h * 128:(h + 1) * 128]
                        v.tensor_copy(
                            kv_hist[:, i4 * 512 + blk * 128:
                                    i4 * 512 + (blk + 1) * 128],
                            kvc).then_inc(s_kvu.sem, 1)
                        v.scalar_tensor_tensor(
                            kvc, kvc, Dmf[h], pbank(kbank, kcol, kcol + 128),
                            ALU.mult, ALU.add).then_inc(s_kvu.sem, 1)
                    s_kvu.bump(2)
                    st["head_kvu"][h] = s_kvu.n - 1
                    st["kvps_free"][kslot] = s_kvu.n

                @block.scalar
                def _(sc):
                    for (po_vv, osw, i4b, obankb, och) in pend_oe:
                        sc.wait_ge(s_po.sem, po_vv)
                        if osw > 0:
                            sc.wait_ge(s_osp[i4b].sem, osw)
                        sc.activation(
                            o_sb[:, i4b * 1024 + blk * 256:
                                 i4b * 1024 + (blk + 1) * 256],
                            pbank(obankb, och * 256, (och + 1) * 256),
                            AF.Copy).then_inc(s_oe.sem, 1)

            st["mstr_free"] = s_msk.n

            for i4 in range(HG):
                h = heads[i4]

                @block.tensor
                def _(t):
                    t.wait_ge(s_kvu.sem, st["head_kvu"][h])
                    if st["ps45_free"] > 0:
                        t.wait_ge(s_dv.sem, st["ps45_free"])
                    for m in range(NBLK_HALF):
                        rhs = kv_hist[:, i4 * 512 + m * 128:
                                      i4 * 512 + (m + 1) * 128]
                        for tqc in range(2):
                            ins = t.matmul(
                                pbank(4 + m // 2, (m % 2) * 256 + tqc * 128,
                                      (m % 2) * 256 + (tqc + 1) * 128),
                                qchk[:, i4 * 1024 + m * 256 + tqc * 128:
                                     i4 * 1024 + m * 256 + (tqc + 1) * 128],
                                rhs, start=True, stop=True)
                            if m == NBLK_HALF - 1 and tqc == 1:
                                ins.then_inc(s_po.sem, 1)
                ph2_v = s_po.bump()

                @block.vector
                def _(v):
                    v.wait_ge(s_po.sem, ph2_v)
                    v.wait_ge(s_oe.sem, oe_of_head[i4])
                    ov = o_sb[:, i4 * 1024:(i4 + 1) * 1024].rearrange(
                        "p (a b) -> p a b", b=512)
                    v.tensor_tensor(ov, ps[:, 4:6, :], ov, ALU.add).then_inc(
                        s_dv.sem, 1)
                add_v = s_dv.bump()
                st["ps45_free"] = add_v

                @block.sync
                def _(s):
                    s.wait_ge(s_dv.sem, add_v)
                    view = aspill.ap()[half * HALF:(half + 1) * HALF,
                                       h * 128:(h + 1) * 128].rearrange(
                        "(q p) e -> p q e", p=128)
                    s.dma_start(view,
                                o_sb[:, i4 * 1024:(i4 + 1) * 1024]
                                .rearrange("p (q e) -> p q e", e=128)
                                ).then_inc(s_osp[i4].sem, 16)
                st["osb_used"][i4] = s_osp[i4].bump(16)
            st["ph2_pe"] = s_po.n

            st["prev_scan_pe"] = s_po.n
            st["prev_grp_tr"] = qtr

    def p2_half(half, skip_A, do_combine=False):
        p2_tr_entry = s_tr.n
        dv_entry = s_dv.n
        ac_entry = s_ac.n
        mm_entry = s_mm.n

        if not skip_A:
            @block.sync
            def _(s):
                s.wait_ge(s_dv.sem, dv_entry)
                s.wait_ge(s_ac.sem, ac_entry)
                s.wait_ge(s_mm.sem, mm_entry)
                s.dma_start(hT[:, 0:16384],
                            hspill.ap()[:, 0:16384]).then_inc(s_hr.sem, 16)

            @block.scalar
            def _(sc):
                sc.wait_ge(s_dv.sem, dv_entry)
                sc.wait_ge(s_mm.sem, mm_entry)
                sc.dma_start(hT[:, 16384:32768],
                             hspill.ap()[:, 16384:32768]).then_inc(s_hr.sem, 16)
            hT_wait = (s_hr, s_hr.bump(32))
        else:
            hT_wait = (s_tr, s_tr.n)
        hT_ready = s_tr.n

        b_scan_entry = (s_msk.n, s_oe.n)
        b_guard = dict(kvp=s_kvp.n, kvu=s_kvu.n, po=s_po.n,
                       osp=[s_osp[ii].n for ii in range(4)],
                       qsp=s_qsp.n, msk=s_msk.n, oe=s_oe.n)
        pend_gev = []

        def flush_gev():
            evs = list(pend_gev)
            pend_gev.clear()

            @block.scalar
            def _(sc):
                for (fcb, pslot, mmv) in evs:
                    sc.wait_ge(s_mm.sem, mmv)
                    if fcb == 0:
                        sc.wait_ge(s_kvp.sem, b_guard["kvp"])
                        sc.wait_ge(s_kvu.sem, b_guard["kvu"])
                        sc.wait_ge(s_po.sem, b_guard["po"])
                        for ii in range(4):
                            sc.wait_ge(s_osp[ii].sem, b_guard["osp"][ii])
                        sc.wait_ge(s_qsp.sem, b_guard["qsp"])
                        sc.wait_ge(s_msk.sem, b_guard["msk"])
                        sc.wait_ge(s_oe.sem, b_guard["oe"])
                        sc.wait_ge(s_dv.sem, dv_entry)
                        sc.wait_ge(s_tr.sem, hT_ready)
                    sc.activation(gnT[:, fcb * 1024:(fcb + 1) * 1024].rearrange(
                        "p (a b) -> p a b", b=512),
                        ppair(pslot), AF.Sigmoid).then_inc(s_ev[pslot].sem, 1)

        for fc in range(NIC):
            extra = [(s_mm, mm_entry), (s_ac, ac_entry)] if fc < 2 else ()
            wslot, wlv = load_wcb(gatep_d[fc], extra=extra)
            pslot = fc % 2

            @block.tensor
            def _(t):
                t.wait_ge(s_wl[wslot].sem, wlv)
                if fc < 2:
                    t.wait_ge(hT_wait[0].sem, hT_wait[1])
                    t.wait_ge(s_dv.sem, dv_entry)
                    if b_scan_entry[0] > 0:
                        t.wait_ge(s_msk.sem, b_scan_entry[0])
                    if b_scan_entry[1] > 0:
                        t.wait_ge(s_oe.sem, b_scan_entry[1])
                c, v = st["ps_main"][pslot]
                if c is not None:
                    t.wait_ge(c.sem, v)
                for ic in range(NIC):
                    lhsT = wbuf[:, wslot * 4096 + ic * 128:wslot * 4096 + (ic + 1) * 128]
                    t.matmul(ps[:, 2 * pslot, :], lhsT,
                             hT[:, ic * 1024:ic * 1024 + 512],
                             start=(ic == 0), stop=(ic == NIC - 1))
                    ins = t.matmul(ps[:, 2 * pslot + 1, :], lhsT,
                                   hT[:, ic * 1024 + 512:(ic + 1) * 1024],
                                   start=(ic == 0), stop=(ic == NIC - 1))
                    if ic == NIC - 1:
                        ins.then_inc(s_mm.sem, 1)
            st["wl_freed"][wslot] = s_mm.bump()
            mmv = s_mm.n

            pend_gev.append((fc, pslot, mmv))
            s_ev[pslot].bump()
            st["ps_main"][pslot] = (s_ev[pslot], s_ev[pslot].n)
            if fc % 4 == 3:
                flush_gev()
        b_mm_done = s_mm.n

        if do_combine:
            emit_combine()

        @block.gpsimd
        def _(g):
            g.wait_ge(s_mm.sem, b_mm_done)
            g.wait_ge(s_tr.sem, p2_tr_entry)
            g.wait_ge(s_dv.sem, st["comb_done"])
            g.dma_start(nw_bc, nwp_d).then_inc(s_nw.sem, 16)
        nw_v = s_nw.bump(16)

        osp_marks = [s_osp[ii].n for ii in range(4)]

        @block.sync
        def _(s):
            s.wait_ge(s_mm.sem, b_mm_done)
            for ii in range(4):
                s.wait_ge(s_osp[ii].sem, osp_marks[ii])
            view = hT.rearrange("p (sub f) -> p sub f", f=4096)[:, 0:4, :]
            src = aspill.ap()[half * HALF:half * HALF + 512, :].rearrange(
                "(sub p) f -> p sub f", p=128)
            s.dma_start(view, src).then_inc(s_att.sem, 16)

        @block.scalar
        def _(sc):
            sc.wait_ge(s_mm.sem, b_mm_done)
            for ii in range(4):
                sc.wait_ge(s_osp[ii].sem, osp_marks[ii])
            view = hT.rearrange("p (sub f) -> p sub f", f=4096)[:, 4:8, :]
            src = aspill.ap()[half * HALF + 512:(half + 1) * HALF, :].rearrange(
                "(sub p) f -> p sub f", p=128)
            sc.dma_start(view, src).then_inc(s_att.sem, 16)
        attn_v = s_att.bump(32)

        qsp_total = s_qsp.n
        for h in range(H):
            qslot = h % 2

            def qrd_load(q):
                if st["qrd_used"][qslot] > 0:
                    q.wait_ge(s_pe.sem, st["qrd_used"][qslot])
                if h < 2:
                    q.wait_ge(s_dv.sem, st["comb_done"])
                    ov = max(st["outsb_used"])
                    if ov > 0:
                        q.wait_ge(s_ou.sem, ov)
                q.wait_ge(s_qsp.sem, qsp_total)
                q.dma_start(qrd[:, qslot * 1024:(qslot + 1) * 1024],
                            qspill.ap()[half, h]).then_inc(s_rd[qslot].sem, 16)

            if qslot == 0:
                @block.sync
                def _(s):
                    qrd_load(s)
            else:
                @block.scalar
                def _(sc):
                    qrd_load(sc)
            qv = s_rd[qslot].bump(16)

            for m in range(NBLK_HALF):
                gm = half * NBLK_HALF + m
                cslot = (h * NBLK_HALF + m) % 2

                @block.tensor
                def _(t):
                    t.wait_ge(s_rd[qslot].sem, qv)
                    if h == 0 and m == 0:
                        t.wait_ge(s_dv.sem, max(st["comb_done"], dv_entry))
                    if st["ps_c"][cslot] > 0:
                        t.wait_ge(s_dv.sem, st["ps_c"][cslot])
                    for tqc in range(2):
                        ins = t.matmul(
                            pbank(4 + cslot, tqc * 128, (tqc + 1) * 128),
                            qrd[:, qslot * 1024 + m * 256 + tqc * 128:
                                qslot * 1024 + m * 256 + (tqc + 1) * 128],
                            kvst_bf[:, h * 128:(h + 1) * 128],
                            start=True, stop=True)
                        if tqc == 1:
                            ins.then_inc(s_pe.sem, 1)
                pe_v = s_pe.bump()

                @block.vector
                def _(v):
                    v.wait_ge(s_pe.sem, pe_v)
                    v.wait_ge(s_att.sem, attn_v)
                    attn_vw = hT.rearrange("p (sub f) -> p sub f", f=4096)[
                        :, m * 2:m * 2 + 2, h * 128:(h + 1) * 128]
                    psv = pbank(4 + cslot, 0, 256).rearrange(
                        "p (q e) -> p q e", e=128)
                    v.scalar_tensor_tensor(attn_vw, psv, Dm_pow[h][gm], attn_vw,
                                           ALU.mult, ALU.add).then_inc(s_dv.sem, 1)
                st["ps_c"][cslot] = s_dv.bump()
            st["qrd_used"][qslot] = s_pe.n
        corr_done = s_dv.n

        for sub in range(NSUB):
            attn_t = hT[:, sub * 4096:(sub + 1) * 4096]

            @block.scalar
            def _(sc):
                if sub == 0:
                    sc.wait_ge(s_dv.sem, corr_done)
                else:
                    sc.wait_ge(s_ac.sem, s_ac.n)
                sc.activation(dump, attn_t[:, 0:2048], AF.Square,
                              accum_out=stats[:, sub:sub + 1]).then_inc(s_ac.sem, 1)
                sc.wait_ge(s_ac.sem, s_ac.bump())
                ins = sc.activation(dump, attn_t[:, 2048:4096], AF.Square,
                                    accum_out=stats[:, 8 + sub:9 + sub])
                if sub == NSUB - 1:
                    ins.then_inc(s_st.sem, 1)
                else:
                    ins.then_inc(s_ac.sem, 1)
                    s_ac.bump()
        sq_all = s_st.bump()

        @block.vector
        def _(v):
            v.wait_ge(s_st.sem, sq_all)
            chain(v, v.tensor_tensor(stats[:, 0:8], stats[:, 0:8],
                                     stats[:, 8:16], ALU.add))
            v.tensor_scalar(stats[:, 16:24], stats[:, 0:8],
                            1.0 / D, EPS, ALU.mult, ALU.add).then_inc(s_dv.sem, 1)
        var_v = s_dv.bump()

        @block.scalar
        def _(sc):
            sc.wait_ge(s_dv.sem, var_v)
            sc.activation(stats[:, 16:24], stats[:, 16:24],
                          AF.Sqrt).then_inc(s_ac.sem, 1)
        std_v = s_ac.bump()

        @block.vector
        def _(v):
            v.wait_ge(s_ac.sem, std_v)
            v.reciprocal(stats[:, 24:32], stats[:, 16:24]).then_inc(s_dv.sem, 1)
        rstd_v = s_dv.bump()

        gn_marks = []
        for sub in range(NSUB):
            attn_t = hT[:, sub * 4096:(sub + 1) * 4096]
            hb = h_bf if sub % 2 == 0 else h_bf2
            hb_key = "hbf2_free" if sub % 2 else "hbf_free"

            @block.vector
            def _(v):
                if sub == 0:
                    v.wait_ge(s_dv.sem, rstd_v)
                    v.wait_ge(s_nw.sem, nw_v)
                v.wait_ge(s_tr.sem, st[hb_key])
                v.scalar_tensor_tensor(hb, attn_t,
                                       stats[:, 24 + sub:25 + sub],
                                       nw_bc, ALU.mult, ALU.mult).then_inc(
                    s_dv.sem, 1)
            norm_v = s_dv.bump()

            @block.sync
            def _(s):
                s.wait_ge(s_dv.sem, norm_v)
                if st["ntmp_used"] > 0:
                    s.wait_ge(s_dv.sem, st["ntmp_used"])
                if sub == 0:
                    ov = max(st["outsb_used"])
                    if ov > 0:
                        s.wait_ge(s_ou.sem, ov)
                tr_dma(s, ntmp.rearrange("p (c t) -> p c t", t=128),
                       hb.rearrange("p (c f) -> p c f", f=128))
            ntr = s_tr.n
            st[hb_key] = ntr

            @block.vector
            def _(v):
                v.wait_ge(s_tr.sem, ntr)
                gv = gnT.rearrange("p (c t) -> p c t", t=1024)[
                    :, :, sub * 128:(sub + 1) * 128]
                v.tensor_tensor(gv, gv,
                                ntmp.rearrange("p (c t) -> p c t", t=128),
                                ALU.mult).then_inc(s_dv.sem, 1)
            st["ntmp_used"] = s_dv.bump()
            gn_marks.append(s_dv.n)
        gn_done = s_dv.n

        NOCH = 32
        evac_marks = {}
        sq_marks = {}

        def emit_stats(oc):
            @block.tensor
            def _(t):
                t.wait_ge(s_ac.sem, sq_marks[oc])
                if oc == 0:
                    t.wait_ge(s_dv.sem, max(st["ps_c"][0], st["ps_c"][1], 1))
                else:
                    t.wait_ge(s_dv.sem, st["srow_add"])
                for th in range(2):
                    t.matmul(ps[0:1, 4 + th, :], ones_sb[:, 0:1],
                             hT[:, oc * 1024 + th * 512:
                                oc * 1024 + (th + 1) * 512],
                             start=True, stop=True)
                    ins = t.matmul(ps[0:1, 6 + th, :], ones_sb[:, 0:1],
                                   sq_sb[:, (oc % 2) * 1024 + th * 512:
                                         (oc % 2) * 1024 + (th + 1) * 512],
                                   start=True, stop=True)
                    if th == 1:
                        ins.then_inc(s_pe.sem, 1)
            st["sq_used"][oc % 2] = s_pe.bump()
            mmv2 = s_pe.n

            @block.vector
            def _(v):
                v.wait_ge(s_pe.sem, mmv2)
                if oc == 0:
                    v.tensor_copy(srow, ps[0:1, 4:8, :].rearrange(
                        "p a b -> p (a b)")).then_inc(s_dv.sem, 1)
                else:
                    v.wait_ge(s_dv.sem, st["srow_add"])
                    v.tensor_tensor(srow, srow, ps[0:1, 4:8, :].rearrange(
                        "p a b -> p (a b)"), ALU.add).then_inc(s_dv.sem, 1)
            st["srow_add"] = s_dv.bump()

        for och in range(NOCH):
            wslot = och % 2
            pb = (och % 2) * 2

            @block.gpsimd
            def _(g):
                if st["cw_used"][wslot] > 0:
                    g.wait_ge(s_mm.sem, st["cw_used"][wslot])
                if och < 2:
                    g.wait_ge(s_mm.sem, max(st["wl_freed"][0],
                                            st["wl_freed"][1]))
                    g.wait_ge(s_dv.sem, dv_entry)
                g.dma_start(woc[:, wslot * 4096:(wslot + 1) * 4096],
                            outpT_d[och]).then_inc(s_wl[wslot].sem, 16)
            wlv = s_wl[wslot].bump(16)

            xslot = och % 2

            @block.gpsimd
            def _(g):
                c, v = st["xt_users"][xslot]
                if c is not None:
                    g.wait_ge(c.sem, v)
                if och < 2:
                    g.wait_ge(s_pe.sem, max(st["qrd_used"][0],
                                            st["qrd_used"][1]))
                g.dma_start(
                    xt_sb[:, xslot * 1024:(xslot + 1) * 1024],
                    xT_d[och * 128:(och + 1) * 128,
                         half * HALF:(half + 1) * HALF]).then_inc(
                    s_x[xslot].sem, 16)
            xv = s_x[xslot].bump(16)

            @block.tensor
            def _(t):
                t.wait_ge(s_wl[wslot].sem, wlv)
                if och == 0:
                    c0, v0 = st["ps_main"][0]
                    c1, v1 = st["ps_main"][1]
                    if c0 is not None:
                        t.wait_ge(c0.sem, v0)
                    if c1 is not None:
                        t.wait_ge(c1.sem, v1)
                if st["ctm_ps"][och % 2] > 0:
                    t.wait_ge(s_dv.sem, st["ctm_ps"][och % 2])
                for th in range(2):
                    if och == 0:
                        t.wait_ge(s_dv.sem, gn_marks[3 if th == 0 else 7])
                    for fch in range(NIC):
                        ins = t.matmul(ps[:, pb + th, :],
                                       woc[:, wslot * 4096 + fch * 128:
                                           wslot * 4096 + (fch + 1) * 128],
                                       gnT[:, fch * 1024 + th * 512:
                                           fch * 1024 + (th + 1) * 512],
                                       start=(fch == 0), stop=(fch == NIC - 1))
                        if fch == NIC - 1 and th == 1:
                            ins.then_inc(s_mm.sem, 1)
            st["cw_used"][wslot] = s_mm.bump()
            cmm_v = s_mm.n

            @block.vector
            def _(v):
                v.wait_ge(s_mm.sem, cmm_v)
                v.wait_ge(s_x[xslot].sem, xv)
                v.tensor_tensor(
                    hT[:, och * 1024:(och + 1) * 1024].rearrange(
                        "p (a b) -> p a b", b=512),
                    ps[:, pb:pb + 2, :],
                    xt_sb[:, xslot * 1024:(xslot + 1) * 1024].rearrange(
                        "p (a b) -> p a b", b=512),
                    ALU.add).then_inc(s_dv.sem, 1)
            evac_marks[och] = s_dv.bump()
            st["ctm_ps"][och % 2] = evac_marks[och]
            st["xt_users"][xslot] = (s_dv, evac_marks[och])

            @block.scalar
            def _(sc):
                sc.wait_ge(s_dv.sem, evac_marks[och])
                if och == 0:
                    sc.wait_ge(s_dv.sem, gn_done)
                if st["sq_used"][och % 2] > 0:
                    sc.wait_ge(s_pe.sem, st["sq_used"][och % 2])
                sc.activation(sq_sb[:, (och % 2) * 1024:(och % 2 + 1) * 1024],
                              hT[:, och * 1024:(och + 1) * 1024],
                              AF.Square).then_inc(s_ac.sem, 1)
            sq_marks[och] = s_ac.bump()

            if och >= 2:
                emit_stats(och - 2)
        emit_stats(NOCH - 2)
        emit_stats(NOCH - 1)
        stats_done_pe = s_pe.n
        c_evac_done = s_dv.n
        st["ps_main"][0] = (s_dv, c_evac_done)
        st["ps_main"][1] = (s_dv, c_evac_done)

        var_row = srow[:, 1024:2048]
        rstd_bf = expsb[0:1, 0:1024]

        @block.vector
        def _(v):
            v.wait_ge(s_dv.sem, st["srow_add"])
            chain(v, v.tensor_scalar(rrow, srow[:, 0:1024],
                                     -1.0 / D, None, ALU.mult))
            chain(v, v.tensor_scalar(var_row, var_row,
                                     1.0 / D, EPS, ALU.mult, ALU.add))
            chain(v, v.tensor_tensor(vtmp_row, rrow, rrow, ALU.mult))
            v.tensor_tensor(var_row, var_row, vtmp_row,
                            ALU.subtract).then_inc(s_dv.sem, 1)
        varrow_v = s_dv.bump()

        @block.scalar
        def _(sc):
            sc.wait_ge(s_dv.sem, varrow_v)
            sc.activation(var_row, var_row, AF.Sqrt).then_inc(s_ac.sem, 1)
        stdrow_v = s_ac.bump()

        @block.vector
        def _(v):
            v.wait_ge(s_ac.sem, stdrow_v)
            chain(v, v.reciprocal(var_row, var_row))
            chain(v, v.tensor_copy(rrow_bf, rrow))
            v.tensor_copy(rstd_bf, var_row).then_inc(s_dv.sem, 1)
        rows_v = s_dv.bump()

        @block.tensor
        def _(t):
            t.wait_ge(s_dv.sem, rows_v)
            t.wait_ge(s_cst.sem, CST_TOTAL)
            for th in range(2):
                t.matmul(ps[:, 4 + th, :], ones_sb[0:1, 8:136],
                         rrow_bf[:, th * 512:(th + 1) * 512],
                         start=True, stop=True)
                ins = t.matmul(ps[:, 0 + th, :], ones_sb[0:1, 8:136],
                               rstd_bf[:, th * 512:(th + 1) * 512],
                               start=True, stop=True)
                if th == 1:
                    ins.then_inc(s_pe.sem, 1)
        bcast_v = s_pe.bump()

        h2_marks = {}
        for oc in range(NOCH):
            @block.vector
            def _(v):
                if oc == 0:
                    v.wait_ge(s_pe.sem, bcast_v)
                gv = gnT[:, oc * 1024:(oc + 1) * 1024].rearrange(
                    "p (a b) -> p a b", b=512)
                xv2 = hT[:, oc * 1024:(oc + 1) * 1024].rearrange(
                    "p (a b) -> p a b", b=512)
                chain(v, v.tensor_tensor(gv, xv2, ps[:, 4:6, :], ALU.add))
                v.tensor_tensor(gv, gv, ps[:, 0:2, :],
                                ALU.mult).then_inc(s_dv.sem, 1)
            h2_marks[oc] = s_dv.bump()
        h2_done = s_dv.n

        @block.gpsimd
        def _(g):
            if st["eo_user"] > 0:
                g.wait_ge(s_pe.sem, st["eo_user"])
            g.dma_start(eo_bf, icsp_d).then_inc(s_eo.sem, 16)
        ics_v = s_eo.bump(16)

        @block.tensor
        def _(t):
            t.wait_ge(s_rw.sem, 16)
            for ic in range(NIC):
                t.wait_ge(s_dv.sem, h2_marks[ic])
                for nh in range(2):
                    ins = t.matmul(ps[0:8, 6 + nh, :],
                                   rw_bf[:, ic * 8:(ic + 1) * 8],
                                   gnT[:, ic * 1024 + nh * 512:
                                       ic * 1024 + (nh + 1) * 512],
                                   start=(ic == 0), stop=(ic == NIC - 1))
                    if ic == NIC - 1 and nh == 1:
                        ins.then_inc(s_pe.sem, 1)
        rtr_mm = s_pe.bump()

        @block.scalar
        def _(sc):
            sc.wait_ge(s_pe.sem, rtr_mm)
            sc.activation(expsb, ps[0:8, 6:8, :], AF.Exp).then_inc(s_ac.sem, 1)
        exp_v = s_ac.bump()

        @block.tensor
        def _(t):
            t.wait_ge(s_ac.sem, exp_v)
            for nh in range(2):
                ins = t.matmul(ps[0:1, 6 + nh, :], ones_sb[0:8, 0:1],
                               expsb[:, nh * 512:(nh + 1) * 512],
                               start=True, stop=True)
                if nh == 1:
                    ins.then_inc(s_pe.sem, 1)
        sum_v = s_pe.bump()

        @block.vector
        def _(v):
            v.wait_ge(s_pe.sem, sum_v)
            chain(v, v.reciprocal(rrow,
                                  ps[0:1, 6:8, :].rearrange("p a b -> p (a b)")))
            v.tensor_copy(rrow_bf, rrow).then_inc(s_dv.sem, 1)
        rcp_v = s_dv.bump()

        @block.tensor
        def _(t):
            t.wait_ge(s_dv.sem, rcp_v)
            for nh in range(2):
                ins = t.matmul(ps[0:8, 6 + nh, :], ones_sb[0:1, 0:8],
                               rrow_bf[:, nh * 512:(nh + 1) * 512],
                               start=True, stop=True)
                if nh == 1:
                    ins.then_inc(s_pe.sem, 1)
        bc_v = s_pe.bump()

        @block.vector
        def _(v):
            v.wait_ge(s_pe.sem, bc_v)
            v.tensor_tensor(rT.rearrange("p (a b) -> p a b", b=512),
                            expsb.rearrange("p (a b) -> p a b", b=512),
                            ps[0:8, 6:8, :], ALU.mult).then_inc(s_dv.sem, 1)
        rT_v = s_dv.bump()

        for ic in range(NIC):
            pslot = ic % 2

            @block.tensor
            def _(t):
                if ic == 0:
                    t.wait_ge(s_dv.sem, rT_v)
                    t.wait_ge(s_eo.sem, ics_v)
                if ic < 2:
                    t.wait_ge(s_dv.sem, h2_done)
                else:
                    t.wait_ge(s_dv.sem, is_dv_guard[pslot])
                for nh in range(2):
                    ins = t.matmul(ps[:, 2 * pslot + nh, :],
                                   eo_bf[:, ic * 128:(ic + 1) * 128],
                                   rT[:, nh * 512:(nh + 1) * 512],
                                   start=True, stop=True)
                    if nh == 1:
                        ins.then_inc(s_pe.sem, 1)
            is_mm = s_pe.bump()

            @block.vector
            def _(v):
                v.wait_ge(s_pe.sem, is_mm)
                hc = gnT[:, ic * 1024:(ic + 1) * 1024].rearrange(
                    "p (a b) -> p a b", b=512)
                v.tensor_tensor(hc, hc, ppair(pslot), ALU.mult).then_inc(
                    s_dv.sem, 1)
            is_dv_guard[pslot] = s_dv.bump()
        st["eo_user"] = s_pe.n
        xsT_done = s_dv.n

        @block.gpsimd
        def _(g):
            g.wait_ge(s_pe.sem, st["eo_user"])
            g.dma_start(eo_bf, ocsp_d).then_inc(s_eo.sem, 16)
        ocs_v = s_eo.bump(16)

        for och in range(NOCH):
            wslot = och % 2
            pb = (och % 2) * 2
            osb = 4 + (och % 2) * 2

            @block.gpsimd
            def _(g):
                if st["woc_used"][wslot] > 0:
                    g.wait_ge(s_mm.sem, st["woc_used"][wslot])
                if och < 2:
                    g.wait_ge(s_mm.sem, max(st["cw_used"][0],
                                            st["cw_used"][1]))
                    g.wait_ge(s_dv.sem, rT_v)
                    g.wait_ge(s_pe.sem, sum_v)
                g.dma_start(woc[:, wslot * 4096:(wslot + 1) * 4096],
                            ffnpT_d[och]).then_inc(s_wl[wslot].sem, 16)
            wlv = s_wl[wslot].bump(16)

            @block.scalar
            def _(sc):
                sc.wait_ge(s_wl[wslot].sem, wlv)
                wv = woc[:, wslot * 4096:(wslot + 1) * 4096]
                sc.activation(wv, wv, AF.Sign).then_inc(s_ac.sem, 1)
            sign_v = s_ac.bump()

            @block.tensor
            def _(t):
                if och == 0:
                    t.wait_ge(s_eo.sem, ocs_v)
                    t.wait_ge(s_dv.sem, h2_done)
                if och == 1:
                    t.wait_ge(s_dv.sem, rT_v)
                if st["os_used"][och % 2] > 0:
                    t.wait_ge(s_dv.sem, st["os_used"][och % 2])
                for th in range(2):
                    ins = t.matmul(ps[:, osb + th, :],
                                   eo_bf[0:8, och * 128:(och + 1) * 128],
                                   rT[0:8, th * 512:(th + 1) * 512],
                                   start=True, stop=True)
                    if th == 1:
                        ins.then_inc(s_pe.sem, 1)
            os_mm = s_pe.bump()

            @block.vector
            def _(v):
                v.wait_ge(s_pe.sem, os_mm)
                v.tensor_copy(
                    os_sb[:, (och % 2) * 1024:(och % 2 + 1) * 1024].rearrange(
                        "p (a b) -> p a b", b=512),
                    ps[:, osb:osb + 2, :]).then_inc(s_dv.sem, 1)
            os_ev = s_dv.bump()
            st["os_used"][och % 2] = os_ev

            @block.tensor
            def _(t):
                if och == 0:
                    t.wait_ge(s_dv.sem, xsT_done)
                if st["ffn_ps_used"][och % 2] > 0:
                    t.wait_ge(s_dv.sem, st["ffn_ps_used"][och % 2])
                for th in range(2):
                    for fch in range(NIC):
                        if th == 0 and fch == 0:
                            t.wait_ge(s_ac.sem, sign_v)
                        ins = t.matmul(ps[:, pb + th, :],
                                       woc[:, wslot * 4096 + fch * 128:
                                           wslot * 4096 + (fch + 1) * 128],
                                       gnT[:, fch * 1024 + th * 512:
                                           fch * 1024 + (th + 1) * 512],
                                       start=(fch == 0), stop=(fch == NIC - 1))
                        if fch == NIC - 1 and th == 1:
                            ins.then_inc(s_mm.sem, 1)
            st["woc_used"][wslot] = s_mm.bump()
            ffn_mm = s_mm.n

            @block.vector
            def _(v):
                v.wait_ge(s_mm.sem, ffn_mm)
                v.wait_ge(s_dv.sem, os_ev)
                if st["outsb_used"][och % 2] > 0:
                    v.wait_ge(s_ou.sem, st["outsb_used"][och % 2])
                if och < 2:
                    v.wait_ge(s_dv.sem, st["comb_done"])
                    v.wait_ge(s_cst.sem, CST_TOTAL)
                ot3 = outsb[:, (och % 2) * 1024:(och % 2 + 1) * 1024].rearrange(
                    "p (a b) -> p a b", b=512)
                chain(v, v.tensor_tensor(
                    ot3, ps[:, pb:pb + 2, :],
                    os_sb[:, (och % 2) * 1024:(och % 2 + 1) * 1024].rearrange(
                        "p (a b) -> p a b", b=512),
                    ALU.mult))
                ins = v.scalar_tensor_tensor(
                    ot3, ot3, bias_col[:, och:och + 1],
                    hT[:, och * 1024:(och + 1) * 1024].rearrange(
                        "p (a b) -> p a b", b=512),
                    ALU.add, ALU.add)
                ins.then_inc(s_dv.sem, 1)
            ep_v = s_dv.bump()
            st["ffn_ps_used"][och % 2] = ep_v

            @block.sync
            def _(s):
                s.wait_ge(s_dv.sem, ep_v)
                s.wait_ge(s_ou.sem, s_ou.n)
                view = outT_d[och * 128:(och + 1) * 128,
                              half * HALF:(half + 1) * HALF]
                s.dma_start(view,
                            outsb[:, (och % 2) * 1024:(och % 2 + 1) * 1024]
                            ).then_inc(s_ou.sem, 16)
            st["outsb_used"][och % 2] = s_ou.bump(16)

    is_dv_guard = [0, 0]

    def emit_combine():
        tr_entry = s_tr.n
        mk0 = s_msk.n

        @block.vector
        def _(v):
            v.memset(kvst_bf, 0.0)
            v.engine_nop().then_inc(s_dv.sem, 1)
        kvz = s_dv.bump()

        for rk in range(NCORE):
            gs = rk % 2

            @block.gpsimd
            def _(g):
                if rk < 2:
                    g.wait_ge(s_cc.sem, 1)
                    g.wait_ge(s_tr.sem, tr_entry)
                    g.wait_ge(s_msk.sem, mk0)
                else:
                    g.wait_ge(s_dv.sem, st["gb_free"][gs])
                g.dma_start(gbuf[:, gs * 4096:(gs + 1) * 4096],
                            gth_out.ap()[rk * 128:(rk + 1) * 128, :]).then_inc(
                    s_gb[gs].sem, 16)
            gv = s_gb[gs].bump(16)

            @block.vector
            def _(v):
                v.wait_ge(s_gb[gs].sem, gv)
                v.wait_ge(s_dv.sem, kvz if rk == 0 else st["comb_prev"])
                for h in range(H):
                    kc = kvst_bf[:, h * 128:(h + 1) * 128]
                    ins = v.scalar_tensor_tensor(
                        kc, gbuf[:, gs * 4096 + h * 128:gs * 4096 + (h + 1) * 128],
                        wcomb_sb[:, rk * H + h:rk * H + h + 1], kc,
                        ALU.mult, ALU.add)
                    if h == H - 1:
                        ins.then_inc(s_dv.sem, 1)
            st["gb_free"][gs] = s_dv.bump()
            st["comb_prev"] = s_dv.n
        st["comb_done"] = s_dv.n

    p1_half(0)
    mm_after_h0 = s_mm.n

    @block.sync
    def _(s):
        s.wait_ge(s_mm.sem, mm_after_h0)
        s.dma_start(hspill.ap(), hT).then_inc(s_hs.sem, 16)
    s_hs.bump(16)

    p1_half(1, hT_guard_mm=mm_after_h0, spill_wait=16)

    kvu_total = s_kvu.n

    @block.sync
    def _(s):
        s.wait_ge(s_kvu.sem, kvu_total)
        s.dma_start(gth_in.ap(), kv_all).then_inc(s_sum.sem, 16)
    sum_out = s_sum.bump(16)

    @block.gpsimd
    def _(g):
        g.wait_ge(s_sum.sem, sum_out)
        g.collective_compute(
            "AllGather", ALU.bypass,
            replica_groups=[list(range(NCORE))],
            ins=[gth_in.ap()],
            outs=[gth_out.ap()],
        ).then_inc(s_cc.sem, 1)

    p2_half(1, skip_A=True, do_combine=True)
    p2_half(0, skip_A=False)

    ou_total = s_ou.n

    @block.sync
    def _(s):
        s.wait_ge(s_ou.sem, ou_total)

    block.__exit__(None, None, None)
    return nc


def _prep_inputs(inputs):
    x = np.ascontiguousarray(np.asarray(inputs["L0_attn_hidden_states"], np.float32))
    qkv_w = np.asarray(inputs["L0_attn_qkv_w"], np.float32)
    out_w = np.asarray(inputs["L0_attn_out_w"], np.float32)
    gate_w = np.asarray(inputs["L0_attn_gate_w"], np.float32)
    norm_w = np.asarray(inputs["L0_attn_norm_w"], np.float32)
    ffn_w = np.asarray(inputs["L0_ffn_weight"], np.float32)
    ffn_b = np.asarray(inputs["L0_ffn_bias"], np.float32)
    rw = np.asarray(inputs["L0_ffn_gate_w"], np.float32)
    ics = np.asarray(inputs["L0_ffn_in_channel_scale"], np.float32)
    ocs = np.asarray(inputs["L0_ffn_out_channel_scale"], np.float32)

    t = qkv_w.reshape(8, HG, 3, DH, NIC, 128)
    qkvp = np.ascontiguousarray(
        t.transpose(0, 2, 1, 5, 4, 3).reshape(96, 128, D))
    gatep = np.ascontiguousarray(
        gate_w.reshape(NIC, 128, NIC, 128).transpose(0, 3, 2, 1).reshape(NIC, 128, D))
    outpT = np.ascontiguousarray(
        out_w.reshape(32, 128, 32, 128).transpose(0, 3, 2, 1).reshape(
            32, 128, D))
    ffnpT = np.ascontiguousarray(
        ffn_w.reshape(32, 128, 32, 128).transpose(0, 3, 2, 1).reshape(
            32, 128, D))
    rwp = np.ascontiguousarray(
        rw.T.reshape(NIC, 128, 8).transpose(1, 0, 2).reshape(128, NIC * 8))
    nwp = np.ascontiguousarray(np.broadcast_to(norm_w[None, :], (128, D)))
    biascp = np.ascontiguousarray(ffn_b.reshape(32, 128).T)

    maskp, qdlp, kdecp, Dm, onesp = _consts()

    shared = dict(qkvp=qkvp, gatep=gatep, outpT=outpT, ffnpT=ffnpT, rwp=rwp,
                  icsp=np.ascontiguousarray(ics), ocsp=np.ascontiguousarray(ocs),
                  nwp=nwp, biascp=biascp, maskp=np.ascontiguousarray(maskp),
                  qdlp=qdlp, kdecp=kdecp, onesp=onesp)

    in_maps = []
    for c in range(NCORE):
        b, jj = c // 4, c % 4
        m = dict(shared)
        xc = np.ascontiguousarray(x[b, jj * TCORE:(jj + 1) * TCORE, :])
        m["x_sh"] = xc
        m["xT_sh"] = np.ascontiguousarray(xc.T)
        m["wcombp"] = _wcomb_for_core(c, Dm)
        in_maps.append(m)
    return in_maps


def kernel(**inputs):
    in_maps = _prep_inputs(inputs)
    nc = _build_nc()
    res = run_bass_kernel_spmd(nc, in_maps, core_ids=list(range(NCORE)))
    out = np.empty((B, S, D), dtype=np.float32)
    for c in range(NCORE):
        b, jj = c // 4, c % 4
        out[b, jj * TCORE:(jj + 1) * TCORE, :] = np.asarray(
            res.results[c]["out_shT"]).T
    return out


# version 14
# speedup vs baseline: 1.0490x; 1.0015x over previous
import numpy as np
import ml_dtypes

import concourse.bass as bass
import concourse.mybir as mybir
from concourse.bass_utils import run_bass_kernel_spmd

F32 = mybir.dt.float32
BF16 = mybir.dt.bfloat16
U8 = mybir.dt.uint8
AF = mybir.ActivationFunctionType
ALU = mybir.AluOpType
AX = mybir.AxisListType
BFNP = ml_dtypes.bfloat16

B = 2
S = 8192
D = 4096
H = 32
DH = 128
BLK = 256
NCORE = 8
TCORE = 2048
HALF = 1024
NSUB = 8
NIC = 32
NBLK_HALF = 4
EPS = 1e-5
HG = 4
NGRP = H // HG
OCB = 256
NOCB = D // OCB

DEBUG_SPILLS = False


def _slopes():
    base = 1.0 / 2.0 ** (8.0 / H)
    factor = 1.0 - 1.0 / (31 + 1e-5) + 1e-5
    return (base ** (np.arange(H, dtype=np.float64) + 1.0)) * factor


def _consts():
    sl = _slopes()
    r = np.arange(BLK, dtype=np.float64)
    diff = r[None, :] - r[:, None]
    maskT = np.where(diff >= 0, np.exp(-sl[:, None, None] * diff[None]), 0.0)
    maskp = maskT.reshape(H, 2, 128, BLK).transpose(0, 2, 1, 3).reshape(H, 128, 2 * BLK)
    maskp = maskp.astype(np.float32).astype(BFNP)
    qdl = np.exp(-sl[:, None] * (r[None, :] + 1.0))
    kdec = np.exp(-sl[:, None] * (BLK - 1.0 - r[None, :]))
    qdlp = qdl.reshape(H, 2, 128).transpose(2, 0, 1).reshape(128, 64).astype(np.float32)
    kdecp = kdec.reshape(H, 2, 128).transpose(2, 0, 1).reshape(128, 64).astype(np.float32)
    Dm = np.exp(-sl * BLK)
    onesp = np.ones((128, 136), dtype=BFNP)
    return maskp, qdlp, kdecp, Dm, onesp


def _wcomb_for_core(core, Dm):
    b, jj = core // 4, core % 4
    w = np.zeros((H, NCORE), dtype=np.float64)
    for c in range(jj):
        w[:, b * 4 + c] = Dm ** (8.0 * (jj - 1 - c))
    out = np.zeros((128, NCORE * H), dtype=np.float32)
    for rk in range(NCORE):
        for h in range(H):
            out[:, rk * H + h] = np.float32(w[h, rk])
    return out


class _Ctr:
    __slots__ = ("sem", "n")

    def __init__(self, sem):
        self.sem = sem
        self.n = 0

    def bump(self, k=1):
        self.n += k
        return self.n


def _build_nc():
    nc = bass.Bass("TRN2", target_bir_lowering=False)
    skind = "ExternalOutput" if DEBUG_SPILLS else "Internal"

    x_d = nc.dram_tensor("x_sh", [TCORE, D], F32, kind="ExternalInput").ap()
    xT_d = nc.dram_tensor("xT_sh", [D, TCORE], F32, kind="ExternalInput").ap()
    qkvp_d = nc.dram_tensor("qkvp", [96, 128, D], F32, kind="ExternalInput").ap()
    gatep_d = nc.dram_tensor("gatep", [32, 128, D], F32, kind="ExternalInput").ap()
    outpT_d = nc.dram_tensor("outpT", [32, 128, D], F32,
                             kind="ExternalInput").ap()
    ffnpT_d = nc.dram_tensor("ffnpT", [32, 128, D], F32,
                             kind="ExternalInput").ap()
    rwp_d = nc.dram_tensor("rwp", [128, 256], F32, kind="ExternalInput").ap()
    icsp_d = nc.dram_tensor("icsp", [8, D], F32, kind="ExternalInput").ap()
    ocsp_d = nc.dram_tensor("ocsp", [8, D], F32, kind="ExternalInput").ap()
    nwp_d = nc.dram_tensor("nwp", [128, D], F32, kind="ExternalInput").ap()
    biascp_d = nc.dram_tensor("biascp", [128, 32], F32, kind="ExternalInput").ap()
    maskp_d = nc.dram_tensor("maskp", [H, 128, 512], BF16, kind="ExternalInput").ap()
    qdlp_d = nc.dram_tensor("qdlp", [128, 64], F32, kind="ExternalInput").ap()
    kdecp_d = nc.dram_tensor("kdecp", [128, 64], F32, kind="ExternalInput").ap()
    wcombp_d = nc.dram_tensor("wcombp", [128, 256], F32, kind="ExternalInput").ap()
    onesp_d = nc.dram_tensor("onesp", [128, 136], BF16, kind="ExternalInput").ap()

    outT_d = nc.dram_tensor("out_shT", [D, TCORE], F32, kind="ExternalOutput").ap()

    aspill = nc.dram_tensor("aspill", [TCORE, D], BF16, kind=skind)
    hspill = nc.dram_tensor("hspill", [128, 32768], BF16, kind=skind)
    qspill = nc.dram_tensor("qspill", [2, H, 128, HALF], BF16, kind=skind)
    gth_in = nc.dram_tensor("gth_in", [128, H * DH], F32, kind=skind)
    gth_out = nc.dram_tensor("gth_out", [NCORE * 128, H * DH], F32, kind=skind)

    Dm64 = np.exp(-_slopes() * BLK)
    Dmf = [float(np.float32(v)) for v in Dm64]
    Dm_pow = [[float(np.float32(Dm64[h] ** m)) for m in range(8)] for h in range(H)]

    A1_OFF = 0
    A2_OFF = 65536
    A3_OFF = 131072
    FX_OFF = A3_OFF + 49152
    fx = {}
    cur = [FX_OFF]

    def fxa(name, nbytes):
        fx[name] = cur[0]
        cur[0] += (nbytes + 31) // 32 * 32

    fxa("h_bf", 8192)
    fxa("dump", 4096)
    fxa("kvst_bf", 8192)
    fxa("eo_bf", 8192)
    fxa("stats", 128)
    fxa("qdl", 256)
    fxa("kdec", 256)
    fxa("wcomb", 1024)
    fxa("ones", 272)
    fxa("rw_bf", 512)
    fxa("bias_col", 128)
    TOTAL = cur[0]
    assert TOTAL <= 212800, TOTAL

    arena = nc.alloc_sbuf_tensor("arena", [128, TOTAL], U8)
    base = nc.lookup_mloc(arena).addr

    def sb(name, shape, dtype, off):
        return nc.alloc_sbuf_tensor_at(name, shape, dtype, offset=base + off).ap()

    hT = sb("hT", [128, 32768], BF16, A1_OFF)
    q_raw = sb("q_raw", [128, 4096], BF16, A2_OFF + 0)
    qchk = sb("qchk", [128, 4096], BF16, A2_OFF + 8192)
    k_sb = sb("k_sb", [128, 4096], BF16, A2_OFF + 16384)
    ktld = sb("ktld", [128, 4096], BF16, A2_OFF + 24576)
    v_tm = sb("v_tm", [128, 4096], BF16, A2_OFF + 32768)
    kv_hist = sb("kv_hist", [128, 2048], BF16, A2_OFF + 40960)
    o_sb = sb("o_sb", [128, 4096], BF16, A2_OFF + 49152)
    smask = sb("smask", [128, 2048], BF16, A2_OFF + 57344)
    gnT = sb("gnT", [128, 32768], BF16, A2_OFF)
    wbuf = sb("wbuf", [128, 8192], BF16, A3_OFF)
    mstr = sb("mstr", [128, 2048], BF16, A3_OFF + 16384)
    ttmp = sb("ttmp", [128, 2048], BF16, A3_OFF + 20480)
    qtm = sb("qtm", [128, 4096], BF16, A3_OFF + 24576)
    kv_all = sb("kv_all", [128, 4096], F32, A3_OFF + 32768)
    gbuf = sb("gbuf", [128, 8192], BF16, A3_OFF + 16384)
    nw_bc = sb("nw_bc", [128, 4096], BF16, A3_OFF + 24576)
    ntmp = sb("ntmp", [128, 4096], BF16, A3_OFF + 16384)
    qrd = sb("qrd", [128, 2048], BF16, A3_OFF + 32768)
    expsb = sb("expsb", [8, 1024], BF16, A3_OFF + 8192)
    h_bf2 = sb("h_bf2", [128, 4096], BF16, A3_OFF + 36864)
    rrow = sb("rrow", [1, 1024], F32, A3_OFF + 10240)
    rrow_bf = sb("rrow_bf", [1, 1024], BF16, A3_OFF + 14336)
    woc = sb("woc", [128, 8192], BF16, A3_OFF)
    outsb = sb("outsb", [128, 2048], F32, A3_OFF + 16384)
    xt_sb = sb("xt_sb", [128, 2048], BF16, A3_OFF + 32768)
    sq_sb = sb("sq_sb", [128, 2048], BF16, A3_OFF + 36864)
    srow = sb("srow", [1, 2048], F32, A3_OFF + 24576)

    h_bf = sb("h_bf_t", [128, 4096], BF16, fx["h_bf"])
    dump = sb("dump_t", [128, 2048], BF16, fx["dump"])
    kvst_bf = sb("kvst_t", [128, 4096], BF16, fx["kvst_bf"])
    xbuf_p1 = sb("xbuf_p1", [128, 8192], BF16, fx["kvst_bf"])
    eo_bf = sb("eo_bf_t", [8, 4096], BF16, fx["eo_bf"])
    os_sb = sb("os_sb_t", [128, 2048], BF16, A3_OFF + 40960)
    rT = sb("rT_t", [8, 1024], BF16, A3_OFF + 45056)
    stats = sb("stats_t", [128, 32], F32, fx["stats"])
    qdl_sb = sb("qdl_t", [128, 64], F32, fx["qdl"])
    kdec_sb = sb("kdec_t", [128, 64], F32, fx["kdec"])
    wcomb_sb = sb("wcomb_t", [128, 256], F32, fx["wcomb"])
    ones_sb = sb("ones_t", [128, 136], BF16, fx["ones"])
    rw_bf = sb("rw_bf_t", [128, 256], BF16, fx["rw_bf"])
    bias_col = sb("bias_col_t", [128, 32], F32, fx["bias_col"])
    vtmp_row = sb("vtmp_row", [1, 1024], F32, fx["dump"])

    SUMC, SQC, NMC, VARC, M2C, STDC, RSTDC, SQ1C = 0, 1, 2, 3, 4, 5, 6, 7

    pst = nc.alloc_psum_tensor("ps", [128, 8, 512], F32)
    ps = pst.ap()

    def pbank(b_, lo=0, hi=512):
        return ps[:, b_, lo:hi]

    def ppair(s):
        return ps[:, 2 * s:2 * s + 2, :]

    def mk(name):
        return _Ctr(nc.alloc_semaphore(name))

    s_x = [mk("s_x0"), mk("s_x1")]
    s_tr = mk("s_tr")
    s_wl = [mk("s_wl0"), mk("s_wl1")]
    s_mm = mk("s_mm")
    s_ev = [mk("s_ev0"), mk("s_ev1")]
    s_msk = mk("s_msk")
    s_po = mk("s_po")
    s_oe = mk("s_oe")
    s_kvp = mk("s_kvp")
    s_kvu = mk("s_kvu")
    s_osp = [mk("s_osp0"), mk("s_osp1"), mk("s_osp2"), mk("s_osp3")]
    s_qsp = mk("s_qsp")
    s_sum = mk("s_sum")
    s_rd = [mk("s_rd0"), mk("s_rd1")]
    s_att = mk("s_att")
    s_mld = mk("s_mld")
    s_nw = mk("s_nw")
    s_bia = mk("s_bia")
    s_eo = mk("s_eo")
    s_cc = mk("s_cc")
    s_gb = [mk("s_gb0"), mk("s_gb1")]
    s_st = mk("s_st")
    s_dv = mk("s_dv")
    s_ac = mk("s_ac")
    s_ou = mk("s_ou")
    s_pe = mk("s_pe")
    s_cst = mk("s_cst")
    s_rw = mk("s_rw")
    s_hs = mk("s_hs")
    s_hr = mk("s_hr")

    st = {
        "xt_users": [(None, 0), (None, 0)],
        "xp1_users": [(None, 0), (None, 0)],
        "sq_used": [0, 0],
        "srow_add": 0,
        "wl_freed": [0, 0],
        "wl_cnt": [0, 0],
        "ps_main": [(None, 0), (None, 0)],
        "scores_free": [0, 0],
        "smask_free": [0, 0, 0, 0],
        "obank_free": [0, 0, 0, 0],
        "osb_used": [0, 0, 0, 0],
        "mstr_free": 0,
        "head_kvu": {},
        "ph2_pe": 0,
        "ps45_free": 0,
        "cw_used": [0, 0],
        "ctm_ps": [0, 0],
        "kvps_free": [0, 0, 0, 0],
        "v_tr": [0, 0, 0, 0],
        "ttmp_user": [(None, 0), (None, 0)],
        "prev_scan_pe": 0,
        "prev_spill": 0,
        "qrd_used": [0, 0],
        "ps_c": [0, 0],
        "ntmp_used": 0,
        "woc_used": [0, 0],
        "os_used": [0, 0],
        "ffn_ps_used": [0, 0],
        "outsb_used": [0, 0],
        "comb_done": 0,
        "gb_free": [0, 0],
        "prev_grp_tr": 0,
        "comb_prev": 0,
        "eo_user": 0,
        "hbf_free": 0,
        "hbf2_free": 0,
    }

    block = nc.Block()
    block.__enter__()

    @block.sync
    def _(s):
        s.dma_start(qdl_sb, qdlp_d).then_inc(s_cst.sem, 16)
        s.dma_start(kdec_sb, kdecp_d).then_inc(s_cst.sem, 16)
        s.dma_start(wcomb_sb, wcombp_d).then_inc(s_cst.sem, 16)
        s.dma_start(ones_sb, onesp_d).then_inc(s_cst.sem, 16)
        s.dma_start(bias_col, biascp_d).then_inc(s_cst.sem, 16)
    s_cst.bump(80)

    @block.gpsimd
    def _(g):
        g.dma_start(rw_bf, rwp_d).then_inc(s_rw.sem, 16)
    s_rw.bump(16)
    CST_TOTAL = s_cst.n

    @block.vector
    def _(v):
        v.memset(kv_all, 0.0)
        v.engine_nop().then_inc(s_dv.sem, 1)
    kv_zero = s_dv.bump()

    def tr_dma(s, dst, src):
        s.wait_ge(s_tr.sem, s_tr.n)
        s.dma_start(dst, src, transpose=True).then_inc(s_tr.sem, 16)
        return s_tr.bump(16)

    def chain(eng, ins):
        ins.then_inc(s_dv.sem, 1)
        eng.wait_ge(s_dv.sem, s_dv.bump())

    def ln_subtile(half, sub, dst2d, hT_guard_mm=0, spill_wait=0):
        slot = sub % 2
        xb = xbuf_p1[:, slot * 4096:(slot + 1) * 4096]

        @block.gpsimd
        def _(g):
            c, v = st["xp1_users"][slot]
            if c is not None:
                g.wait_ge(c.sem, v)
            g.dma_start(xb, x_d[half * HALF + sub * 128:
                               half * HALF + (sub + 1) * 128, :]).then_inc(
                s_x[slot].sem, 16)
        xv = s_x[slot].bump(16)

        @block.scalar
        def _(sc):
            sc.wait_ge(s_x[slot].sem, xv)
            sc.wait_ge(s_st.sem, s_st.n)
            sc.activation(dump, xb[:, 0:2048], AF.Square,
                          accum_out=stats[:, SQC:SQC + 1]).then_inc(s_ac.sem, 1)
            sc.wait_ge(s_ac.sem, s_ac.bump())
            sc.activation(dump, xb[:, 2048:4096], AF.Square,
                          accum_out=stats[:, SQ1C:SQ1C + 1]).then_inc(s_st.sem, 1)
        sq_v = s_st.bump()

        @block.vector
        def _(v):
            v.wait_ge(s_x[slot].sem, xv)
            chain(v, v.reduce_sum(stats[:, SUMC:SUMC + 1], xb, axis=AX.X))
            v.wait_ge(s_st.sem, sq_v)
            chain(v, v.tensor_tensor(stats[:, SQC:SQC + 1], stats[:, SQC:SQC + 1],
                                     stats[:, SQ1C:SQ1C + 1], ALU.add))
            chain(v, v.tensor_scalar(stats[:, NMC:NMC + 1],
                                     stats[:, SUMC:SUMC + 1],
                                     -1.0 / D, None, ALU.mult))
            chain(v, v.tensor_scalar(stats[:, VARC:VARC + 1],
                                     stats[:, SQC:SQC + 1],
                                     1.0 / D, EPS, ALU.mult, ALU.add))
            chain(v, v.tensor_tensor(stats[:, M2C:M2C + 1], stats[:, NMC:NMC + 1],
                                     stats[:, NMC:NMC + 1], ALU.mult))
            v.tensor_tensor(stats[:, VARC:VARC + 1], stats[:, VARC:VARC + 1],
                            stats[:, M2C:M2C + 1], ALU.subtract).then_inc(s_dv.sem, 1)
        var_v = s_dv.bump()

        @block.scalar
        def _(sc):
            sc.wait_ge(s_dv.sem, var_v)
            sc.activation(stats[:, STDC:STDC + 1], stats[:, VARC:VARC + 1],
                          AF.Sqrt).then_inc(s_ac.sem, 1)
        std_v = s_ac.bump()

        @block.vector
        def _(v):
            v.wait_ge(s_ac.sem, std_v)
            chain(v, v.reciprocal(stats[:, RSTDC:RSTDC + 1],
                                  stats[:, STDC:STDC + 1]))
            v.wait_ge(s_tr.sem, st["hbf_free"])
            v.tensor_scalar(h_bf, xb, stats[:, NMC:NMC + 1],
                            stats[:, RSTDC:RSTDC + 1], ALU.add, ALU.mult).then_inc(
                s_dv.sem, 1)
        norm_v = s_dv.bump()
        st["xp1_users"][slot] = (s_dv, norm_v)

        @block.sync
        def _(s):
            s.wait_ge(s_dv.sem, norm_v)
            if sub == 0:
                if hT_guard_mm > 0:
                    s.wait_ge(s_mm.sem, hT_guard_mm)
                if spill_wait > 0:
                    s.wait_ge(s_hs.sem, spill_wait)
            tr_dma(s,
                   dst2d.rearrange("p (c t) -> p c t", t=1024)[
                       :, :, sub * 128:(sub + 1) * 128],
                   h_bf.rearrange("p (c f) -> p c f", f=128))
        st["hbf_free"] = s_tr.n

    def load_wcb(src_ap, elems=4096, extra=()):
        t = load_wcb.count
        slot = t % 2
        load_wcb.count += 1

        @block.gpsimd
        def _(g):
            if st["wl_cnt"][slot] > 0:
                g.wait_ge(s_mm.sem, st["wl_freed"][slot])
            for cc, vv in extra:
                if vv > 0:
                    g.wait_ge(cc.sem, vv)
            g.dma_start(wbuf[:, slot * 4096:slot * 4096 + elems],
                        src_ap).then_inc(s_wl[slot].sem, 16)
        st["wl_cnt"][slot] += 1
        return slot, s_wl[slot].bump(16)

    load_wcb.count = 0

    def fm_cb_matmul(wslot, wlv, rhs2d, pslot, pre=None):
        @block.tensor
        def _(t):
            t.wait_ge(s_wl[wslot].sem, wlv)
            c, v = st["ps_main"][pslot]
            if c is not None:
                t.wait_ge(c.sem, v)
            if pre is not None:
                for (cc, vv) in pre:
                    if vv > 0:
                        t.wait_ge(cc.sem, vv)
            for ic in range(NIC):
                lhsT = wbuf[:, wslot * 4096 + ic * 128:wslot * 4096 + (ic + 1) * 128]
                t.matmul(ps[:, 2 * pslot, :], lhsT,
                         rhs2d[:, ic * 1024:ic * 1024 + 512],
                         start=(ic == 0), stop=(ic == NIC - 1))
                ins = t.matmul(ps[:, 2 * pslot + 1, :], lhsT,
                               rhs2d[:, ic * 1024 + 512:(ic + 1) * 1024],
                               start=(ic == 0), stop=(ic == NIC - 1))
                if ic == NIC - 1:
                    ins.then_inc(s_mm.sem, 1)
        st["wl_freed"][wslot] = s_mm.bump()
        return s_mm.n

    def p1_half(half, hT_guard_mm=0, spill_wait=0):
        for sub in range(NSUB):
            ln_subtile(half, sub, hT, hT_guard_mm, spill_wait)
        hT_ready = s_tr.n

        for hg in range(NGRP):
            heads = [hg * HG + i for i in range(HG)]
            ev_marks = {}
            prev_pe = st["prev_scan_pe"]
            prev_grp_tr = st["prev_grp_tr"]
            scan_entry = (s_msk.n, s_oe.n)

            pend_ev = []

            def qkv_mm(comp, i4):
                cb = hg * 12 + comp * 4 + i4
                wslot, wlv = load_wcb(qkvp_d[cb])
                pslot = load_wcb.count % 2

                @block.tensor
                def _(t):
                    t.wait_ge(s_wl[wslot].sem, wlv)
                    t.wait_ge(s_tr.sem, hT_ready)
                    if comp == 0 and i4 < 2:
                        if scan_entry[0] > 0:
                            t.wait_ge(s_msk.sem, scan_entry[0])
                        if scan_entry[1] > 0:
                            t.wait_ge(s_oe.sem, scan_entry[1])
                    c, v = st["ps_main"][pslot]
                    if c is not None:
                        t.wait_ge(c.sem, v)
                    for ic in range(NIC):
                        lhsT = wbuf[:, wslot * 4096 + ic * 128:
                                    wslot * 4096 + (ic + 1) * 128]
                        t.matmul(ps[:, 2 * pslot, :], lhsT,
                                 hT[:, ic * 1024:ic * 1024 + 512],
                                 start=(ic == 0), stop=(ic == NIC - 1))
                        ins = t.matmul(ps[:, 2 * pslot + 1, :], lhsT,
                                       hT[:, ic * 1024 + 512:(ic + 1) * 1024],
                                       start=(ic == 0), stop=(ic == NIC - 1))
                        if ic == NIC - 1:
                            ins.then_inc(s_mm.sem, 1)
                st["wl_freed"][wslot] = s_mm.bump()
                mmv = s_mm.n

                if comp < 2:
                    evdst = (q_raw, k_sb)[comp][:, i4 * 1024:(i4 + 1) * 1024]
                else:
                    evdst = ttmp[:, (i4 % 2) * 1024:(i4 % 2 + 1) * 1024]
                tw = st["ttmp_user"][i4 % 2] if comp == 2 else None
                pend_ev.append((pslot, mmv, evdst, tw))
                s_ev[pslot].bump()
                st["ps_main"][pslot] = (s_ev[pslot], s_ev[pslot].n)
                ev_marks[(comp, i4)] = (s_ev[pslot], s_ev[pslot].n)

            def flush_ev():
                evs = list(pend_ev)
                pend_ev.clear()

                @block.scalar
                def _(sc):
                    if prev_pe > 0:
                        sc.wait_ge(s_po.sem, prev_pe)
                    if prev_grp_tr > 0:
                        sc.wait_ge(s_tr.sem, prev_grp_tr)
                    for (pslot, mmv, evdst, tw) in evs:
                        sc.wait_ge(s_mm.sem, mmv)
                        if tw is not None and tw[0] is not None:
                            sc.wait_ge(tw[0].sem, tw[1])
                        sc.activation(evdst, ppair(pslot), AF.Silu).then_inc(
                            s_ev[pslot].sem, 1)

            def v_tr(i4):
                @block.sync
                def _(s):
                    c, v = ev_marks[(2, i4)]
                    s.wait_ge(c.sem, v)
                    if prev_pe > 0:
                        s.wait_ge(s_po.sem, prev_pe)
                    tr_dma(s,
                           v_tm[:, i4 * 1024:(i4 + 1) * 1024].rearrange(
                               "p (c t) -> p c t", t=128),
                           ttmp[:, (i4 % 2) * 1024:(i4 % 2 + 1) * 1024]
                           .rearrange("p (c f) -> p c f", f=128))
                st["ttmp_user"][i4 % 2] = (s_tr, s_tr.n)
                st["v_tr"][i4] = s_tr.n

            for comp in range(2):
                for i4 in range(HG):
                    qkv_mm(comp, i4)
                flush_ev()

            kq_ev = [(s_ev[0], s_ev[0].n), (s_ev[1], s_ev[1].n)]

            @block.sync
            def _(s):
                for cc, vv in kq_ev:
                    s.wait_ge(cc.sem, vv)
                if prev_pe > 0:
                    s.wait_ge(s_po.sem, prev_pe)
                tr_dma(s, ktld.rearrange("p (c t) -> p c t", t=128),
                       k_sb.rearrange("p (c f) -> p c f", f=128))
            ktr = s_tr.n

            @block.vector
            def _(v):
                v.wait_ge(s_tr.sem, ktr)
                v.wait_ge(s_cst.sem, CST_TOTAL)
                for i4 in range(HG):
                    h = heads[i4]
                    for par in range(2):
                        view = ktld[:, i4 * 1024:(i4 + 1) * 1024].rearrange(
                            "p (c t) -> p c t", t=128)[:, par::2, :]
                        ins = v.tensor_scalar(view, view,
                                              kdec_sb[:, 2 * h + par:2 * h + par + 1],
                                              None, ALU.mult)
                        if i4 == HG - 1 and par == 1:
                            ins.then_inc(s_dv.sem, 1)
            kscale_v = s_dv.bump()

            @block.sync
            def _(s):
                for cc, vv in kq_ev:
                    s.wait_ge(cc.sem, vv)
                tr_dma(s, qtm.rearrange("p (c t) -> p c t", t=128),
                       q_raw.rearrange("p (c f) -> p c f", f=128))
            qtr = s_tr.n

            @block.vector
            def _(v):
                v.wait_ge(s_tr.sem, qtr)
                v.wait_ge(s_cst.sem, CST_TOTAL)
                for i4 in range(HG):
                    h = heads[i4]
                    for par in range(2):
                        view = qtm[:, i4 * 1024:(i4 + 1) * 1024].rearrange(
                            "p (c t) -> p c t", t=128)[:, par::2, :]
                        ins = v.tensor_scalar(view, view,
                                              qdl_sb[:, 2 * h + par:2 * h + par + 1],
                                              None, ALU.mult)
                        if i4 == HG - 1 and par == 1:
                            ins.then_inc(s_dv.sem, 1)
            qscale_v = s_dv.bump()

            @block.sync
            def _(s):
                s.wait_ge(s_dv.sem, qscale_v)
                s.wait_ge(s_qsp.sem, st["prev_spill"])
                if prev_pe > 0:
                    s.wait_ge(s_po.sem, prev_pe)
                tr_dma(s, qchk.rearrange("p (c t) -> p c t", t=128),
                       qtm.rearrange("p (c f) -> p c f", f=128))
            qctr = s_tr.n

            @block.sync
            def _(s):
                s.wait_ge(s_tr.sem, qctr)
                dview = qspill.ap()[half, hg * HG:(hg + 1) * HG].rearrange(
                    "h p t -> p h t")
                s.dma_start(dview,
                            qchk.rearrange("p (h t) -> p h t", t=HALF)
                            ).then_inc(s_qsp.sem, 16)
            s_qsp.bump(16)
            st["prev_spill"] = s_qsp.n

            @block.sync
            def _(s):
                if st["mstr_free"] > 0:
                    s.wait_ge(s_msk.sem, st["mstr_free"])
                dview = maskp_d[hg * HG:(hg + 1) * HG].rearrange("h p n -> p h n")
                s.dma_start(mstr.rearrange("p (h n) -> p h n", n=512),
                            dview).then_inc(s_mld.sem, 16)
            mload = s_mld.bump(16)

            for i4 in range(2):
                qkv_mm(2, i4)
            flush_ev()
            v_tr(0)
            v_tr(1)
            for i4 in range(2, HG):
                qkv_mm(2, i4)
            flush_ev()
            v_tr(2)
            v_tr(3)

            grp_dv = s_dv.n
            kq_ev2 = [(s_ev[0], s_ev[0].n), (s_ev[1], s_ev[1].n)]

            def emit_scores(k):
                blk, i4 = k // HG, k % HG
                sslot = k % 2

                @block.tensor
                def _(t):
                    if k == 0:
                        t.wait_ge(s_dv.sem, max(grp_dv, kv_zero))
                        for cc, vv in kq_ev2:
                            t.wait_ge(cc.sem, vv)
                    if st["scores_free"][sslot] > 0:
                        t.wait_ge(s_msk.sem, st["scores_free"][sslot])
                    for tkc in range(2):
                        lhsT = k_sb[:, i4 * 1024 + blk * 256 + tkc * 128:
                                    i4 * 1024 + blk * 256 + (tkc + 1) * 128]
                        rhs = q_raw[:, i4 * 1024 + blk * 256:
                                    i4 * 1024 + (blk + 1) * 256]
                        ins = t.matmul(pbank(sslot, tkc * 256, (tkc + 1) * 256),
                                       lhsT, rhs, start=True, stop=True)
                        if tkc == 1:
                            ins.then_inc(s_pe.sem, 1)
                return s_pe.bump()

            sc_done = {}
            sc_done[0] = emit_scores(0)
            oe_of_head = [0, 0, 0, 0]
            for blk in range(NBLK_HALF):
                pend_oe = []
                for i4 in range(HG):
                    h = heads[i4]
                    k = blk * HG + i4
                    sslot = k % 2
                    obank = 2 + i4 % 2
                    ocolh = i4 // 2
                    kslot = k % 4
                    kbank = 6 + kslot % 2
                    kcol = (kslot // 2) * 128

                    if k + 1 < NBLK_HALF * HG:
                        sc_done[k + 1] = emit_scores(k + 1)

                    @block.vector
                    def _(v):
                        v.wait_ge(s_pe.sem, sc_done[k])
                        v.wait_ge(s_mld.sem, mload)
                        if st["smask_free"][i4] > 0:
                            v.wait_ge(s_po.sem, st["smask_free"][i4])
                        v.tensor_tensor(smask[:, i4 * 512:(i4 + 1) * 512],
                                        pbank(sslot),
                                        mstr[:, i4 * 512:(i4 + 1) * 512],
                                        ALU.mult).then_inc(s_msk.sem, 1)
                    msk_v = s_msk.bump()
                    st["scores_free"][sslot] = msk_v

                    @block.tensor
                    def _(t):
                        t.wait_ge(s_msk.sem, msk_v)
                        if blk == 0:
                            t.wait_ge(s_tr.sem, st["v_tr"][i4])
                        if st["obank_free"][i4] > 0:
                            t.wait_ge(s_oe.sem, st["obank_free"][i4])
                        for tqc in range(2):
                            dst = pbank(obank, ocolh * 256 + tqc * 128,
                                        ocolh * 256 + (tqc + 1) * 128)
                            for tkc in range(2):
                                lhsT = smask[:, i4 * 512 + tkc * 256 + tqc * 128:
                                             i4 * 512 + tkc * 256 + (tqc + 1) * 128]
                                rhs = v_tm[:, i4 * 1024 + (blk * 2 + tkc) * 128:
                                           i4 * 1024 + (blk * 2 + tkc + 1) * 128]
                                ins = t.matmul(dst, lhsT, rhs,
                                               start=(tkc == 0), stop=(tkc == 1))
                                if tqc == 1 and tkc == 1:
                                    ins.then_inc(s_po.sem, 1)
                        if st["kvps_free"][kslot] > 0:
                            t.wait_ge(s_kvu.sem, st["kvps_free"][kslot])
                        for tkc in range(2):
                            ins = t.matmul(
                                pbank(kbank, kcol, kcol + 128),
                                ktld[:, i4 * 1024 + (blk * 2 + tkc) * 128:
                                     i4 * 1024 + (blk * 2 + tkc + 1) * 128],
                                v_tm[:, i4 * 1024 + (blk * 2 + tkc) * 128:
                                     i4 * 1024 + (blk * 2 + tkc + 1) * 128],
                                start=(tkc == 0), stop=(tkc == 1))
                            if tkc == 1:
                                ins.then_inc(s_kvp.sem, 1)
                    po_v = s_po.bump()
                    kvp_v = s_kvp.bump()
                    st["smask_free"][i4] = po_v

                    osw = st["osb_used"][i4] if blk == 0 else 0
                    pend_oe.append((po_v, osw, i4, obank, ocolh))
                    st["obank_free"][i4] = s_oe.bump()
                    oe_of_head[i4] = s_oe.n

                    @block.vector
                    def _(v):
                        v.wait_ge(s_kvp.sem, kvp_v)
                        if blk == 0:
                            v.wait_ge(s_dv.sem, kv_zero)
                            if st["ph2_pe"] > 0:
                                v.wait_ge(s_po.sem, st["ph2_pe"])
                        v.wait_ge(s_kvu.sem, s_kvu.n)
                        kvc = kv_all[:, h * 128:(h + 1) * 128]
                        v.tensor_copy(
                            kv_hist[:, i4 * 512 + blk * 128:
                                    i4 * 512 + (blk + 1) * 128],
                            kvc).then_inc(s_kvu.sem, 1)
                        v.scalar_tensor_tensor(
                            kvc, kvc, Dmf[h], pbank(kbank, kcol, kcol + 128),
                            ALU.mult, ALU.add).then_inc(s_kvu.sem, 1)
                    s_kvu.bump(2)
                    st["head_kvu"][h] = s_kvu.n - 1
                    st["kvps_free"][kslot] = s_kvu.n

                @block.scalar
                def _(sc):
                    for (po_vv, osw, i4b, obankb, och) in pend_oe:
                        sc.wait_ge(s_po.sem, po_vv)
                        if osw > 0:
                            sc.wait_ge(s_osp[i4b].sem, osw)
                        sc.activation(
                            o_sb[:, i4b * 1024 + blk * 256:
                                 i4b * 1024 + (blk + 1) * 256],
                            pbank(obankb, och * 256, (och + 1) * 256),
                            AF.Copy).then_inc(s_oe.sem, 1)

            st["mstr_free"] = s_msk.n

            for i4 in range(HG):
                h = heads[i4]

                @block.tensor
                def _(t):
                    t.wait_ge(s_kvu.sem, st["head_kvu"][h])
                    if st["ps45_free"] > 0:
                        t.wait_ge(s_dv.sem, st["ps45_free"])
                    for m in range(NBLK_HALF):
                        rhs = kv_hist[:, i4 * 512 + m * 128:
                                      i4 * 512 + (m + 1) * 128]
                        for tqc in range(2):
                            ins = t.matmul(
                                pbank(4 + m // 2, (m % 2) * 256 + tqc * 128,
                                      (m % 2) * 256 + (tqc + 1) * 128),
                                qchk[:, i4 * 1024 + m * 256 + tqc * 128:
                                     i4 * 1024 + m * 256 + (tqc + 1) * 128],
                                rhs, start=True, stop=True)
                            if m == NBLK_HALF - 1 and tqc == 1:
                                ins.then_inc(s_po.sem, 1)
                ph2_v = s_po.bump()

                @block.vector
                def _(v):
                    v.wait_ge(s_po.sem, ph2_v)
                    v.wait_ge(s_oe.sem, oe_of_head[i4])
                    ov = o_sb[:, i4 * 1024:(i4 + 1) * 1024].rearrange(
                        "p (a b) -> p a b", b=512)
                    v.tensor_tensor(ov, ps[:, 4:6, :], ov, ALU.add).then_inc(
                        s_dv.sem, 1)
                add_v = s_dv.bump()
                st["ps45_free"] = add_v

                @block.sync
                def _(s):
                    s.wait_ge(s_dv.sem, add_v)
                    view = aspill.ap()[half * HALF:(half + 1) * HALF,
                                       h * 128:(h + 1) * 128].rearrange(
                        "(q p) e -> p q e", p=128)
                    s.dma_start(view,
                                o_sb[:, i4 * 1024:(i4 + 1) * 1024]
                                .rearrange("p (q e) -> p q e", e=128)
                                ).then_inc(s_osp[i4].sem, 16)
                st["osb_used"][i4] = s_osp[i4].bump(16)
            st["ph2_pe"] = s_po.n

            st["prev_scan_pe"] = s_po.n
            st["prev_grp_tr"] = qtr

    def p2_half(half, skip_A, do_combine=False):
        p2_tr_entry = s_tr.n
        dv_entry = s_dv.n
        ac_entry = s_ac.n
        mm_entry = s_mm.n

        if not skip_A:
            @block.sync
            def _(s):
                s.wait_ge(s_dv.sem, dv_entry)
                s.wait_ge(s_ac.sem, ac_entry)
                s.wait_ge(s_mm.sem, mm_entry)
                s.dma_start(hT[:, 0:16384],
                            hspill.ap()[:, 0:16384]).then_inc(s_hr.sem, 16)

            @block.scalar
            def _(sc):
                sc.wait_ge(s_dv.sem, dv_entry)
                sc.wait_ge(s_mm.sem, mm_entry)
                sc.dma_start(hT[:, 16384:32768],
                             hspill.ap()[:, 16384:32768]).then_inc(s_hr.sem, 16)
            hT_wait = (s_hr, s_hr.bump(32))
        else:
            hT_wait = (s_tr, s_tr.n)
        hT_ready = s_tr.n

        b_scan_entry = (s_msk.n, s_oe.n)
        b_guard = dict(kvp=s_kvp.n, kvu=s_kvu.n, po=s_po.n,
                       osp=[s_osp[ii].n for ii in range(4)],
                       qsp=s_qsp.n, msk=s_msk.n, oe=s_oe.n)
        pend_gev = []

        def flush_gev():
            evs = list(pend_gev)
            pend_gev.clear()

            @block.scalar
            def _(sc):
                for (fcb, pslot, mmv) in evs:
                    sc.wait_ge(s_mm.sem, mmv)
                    if fcb == 0:
                        sc.wait_ge(s_kvp.sem, b_guard["kvp"])
                        sc.wait_ge(s_kvu.sem, b_guard["kvu"])
                        sc.wait_ge(s_po.sem, b_guard["po"])
                        for ii in range(4):
                            sc.wait_ge(s_osp[ii].sem, b_guard["osp"][ii])
                        sc.wait_ge(s_qsp.sem, b_guard["qsp"])
                        sc.wait_ge(s_msk.sem, b_guard["msk"])
                        sc.wait_ge(s_oe.sem, b_guard["oe"])
                        sc.wait_ge(s_dv.sem, dv_entry)
                        sc.wait_ge(s_tr.sem, hT_ready)
                    sc.activation(gnT[:, fcb * 1024:(fcb + 1) * 1024].rearrange(
                        "p (a b) -> p a b", b=512),
                        ppair(pslot), AF.Sigmoid).then_inc(s_ev[pslot].sem, 1)

        for fc in range(NIC):
            extra = [(s_mm, mm_entry), (s_ac, ac_entry)] if fc < 2 else ()
            wslot, wlv = load_wcb(gatep_d[fc], extra=extra)
            pslot = fc % 2

            @block.tensor
            def _(t):
                t.wait_ge(s_wl[wslot].sem, wlv)
                if fc < 2:
                    t.wait_ge(hT_wait[0].sem, hT_wait[1])
                    t.wait_ge(s_dv.sem, dv_entry)
                    if b_scan_entry[0] > 0:
                        t.wait_ge(s_msk.sem, b_scan_entry[0])
                    if b_scan_entry[1] > 0:
                        t.wait_ge(s_oe.sem, b_scan_entry[1])
                c, v = st["ps_main"][pslot]
                if c is not None:
                    t.wait_ge(c.sem, v)
                for ic in range(NIC):
                    lhsT = wbuf[:, wslot * 4096 + ic * 128:wslot * 4096 + (ic + 1) * 128]
                    t.matmul(ps[:, 2 * pslot, :], lhsT,
                             hT[:, ic * 1024:ic * 1024 + 512],
                             start=(ic == 0), stop=(ic == NIC - 1))
                    ins = t.matmul(ps[:, 2 * pslot + 1, :], lhsT,
                                   hT[:, ic * 1024 + 512:(ic + 1) * 1024],
                                   start=(ic == 0), stop=(ic == NIC - 1))
                    if ic == NIC - 1:
                        ins.then_inc(s_mm.sem, 1)
            st["wl_freed"][wslot] = s_mm.bump()
            mmv = s_mm.n

            pend_gev.append((fc, pslot, mmv))
            s_ev[pslot].bump()
            st["ps_main"][pslot] = (s_ev[pslot], s_ev[pslot].n)
            if fc % 4 == 3:
                flush_gev()
        b_mm_done = s_mm.n

        if do_combine:
            emit_combine()

        @block.gpsimd
        def _(g):
            g.wait_ge(s_mm.sem, b_mm_done)
            g.wait_ge(s_tr.sem, p2_tr_entry)
            g.wait_ge(s_dv.sem, st["comb_done"])
            g.dma_start(nw_bc, nwp_d).then_inc(s_nw.sem, 16)
        nw_v = s_nw.bump(16)

        osp_marks = [s_osp[ii].n for ii in range(4)]

        @block.sync
        def _(s):
            s.wait_ge(s_mm.sem, b_mm_done)
            for ii in range(4):
                s.wait_ge(s_osp[ii].sem, osp_marks[ii])
            view = hT.rearrange("p (sub f) -> p sub f", f=4096)[:, 0:4, :]
            src = aspill.ap()[half * HALF:half * HALF + 512, :].rearrange(
                "(sub p) f -> p sub f", p=128)
            s.dma_start(view, src).then_inc(s_att.sem, 16)

        @block.scalar
        def _(sc):
            sc.wait_ge(s_mm.sem, b_mm_done)
            for ii in range(4):
                sc.wait_ge(s_osp[ii].sem, osp_marks[ii])
            view = hT.rearrange("p (sub f) -> p sub f", f=4096)[:, 4:8, :]
            src = aspill.ap()[half * HALF + 512:(half + 1) * HALF, :].rearrange(
                "(sub p) f -> p sub f", p=128)
            sc.dma_start(view, src).then_inc(s_att.sem, 16)
        attn_v = s_att.bump(32)

        qsp_total = s_qsp.n
        for h in range(H):
            qslot = h % 2

            def qrd_load(q):
                if st["qrd_used"][qslot] > 0:
                    q.wait_ge(s_pe.sem, st["qrd_used"][qslot])
                if h < 2:
                    q.wait_ge(s_dv.sem, st["comb_done"])
                    ov = max(st["outsb_used"])
                    if ov > 0:
                        q.wait_ge(s_ou.sem, ov)
                q.wait_ge(s_qsp.sem, qsp_total)
                q.dma_start(qrd[:, qslot * 1024:(qslot + 1) * 1024],
                            qspill.ap()[half, h]).then_inc(s_rd[qslot].sem, 16)

            if qslot == 0:
                @block.sync
                def _(s):
                    qrd_load(s)
            else:
                @block.scalar
                def _(sc):
                    qrd_load(sc)
            qv = s_rd[qslot].bump(16)

            for m in range(NBLK_HALF):
                gm = half * NBLK_HALF + m
                cslot = (h * NBLK_HALF + m) % 2

                @block.tensor
                def _(t):
                    t.wait_ge(s_rd[qslot].sem, qv)
                    if h == 0 and m == 0:
                        t.wait_ge(s_dv.sem, max(st["comb_done"], dv_entry))
                    if st["ps_c"][cslot] > 0:
                        t.wait_ge(s_dv.sem, st["ps_c"][cslot])
                    for tqc in range(2):
                        ins = t.matmul(
                            pbank(4 + cslot, tqc * 128, (tqc + 1) * 128),
                            qrd[:, qslot * 1024 + m * 256 + tqc * 128:
                                qslot * 1024 + m * 256 + (tqc + 1) * 128],
                            kvst_bf[:, h * 128:(h + 1) * 128],
                            start=True, stop=True)
                        if tqc == 1:
                            ins.then_inc(s_pe.sem, 1)
                pe_v = s_pe.bump()

                @block.vector
                def _(v):
                    v.wait_ge(s_pe.sem, pe_v)
                    v.wait_ge(s_att.sem, attn_v)
                    attn_vw = hT.rearrange("p (sub f) -> p sub f", f=4096)[
                        :, m * 2:m * 2 + 2, h * 128:(h + 1) * 128]
                    psv = pbank(4 + cslot, 0, 256).rearrange(
                        "p (q e) -> p q e", e=128)
                    v.scalar_tensor_tensor(attn_vw, psv, Dm_pow[h][gm], attn_vw,
                                           ALU.mult, ALU.add).then_inc(s_dv.sem, 1)
                st["ps_c"][cslot] = s_dv.bump()
            st["qrd_used"][qslot] = s_pe.n
        corr_done = s_dv.n

        for sub in range(NSUB):
            attn_t = hT[:, sub * 4096:(sub + 1) * 4096]

            @block.scalar
            def _(sc):
                if sub == 0:
                    sc.wait_ge(s_dv.sem, corr_done)
                else:
                    sc.wait_ge(s_ac.sem, s_ac.n)
                sc.activation(dump, attn_t[:, 0:2048], AF.Square,
                              accum_out=stats[:, sub:sub + 1]).then_inc(s_ac.sem, 1)
                sc.wait_ge(s_ac.sem, s_ac.bump())
                ins = sc.activation(dump, attn_t[:, 2048:4096], AF.Square,
                                    accum_out=stats[:, 8 + sub:9 + sub])
                if sub == NSUB - 1:
                    ins.then_inc(s_st.sem, 1)
                else:
                    ins.then_inc(s_ac.sem, 1)
                    s_ac.bump()
        sq_all = s_st.bump()

        @block.vector
        def _(v):
            v.wait_ge(s_st.sem, sq_all)
            chain(v, v.tensor_tensor(stats[:, 0:8], stats[:, 0:8],
                                     stats[:, 8:16], ALU.add))
            v.tensor_scalar(stats[:, 16:24], stats[:, 0:8],
                            1.0 / D, EPS, ALU.mult, ALU.add).then_inc(s_dv.sem, 1)
        var_v = s_dv.bump()

        @block.scalar
        def _(sc):
            sc.wait_ge(s_dv.sem, var_v)
            sc.activation(stats[:, 16:24], stats[:, 16:24],
                          AF.Sqrt).then_inc(s_ac.sem, 1)
        std_v = s_ac.bump()

        @block.vector
        def _(v):
            v.wait_ge(s_ac.sem, std_v)
            v.reciprocal(stats[:, 24:32], stats[:, 16:24]).then_inc(s_dv.sem, 1)
        rstd_v = s_dv.bump()

        gn_marks = []
        for sub in range(NSUB):
            attn_t = hT[:, sub * 4096:(sub + 1) * 4096]
            hb = h_bf if sub % 2 == 0 else h_bf2
            hb_key = "hbf2_free" if sub % 2 else "hbf_free"

            @block.vector
            def _(v):
                if sub == 0:
                    v.wait_ge(s_dv.sem, rstd_v)
                    v.wait_ge(s_nw.sem, nw_v)
                v.wait_ge(s_tr.sem, st[hb_key])
                v.scalar_tensor_tensor(hb, attn_t,
                                       stats[:, 24 + sub:25 + sub],
                                       nw_bc, ALU.mult, ALU.mult).then_inc(
                    s_dv.sem, 1)
            norm_v = s_dv.bump()

            @block.sync
            def _(s):
                s.wait_ge(s_dv.sem, norm_v)
                if st["ntmp_used"] > 0:
                    s.wait_ge(s_dv.sem, st["ntmp_used"])
                if sub == 0:
                    ov = max(st["outsb_used"])
                    if ov > 0:
                        s.wait_ge(s_ou.sem, ov)
                tr_dma(s, ntmp.rearrange("p (c t) -> p c t", t=128),
                       hb.rearrange("p (c f) -> p c f", f=128))
            ntr = s_tr.n
            st[hb_key] = ntr

            @block.vector
            def _(v):
                v.wait_ge(s_tr.sem, ntr)
                gv = gnT.rearrange("p (c t) -> p c t", t=1024)[
                    :, :, sub * 128:(sub + 1) * 128]
                v.tensor_tensor(gv, gv,
                                ntmp.rearrange("p (c t) -> p c t", t=128),
                                ALU.mult).then_inc(s_dv.sem, 1)
            st["ntmp_used"] = s_dv.bump()
            gn_marks.append(s_dv.n)
        gn_done = s_dv.n

        NOCH = 32
        evac_marks = {}
        sq_marks = {}

        def emit_stats(oc):
            @block.tensor
            def _(t):
                t.wait_ge(s_ac.sem, sq_marks[oc])
                if oc == 0:
                    t.wait_ge(s_dv.sem, max(st["ps_c"][0], st["ps_c"][1], 1))
                else:
                    t.wait_ge(s_dv.sem, st["srow_add"])
                for th in range(2):
                    t.matmul(ps[0:1, 4 + th, :], ones_sb[:, 0:1],
                             hT[:, oc * 1024 + th * 512:
                                oc * 1024 + (th + 1) * 512],
                             start=True, stop=True)
                    ins = t.matmul(ps[0:1, 6 + th, :], ones_sb[:, 0:1],
                                   sq_sb[:, (oc % 2) * 1024 + th * 512:
                                         (oc % 2) * 1024 + (th + 1) * 512],
                                   start=True, stop=True)
                    if th == 1:
                        ins.then_inc(s_pe.sem, 1)
            st["sq_used"][oc % 2] = s_pe.bump()
            mmv2 = s_pe.n

            @block.vector
            def _(v):
                v.wait_ge(s_pe.sem, mmv2)
                if oc == 0:
                    v.tensor_copy(srow, ps[0:1, 4:8, :].rearrange(
                        "p a b -> p (a b)")).then_inc(s_dv.sem, 1)
                else:
                    v.wait_ge(s_dv.sem, st["srow_add"])
                    v.tensor_tensor(srow, srow, ps[0:1, 4:8, :].rearrange(
                        "p a b -> p (a b)"), ALU.add).then_inc(s_dv.sem, 1)
            st["srow_add"] = s_dv.bump()

        for och in range(NOCH):
            wslot = och % 2
            pb = (och % 2) * 2

            @block.gpsimd
            def _(g):
                if st["cw_used"][wslot] > 0:
                    g.wait_ge(s_mm.sem, st["cw_used"][wslot])
                if och < 2:
                    g.wait_ge(s_mm.sem, max(st["wl_freed"][0],
                                            st["wl_freed"][1]))
                    g.wait_ge(s_dv.sem, dv_entry)
                g.dma_start(woc[:, wslot * 4096:(wslot + 1) * 4096],
                            outpT_d[och]).then_inc(s_wl[wslot].sem, 16)
            wlv = s_wl[wslot].bump(16)

            xslot = och % 2

            @block.gpsimd
            def _(g):
                c, v = st["xt_users"][xslot]
                if c is not None:
                    g.wait_ge(c.sem, v)
                if och < 2:
                    g.wait_ge(s_pe.sem, max(st["qrd_used"][0],
                                            st["qrd_used"][1]))
                g.dma_start(
                    xt_sb[:, xslot * 1024:(xslot + 1) * 1024],
                    xT_d[och * 128:(och + 1) * 128,
                         half * HALF:(half + 1) * HALF]).then_inc(
                    s_x[xslot].sem, 16)
            xv = s_x[xslot].bump(16)

            @block.tensor
            def _(t):
                t.wait_ge(s_wl[wslot].sem, wlv)
                if och == 0:
                    c0, v0 = st["ps_main"][0]
                    c1, v1 = st["ps_main"][1]
                    if c0 is not None:
                        t.wait_ge(c0.sem, v0)
                    if c1 is not None:
                        t.wait_ge(c1.sem, v1)
                if st["ctm_ps"][och % 2] > 0:
                    t.wait_ge(s_dv.sem, st["ctm_ps"][och % 2])
                if och == 0:
                    for th in range(2):
                        t.wait_ge(s_dv.sem, gn_marks[3 if th == 0 else 7])
                        for fch in range(NIC):
                            ins = t.matmul(
                                ps[:, pb + th, :],
                                woc[:, wslot * 4096 + fch * 128:
                                    wslot * 4096 + (fch + 1) * 128],
                                gnT[:, fch * 1024 + th * 512:
                                    fch * 1024 + (th + 1) * 512],
                                start=(fch == 0), stop=(fch == NIC - 1))
                            if fch == NIC - 1 and th == 1:
                                ins.then_inc(s_mm.sem, 1)
                else:
                    for fch in range(NIC):
                        for th in range(2):
                            ins = t.matmul(
                                ps[:, pb + th, :],
                                woc[:, wslot * 4096 + fch * 128:
                                    wslot * 4096 + (fch + 1) * 128],
                                gnT[:, fch * 1024 + th * 512:
                                    fch * 1024 + (th + 1) * 512],
                                start=(fch == 0), stop=(fch == NIC - 1))
                            if fch == NIC - 1 and th == 1:
                                ins.then_inc(s_mm.sem, 1)
            st["cw_used"][wslot] = s_mm.bump()
            cmm_v = s_mm.n

            @block.vector
            def _(v):
                v.wait_ge(s_mm.sem, cmm_v)
                v.wait_ge(s_x[xslot].sem, xv)
                v.tensor_tensor(
                    hT[:, och * 1024:(och + 1) * 1024].rearrange(
                        "p (a b) -> p a b", b=512),
                    ps[:, pb:pb + 2, :],
                    xt_sb[:, xslot * 1024:(xslot + 1) * 1024].rearrange(
                        "p (a b) -> p a b", b=512),
                    ALU.add).then_inc(s_dv.sem, 1)
            evac_marks[och] = s_dv.bump()
            st["ctm_ps"][och % 2] = evac_marks[och]
            st["xt_users"][xslot] = (s_dv, evac_marks[och])

            @block.scalar
            def _(sc):
                sc.wait_ge(s_dv.sem, evac_marks[och])
                if och == 0:
                    sc.wait_ge(s_dv.sem, gn_done)
                if st["sq_used"][och % 2] > 0:
                    sc.wait_ge(s_pe.sem, st["sq_used"][och % 2])
                sc.activation(sq_sb[:, (och % 2) * 1024:(och % 2 + 1) * 1024],
                              hT[:, och * 1024:(och + 1) * 1024],
                              AF.Square).then_inc(s_ac.sem, 1)
            sq_marks[och] = s_ac.bump()

            if och >= 2:
                emit_stats(och - 2)
        emit_stats(NOCH - 2)
        emit_stats(NOCH - 1)
        stats_done_pe = s_pe.n
        c_evac_done = s_dv.n
        st["ps_main"][0] = (s_dv, c_evac_done)
        st["ps_main"][1] = (s_dv, c_evac_done)

        var_row = srow[:, 1024:2048]
        rstd_bf = expsb[0:1, 0:1024]

        @block.vector
        def _(v):
            v.wait_ge(s_dv.sem, st["srow_add"])
            chain(v, v.tensor_scalar(rrow, srow[:, 0:1024],
                                     -1.0 / D, None, ALU.mult))
            chain(v, v.tensor_scalar(var_row, var_row,
                                     1.0 / D, EPS, ALU.mult, ALU.add))
            chain(v, v.tensor_tensor(vtmp_row, rrow, rrow, ALU.mult))
            v.tensor_tensor(var_row, var_row, vtmp_row,
                            ALU.subtract).then_inc(s_dv.sem, 1)
        varrow_v = s_dv.bump()

        @block.scalar
        def _(sc):
            sc.wait_ge(s_dv.sem, varrow_v)
            sc.activation(var_row, var_row, AF.Sqrt).then_inc(s_ac.sem, 1)
        stdrow_v = s_ac.bump()

        @block.vector
        def _(v):
            v.wait_ge(s_ac.sem, stdrow_v)
            chain(v, v.reciprocal(var_row, var_row))
            chain(v, v.tensor_copy(rrow_bf, rrow))
            v.tensor_copy(rstd_bf, var_row).then_inc(s_dv.sem, 1)
        rows_v = s_dv.bump()

        @block.tensor
        def _(t):
            t.wait_ge(s_dv.sem, rows_v)
            t.wait_ge(s_cst.sem, CST_TOTAL)
            for th in range(2):
                t.matmul(ps[:, 4 + th, :], ones_sb[0:1, 8:136],
                         rrow_bf[:, th * 512:(th + 1) * 512],
                         start=True, stop=True)
                ins = t.matmul(ps[:, 0 + th, :], ones_sb[0:1, 8:136],
                               rstd_bf[:, th * 512:(th + 1) * 512],
                               start=True, stop=True)
                if th == 1:
                    ins.then_inc(s_pe.sem, 1)
        bcast_v = s_pe.bump()

        h2_marks = {}
        for oc in range(NOCH):
            @block.vector
            def _(v):
                if oc == 0:
                    v.wait_ge(s_pe.sem, bcast_v)
                gv = gnT[:, oc * 1024:(oc + 1) * 1024].rearrange(
                    "p (a b) -> p a b", b=512)
                xv2 = hT[:, oc * 1024:(oc + 1) * 1024].rearrange(
                    "p (a b) -> p a b", b=512)
                chain(v, v.tensor_tensor(gv, xv2, ps[:, 4:6, :], ALU.add))
                v.tensor_tensor(gv, gv, ps[:, 0:2, :],
                                ALU.mult).then_inc(s_dv.sem, 1)
            h2_marks[oc] = s_dv.bump()
        h2_done = s_dv.n

        @block.gpsimd
        def _(g):
            if st["eo_user"] > 0:
                g.wait_ge(s_pe.sem, st["eo_user"])
            g.dma_start(eo_bf, icsp_d).then_inc(s_eo.sem, 16)
        ics_v = s_eo.bump(16)

        @block.tensor
        def _(t):
            t.wait_ge(s_rw.sem, 16)
            for ic in range(NIC):
                t.wait_ge(s_dv.sem, h2_marks[ic])
                for nh in range(2):
                    ins = t.matmul(ps[0:8, 6 + nh, :],
                                   rw_bf[:, ic * 8:(ic + 1) * 8],
                                   gnT[:, ic * 1024 + nh * 512:
                                       ic * 1024 + (nh + 1) * 512],
                                   start=(ic == 0), stop=(ic == NIC - 1))
                    if ic == NIC - 1 and nh == 1:
                        ins.then_inc(s_pe.sem, 1)
        rtr_mm = s_pe.bump()

        @block.scalar
        def _(sc):
            sc.wait_ge(s_pe.sem, rtr_mm)
            sc.activation(expsb, ps[0:8, 6:8, :], AF.Exp).then_inc(s_ac.sem, 1)
        exp_v = s_ac.bump()

        @block.tensor
        def _(t):
            t.wait_ge(s_ac.sem, exp_v)
            for nh in range(2):
                ins = t.matmul(ps[0:1, 6 + nh, :], ones_sb[0:8, 0:1],
                               expsb[:, nh * 512:(nh + 1) * 512],
                               start=True, stop=True)
                if nh == 1:
                    ins.then_inc(s_pe.sem, 1)
        sum_v = s_pe.bump()

        @block.vector
        def _(v):
            v.wait_ge(s_pe.sem, sum_v)
            chain(v, v.reciprocal(rrow,
                                  ps[0:1, 6:8, :].rearrange("p a b -> p (a b)")))
            v.tensor_copy(rrow_bf, rrow).then_inc(s_dv.sem, 1)
        rcp_v = s_dv.bump()

        @block.tensor
        def _(t):
            t.wait_ge(s_dv.sem, rcp_v)
            for nh in range(2):
                ins = t.matmul(ps[0:8, 6 + nh, :], ones_sb[0:1, 0:8],
                               rrow_bf[:, nh * 512:(nh + 1) * 512],
                               start=True, stop=True)
                if nh == 1:
                    ins.then_inc(s_pe.sem, 1)
        bc_v = s_pe.bump()

        @block.vector
        def _(v):
            v.wait_ge(s_pe.sem, bc_v)
            v.tensor_tensor(rT.rearrange("p (a b) -> p a b", b=512),
                            expsb.rearrange("p (a b) -> p a b", b=512),
                            ps[0:8, 6:8, :], ALU.mult).then_inc(s_dv.sem, 1)
        rT_v = s_dv.bump()

        for ic in range(NIC):
            pslot = ic % 2

            @block.tensor
            def _(t):
                if ic == 0:
                    t.wait_ge(s_dv.sem, rT_v)
                    t.wait_ge(s_eo.sem, ics_v)
                if ic < 2:
                    t.wait_ge(s_dv.sem, h2_done)
                else:
                    t.wait_ge(s_dv.sem, is_dv_guard[pslot])
                for nh in range(2):
                    ins = t.matmul(ps[:, 2 * pslot + nh, :],
                                   eo_bf[:, ic * 128:(ic + 1) * 128],
                                   rT[:, nh * 512:(nh + 1) * 512],
                                   start=True, stop=True)
                    if nh == 1:
                        ins.then_inc(s_pe.sem, 1)
            is_mm = s_pe.bump()

            @block.vector
            def _(v):
                v.wait_ge(s_pe.sem, is_mm)
                hc = gnT[:, ic * 1024:(ic + 1) * 1024].rearrange(
                    "p (a b) -> p a b", b=512)
                v.tensor_tensor(hc, hc, ppair(pslot), ALU.mult).then_inc(
                    s_dv.sem, 1)
            is_dv_guard[pslot] = s_dv.bump()
        st["eo_user"] = s_pe.n
        xsT_done = s_dv.n

        @block.gpsimd
        def _(g):
            g.wait_ge(s_pe.sem, st["eo_user"])
            g.dma_start(eo_bf, ocsp_d).then_inc(s_eo.sem, 16)
        ocs_v = s_eo.bump(16)

        for och in range(NOCH):
            wslot = och % 2
            pb = (och % 2) * 2
            osb = 4 + (och % 2) * 2

            @block.gpsimd
            def _(g):
                if st["woc_used"][wslot] > 0:
                    g.wait_ge(s_mm.sem, st["woc_used"][wslot])
                if och < 2:
                    g.wait_ge(s_mm.sem, max(st["cw_used"][0],
                                            st["cw_used"][1]))
                    g.wait_ge(s_dv.sem, rT_v)
                    g.wait_ge(s_pe.sem, sum_v)
                g.dma_start(woc[:, wslot * 4096:(wslot + 1) * 4096],
                            ffnpT_d[och]).then_inc(s_wl[wslot].sem, 16)
            wlv = s_wl[wslot].bump(16)

            @block.scalar
            def _(sc):
                sc.wait_ge(s_wl[wslot].sem, wlv)
                wv = woc[:, wslot * 4096:(wslot + 1) * 4096]
                sc.activation(wv, wv, AF.Sign).then_inc(s_ac.sem, 1)
            sign_v = s_ac.bump()

            @block.tensor
            def _(t):
                if och == 0:
                    t.wait_ge(s_eo.sem, ocs_v)
                    t.wait_ge(s_dv.sem, h2_done)
                if och == 1:
                    t.wait_ge(s_dv.sem, rT_v)
                if st["os_used"][och % 2] > 0:
                    t.wait_ge(s_dv.sem, st["os_used"][och % 2])
                for th in range(2):
                    ins = t.matmul(ps[:, osb + th, :],
                                   eo_bf[0:8, och * 128:(och + 1) * 128],
                                   rT[0:8, th * 512:(th + 1) * 512],
                                   start=True, stop=True)
                    if th == 1:
                        ins.then_inc(s_pe.sem, 1)
            os_mm = s_pe.bump()

            @block.vector
            def _(v):
                v.wait_ge(s_pe.sem, os_mm)
                v.tensor_copy(
                    os_sb[:, (och % 2) * 1024:(och % 2 + 1) * 1024].rearrange(
                        "p (a b) -> p a b", b=512),
                    ps[:, osb:osb + 2, :]).then_inc(s_dv.sem, 1)
            os_ev = s_dv.bump()
            st["os_used"][och % 2] = os_ev

            @block.tensor
            def _(t):
                if och == 0:
                    t.wait_ge(s_dv.sem, xsT_done)
                if st["ffn_ps_used"][och % 2] > 0:
                    t.wait_ge(s_dv.sem, st["ffn_ps_used"][och % 2])
                t.wait_ge(s_ac.sem, sign_v)
                for fch in range(NIC):
                    for th in range(2):
                        ins = t.matmul(ps[:, pb + th, :],
                                       woc[:, wslot * 4096 + fch * 128:
                                           wslot * 4096 + (fch + 1) * 128],
                                       gnT[:, fch * 1024 + th * 512:
                                           fch * 1024 + (th + 1) * 512],
                                       start=(fch == 0), stop=(fch == NIC - 1))
                        if fch == NIC - 1 and th == 1:
                            ins.then_inc(s_mm.sem, 1)
            st["woc_used"][wslot] = s_mm.bump()
            ffn_mm = s_mm.n

            @block.vector
            def _(v):
                v.wait_ge(s_mm.sem, ffn_mm)
                v.wait_ge(s_dv.sem, os_ev)
                if st["outsb_used"][och % 2] > 0:
                    v.wait_ge(s_ou.sem, st["outsb_used"][och % 2])
                if och < 2:
                    v.wait_ge(s_dv.sem, st["comb_done"])
                    v.wait_ge(s_cst.sem, CST_TOTAL)
                ot3 = outsb[:, (och % 2) * 1024:(och % 2 + 1) * 1024].rearrange(
                    "p (a b) -> p a b", b=512)
                chain(v, v.tensor_tensor(
                    ot3, ps[:, pb:pb + 2, :],
                    os_sb[:, (och % 2) * 1024:(och % 2 + 1) * 1024].rearrange(
                        "p (a b) -> p a b", b=512),
                    ALU.mult))
                ins = v.scalar_tensor_tensor(
                    ot3, ot3, bias_col[:, och:och + 1],
                    hT[:, och * 1024:(och + 1) * 1024].rearrange(
                        "p (a b) -> p a b", b=512),
                    ALU.add, ALU.add)
                ins.then_inc(s_dv.sem, 1)
            ep_v = s_dv.bump()
            st["ffn_ps_used"][och % 2] = ep_v

            @block.sync
            def _(s):
                s.wait_ge(s_dv.sem, ep_v)
                s.wait_ge(s_ou.sem, s_ou.n)
                view = outT_d[och * 128:(och + 1) * 128,
                              half * HALF:(half + 1) * HALF]
                s.dma_start(view,
                            outsb[:, (och % 2) * 1024:(och % 2 + 1) * 1024]
                            ).then_inc(s_ou.sem, 16)
            st["outsb_used"][och % 2] = s_ou.bump(16)

    is_dv_guard = [0, 0]

    def emit_combine():
        tr_entry = s_tr.n
        mk0 = s_msk.n

        @block.vector
        def _(v):
            v.memset(kvst_bf, 0.0)
            v.engine_nop().then_inc(s_dv.sem, 1)
        kvz = s_dv.bump()

        for rk in range(NCORE):
            gs = rk % 2

            @block.gpsimd
            def _(g):
                if rk < 2:
                    g.wait_ge(s_cc.sem, 1)
                    g.wait_ge(s_tr.sem, tr_entry)
                    g.wait_ge(s_msk.sem, mk0)
                else:
                    g.wait_ge(s_dv.sem, st["gb_free"][gs])
                g.dma_start(gbuf[:, gs * 4096:(gs + 1) * 4096],
                            gth_out.ap()[rk * 128:(rk + 1) * 128, :]).then_inc(
                    s_gb[gs].sem, 16)
            gv = s_gb[gs].bump(16)

            @block.vector
            def _(v):
                v.wait_ge(s_gb[gs].sem, gv)
                v.wait_ge(s_dv.sem, kvz if rk == 0 else st["comb_prev"])
                for h in range(H):
                    kc = kvst_bf[:, h * 128:(h + 1) * 128]
                    ins = v.scalar_tensor_tensor(
                        kc, gbuf[:, gs * 4096 + h * 128:gs * 4096 + (h + 1) * 128],
                        wcomb_sb[:, rk * H + h:rk * H + h + 1], kc,
                        ALU.mult, ALU.add)
                    if h == H - 1:
                        ins.then_inc(s_dv.sem, 1)
            st["gb_free"][gs] = s_dv.bump()
            st["comb_prev"] = s_dv.n
        st["comb_done"] = s_dv.n

    p1_half(0)
    mm_after_h0 = s_mm.n

    @block.sync
    def _(s):
        s.wait_ge(s_mm.sem, mm_after_h0)
        s.dma_start(hspill.ap(), hT).then_inc(s_hs.sem, 16)
    s_hs.bump(16)

    p1_half(1, hT_guard_mm=mm_after_h0, spill_wait=16)

    kvu_total = s_kvu.n

    @block.sync
    def _(s):
        s.wait_ge(s_kvu.sem, kvu_total)
        s.dma_start(gth_in.ap(), kv_all).then_inc(s_sum.sem, 16)
    sum_out = s_sum.bump(16)

    @block.gpsimd
    def _(g):
        g.wait_ge(s_sum.sem, sum_out)
        g.collective_compute(
            "AllGather", ALU.bypass,
            replica_groups=[list(range(NCORE))],
            ins=[gth_in.ap()],
            outs=[gth_out.ap()],
        ).then_inc(s_cc.sem, 1)

    p2_half(1, skip_A=True, do_combine=True)
    p2_half(0, skip_A=False)

    ou_total = s_ou.n

    @block.sync
    def _(s):
        s.wait_ge(s_ou.sem, ou_total)

    block.__exit__(None, None, None)
    return nc


def _prep_inputs(inputs):
    x = np.ascontiguousarray(np.asarray(inputs["L0_attn_hidden_states"], np.float32))
    qkv_w = np.asarray(inputs["L0_attn_qkv_w"], np.float32)
    out_w = np.asarray(inputs["L0_attn_out_w"], np.float32)
    gate_w = np.asarray(inputs["L0_attn_gate_w"], np.float32)
    norm_w = np.asarray(inputs["L0_attn_norm_w"], np.float32)
    ffn_w = np.asarray(inputs["L0_ffn_weight"], np.float32)
    ffn_b = np.asarray(inputs["L0_ffn_bias"], np.float32)
    rw = np.asarray(inputs["L0_ffn_gate_w"], np.float32)
    ics = np.asarray(inputs["L0_ffn_in_channel_scale"], np.float32)
    ocs = np.asarray(inputs["L0_ffn_out_channel_scale"], np.float32)

    t = qkv_w.reshape(8, HG, 3, DH, NIC, 128)
    qkvp = np.ascontiguousarray(
        t.transpose(0, 2, 1, 5, 4, 3).reshape(96, 128, D))
    gatep = np.ascontiguousarray(
        gate_w.reshape(NIC, 128, NIC, 128).transpose(0, 3, 2, 1).reshape(NIC, 128, D))
    outpT = np.ascontiguousarray(
        out_w.reshape(32, 128, 32, 128).transpose(0, 3, 2, 1).reshape(
            32, 128, D))
    ffnpT = np.ascontiguousarray(
        ffn_w.reshape(32, 128, 32, 128).transpose(0, 3, 2, 1).reshape(
            32, 128, D))
    rwp = np.ascontiguousarray(
        rw.T.reshape(NIC, 128, 8).transpose(1, 0, 2).reshape(128, NIC * 8))
    nwp = np.ascontiguousarray(np.broadcast_to(norm_w[None, :], (128, D)))
    biascp = np.ascontiguousarray(ffn_b.reshape(32, 128).T)

    maskp, qdlp, kdecp, Dm, onesp = _consts()

    shared = dict(qkvp=qkvp, gatep=gatep, outpT=outpT, ffnpT=ffnpT, rwp=rwp,
                  icsp=np.ascontiguousarray(ics), ocsp=np.ascontiguousarray(ocs),
                  nwp=nwp, biascp=biascp, maskp=np.ascontiguousarray(maskp),
                  qdlp=qdlp, kdecp=kdecp, onesp=onesp)

    in_maps = []
    for c in range(NCORE):
        b, jj = c // 4, c % 4
        m = dict(shared)
        xc = np.ascontiguousarray(x[b, jj * TCORE:(jj + 1) * TCORE, :])
        m["x_sh"] = xc
        m["xT_sh"] = np.ascontiguousarray(xc.T)
        m["wcombp"] = _wcomb_for_core(c, Dm)
        in_maps.append(m)
    return in_maps


def kernel(**inputs):
    in_maps = _prep_inputs(inputs)
    nc = _build_nc()
    res = run_bass_kernel_spmd(nc, in_maps, core_ids=list(range(NCORE)))
    out = np.empty((B, S, D), dtype=np.float32)
    for c in range(NCORE):
        b, jj = c // 4, c % 4
        out[b, jj * TCORE:(jj + 1) * TCORE, :] = np.asarray(
            res.results[c]["out_shT"]).T
    return out
